# version 16
# baseline (speedup 1.0000x reference)
import math
import contextlib
import numpy as np
import ml_dtypes
import concourse.bass as bass
import concourse.mybir as mybir
from concourse.bass_utils import run_bass_kernel_spmd

F32 = mybir.dt.float32
BF16 = mybir.dt.bfloat16
I32 = mybir.dt.int32
AF = mybir.ActivationFunctionType
ALU = mybir.AluOpType
AX = mybir.AxisListType
BF = ml_dtypes.bfloat16


class Cfg:
    def __init__(self, S=4096, C=2048, I=5632, H=16, KV=4, TG=1024, debug=False,
                 phases="all"):
        self.S, self.C, self.I, self.H, self.KV, self.TG = S, C, I, H, KV, TG
        self.phases = phases
        self.T = S // 2
        self.D = 128
        self.debug = debug
        assert C == H * self.D and C % 1024 == 0 and S % 1024 == 0
        assert TG % 512 == 0 and self.T % TG == 0 and I % 512 == 0
        self.NC = C // 128
        self.NS = S // 128
        self.NG = self.T // TG
        self.NMIX = TG // 128
        self.NI = I // 128
        self.NT = self.T // 128
        self.NTG = TG // 128
        self.G = H // KV
        self.NOB = C // 512


def host_consts(cfg):
    D, H, TG, NMIX = cfg.D, cfg.H, cfg.TG, cfg.NMIX
    half = D // 2
    inv_freq = 1.0 / 10000.0 ** (np.arange(0, D, 2, dtype=np.float64) / D)
    pos = np.arange(H, dtype=np.float64)
    freqs = np.einsum("i,j->ij", pos, inv_freq)
    emb = np.concatenate([freqs, freqs], axis=-1)
    cos, sin = np.cos(emb), np.sin(emb)
    RT = np.zeros((H, D, D), dtype=np.float64)
    for h in range(H):
        R = np.zeros((D, D))
        for d in range(D):
            R[d, d] = cos[h, d]
        for d in range(half):
            R[d, d + half] = -sin[h, d]
            R[d + half, d] = sin[h, d + half]
        RT[h] = R.T
    rts = np.ascontiguousarray(RT.reshape(H * D, D)).astype(np.float32)
    W = TG + 128 * (NMIX - 1)
    u = np.arange(W)[None, :]
    p = np.arange(128)[:, None]
    base = np.exp((u >= (NMIX - 1) * 128 + p).astype(np.float64)).astype(BF)
    return dict(rts=rts, mbase=np.ascontiguousarray(base),
                ident=np.eye(128, dtype=np.float32).astype(BF),
                ones_col=np.ones((128, 1), dtype=np.float32).astype(BF),
                ones_row=np.ones((1, 128), dtype=np.float32),
                ones_rowb=np.ones((1, 128), dtype=np.float32).astype(BF),
                ones_f=np.full((128, 128), 1.0 / 128, dtype=np.float32)
                .astype(BF),
                ones_b=np.ones((128, 128), dtype=np.float32).astype(BF),
                scale_col=np.full((128, 1), D ** -0.25, dtype=np.float32))


def host_beta(cfg, core_half):
    S, TG, NG, NS, T = cfg.S, cfg.TG, cfg.NG, cfg.NS, cfg.T
    r = core_half * T
    beta = np.zeros((128, NG * NS), dtype=np.float32)
    for g in range(NG):
        t_lo = r + g * TG
        for j in range(NS):
            srot0 = 128 * j
            if g * TG <= srot0 < (g + 1) * TG:
                continue
            for p in range(128):
                s_orig = (srot0 + p + r) % S
                beta[p, g * NS + j] = 1.0 if s_orig <= t_lo else 0.0
    return beta


class Sem:
    __slots__ = ("h", "n", "name", "log_seen")

    def __init__(self, es, nc, name):
        self.h = es.enter_context(nc.semaphore(name))
        self.n = 0
        self.name = name
        self.log_seen = 0

    def will(self, k=1):
        self.n += k
        assert self.n < 2 ** 31
        return self.n


class Prog:
    ENGS = ("sync", "tensor", "vector", "scalar", "gpsimd")

    def __init__(self):
        self.ops = {e: [] for e in self.ENGS}
        self.selfsem = {}

    def op(self, eng, meth, *args, waits=(), inc=None, **kw):
        ss = self.selfsem.get(eng)
        if ss is not None and ss.n > 0:
            waits = tuple(waits) + ((ss, ss.n),)
        tot = None
        if inc is not None:
            tot = inc[0].will(inc[1])
        self.ops[eng].append((meth, args, kw, tuple(waits), inc))
        return tot

    def wait(self, eng, sem, val):
        if val > 0:
            self.ops[eng].append(("__wait__", (sem, val), {}, (), None))

    def sem_inc(self, eng, sem, k=1):
        tot = sem.will(k)
        self.ops[eng].append(("__inc__", (sem, k), {}, (), None))
        return tot

    def replay(self, eng, e, nc=None, collect=None, fix=None):
        def xlate(s, v):
            if fix is not None:
                return fix.get(s.h.__class__ and getattr(s, 'name', ''), {}).get(v, v)
            return v

        for meth, args, kw, waits, inc in self.ops[eng]:
            if meth == "__wait__":
                s, v = args
                e.wait_ge(s.h, self._fx(fix, s, v))
                continue
            if meth == "__inc__":
                e.sem_inc(args[0].h, args[1])
                continue
            for s, v in waits:
                if v > 0:
                    e.wait_ge(s.h, self._fx(fix, s, v))
            if meth == "dma_start" and (collect is not None) and inc is not None:
                n0 = nc.n_instructions()
                ins = getattr(e, meth)(*args, **kw)
                ndma = nc.n_instructions() - n0
                ins.then_inc(inc[0].h, inc[1])
                s = inc[0]
                st = collect.setdefault(s.name, {"act": 0, "map": {0: 0}})
                st["act"] += 16 * max(1, ndma)
                st["map"][s.log_seen + inc[1]] = st["act"]
                s.log_seen += inc[1]
            else:
                ins = getattr(e, meth)(*args, **kw)
                if inc is not None:
                    ins.then_inc(inc[0].h, inc[1])
        return None

    @staticmethod
    def _fx(fix, s, v):
        if fix is None:
            return v
        m = fix.get(s.name)
        if m is None:
            return v
        return m["map"].get(v, v)


class TreeSum:

    def __init__(self, P, ve, pool):
        self.P, self.ve = P, ve
        self.pool = list(pool)
        self.levels = []

    def feed_pair(self, ap_a, rdy_a, ap_b, rdy_b):
        out = self.pool.pop()
        n = self.P.op("vector", "tensor_tensor", out, ap_a, ap_b, ALU.add,
                      waits=[rdy_a, rdy_b], inc=(self.ve, 1))
        carry = (out, (self.ve, n))
        lvl = 0
        while lvl < len(self.levels) and self.levels[lvl] is not None:
            a, ra = self.levels[lvl]
            n = self.P.op("vector", "tensor_tensor", a, a, carry[0], ALU.add,
                          waits=[ra, carry[1]], inc=(self.ve, 1))
            self.pool.append(carry[0])
            carry = (a, (self.ve, n))
            self.levels[lvl] = None
            lvl += 1
        if lvl == len(self.levels):
            self.levels.append(None)
        self.levels[lvl] = carry

    def root(self):
        occ = [x for x in self.levels if x is not None]
        assert occ
        ap, rdy = occ[0]
        for ap2, rdy2 in occ[1:]:
            n = self.P.op("vector", "tensor_tensor", ap, ap, ap2, ALU.add,
                          waits=[rdy, rdy2], inc=(self.ve, 1))
            rdy = (self.ve, n)
        return ap, rdy


def _build(cfg, collect=None, fix=None):
    nc = bass.Bass("TRN2", target_bir_lowering=False)
    S, T, C, I, H, KV, D, TG = (cfg.S, cfg.T, cfg.C, cfg.I, cfg.H, cfg.KV,
                                cfg.D, cfg.TG)
    NC, NS, NG, NMIX, NI, NT, NTG, NOB, G = (cfg.NC, cfg.NS, cfg.NG, cfg.NMIX,
                                             cfg.NI, cfg.NT, cfg.NTG, cfg.NOB,
                                             cfg.G)
    KVD = KV * D
    I2 = 2 * I
    MW = TG + 128 * (NMIX - 1)
    NW1 = I2 // 128
    NW2 = C // 128
    TH = TG // 512
    NSG = S // 1024

    dt = nc.dram_tensor
    xT_d = dt("xT", [C, S], F32, kind="ExternalInput").ap()
    xown_d = dt("x_own", [T, C], F32, kind="ExternalInput").ap()
    wqT_d = dt("wqT", [H, C, D], F32, kind="ExternalInput").ap()
    wkT_d = dt("wkT", [C, KVD], F32, kind="ExternalInput").ap()
    wvT_d = dt("wvT", [C, KVD], F32, kind="ExternalInput").ap()
    woT_d = dt("woT", [C, C], F32, kind="ExternalInput").ap()
    w1_d = dt("w1", [I2, C], F32, kind="ExternalInput").ap()
    w2_d = dt("w2", [C, I], F32, kind="ExternalInput").ap()
    bqT_d = dt("bqT", [128, H], F32, kind="ExternalInput").ap()
    bkT_d = dt("bkT", [128, KV], F32, kind="ExternalInput").ap()
    bv_d = dt("bv_r", [1, KVD], F32, kind="ExternalInput").ap()
    bo_d = dt("bo_r", [1, C], F32, kind="ExternalInput").ap()
    beta_d = dt("beta", [128, NG * NS], F32, kind="ExternalInput").ap()
    rts_d = dt("rts", [H * D, D], F32, kind="ExternalInput").ap()
    mbase_d = dt("mbase", [128, MW], BF16, kind="ExternalInput").ap()
    ident_d = dt("ident", [128, 128], BF16, kind="ExternalInput").ap()
    onesc_d = dt("ones_col", [128, 1], BF16, kind="ExternalInput").ap()
    onesr_d = dt("ones_row", [1, 128], F32, kind="ExternalInput").ap()
    onesrb_d = dt("ones_rowb", [1, 128], BF16, kind="ExternalInput").ap()
    onesf_d = dt("ones_f", [128, 128], BF16, kind="ExternalInput").ap()
    onesb_d = dt("ones_b", [128, 128], BF16, kind="ExternalInput").ap()
    scalec_d = dt("scale_col", [128, 1], F32, kind="ExternalInput").ap()

    out_d = dt("out", [T, C], F32, kind="ExternalOutput").ap()

    spk = "ExternalOutput" if cfg.debug else "Internal"
    w1q_d = dt("w1q_d", [I2, C], BF16, kind=spk).ap()
    w2q_d = dt("w2q_d", [C, I], BF16, kind="Internal").ap()
    xTbf_d = dt("xTbf_d", [C, T], BF16, kind="Internal").ap()
    woTbf_d = dt("woTbf_d", [C, C], BF16, kind="Internal").ap()
    wqbf_d = dt("wqbf_d", [H, C, D], BF16, kind="Internal").ap()
    w2qT_d = dt("w2qT_d", [NOB * I, 512], BF16, kind="Internal").ap()
    x2_d = dt("x2_d", [T, C], BF16, kind=spk).ap()
    xq_d = dt("xq_d", [T, C], BF16, kind=spk).ap()
    h_d = dt("h_d", [T, I], F32, kind=spk).ap()

    dbg = {}
    if cfg.debug:
        dbg["k"] = dt("dbg_k", [128, KV * S], F32, kind="ExternalOutput").ap()
        dbg["v"] = dt("dbg_v", [128, NS * KVD], F32, kind="ExternalOutput").ap()
        dbg["at"] = dt("dbg_at", [128, NG * H * TG], F32,
                       kind="ExternalOutput").ap()
        dbg["q"] = dt("dbg_q", [128, TG], F32, kind="ExternalOutput").ap()
        dbg["sg"] = dt("dbg_sg", [1, TG], F32, kind="ExternalOutput").ap()
        dbg["pt"] = dt("dbg_pt", [128, 4 * TG], F32, kind="ExternalOutput").ap()
        dbg["sm"] = dt("dbg_sm", [128, TG], F32, kind="ExternalOutput").ap()

    P = Prog()
    es = contextlib.ExitStack()

    def sb(stack, name, shape, dtp):
        return stack.enter_context(nc.sbuf_tensor(name, shape, dtp))

    with es:
        consts = sb(es, "consts", [128, H * D + 385], BF16)
        mbase = sb(es, "mbase_s", [128, MW], BF16)
        beta = sb(es, "beta_s", [128, NG * NS], F32)
        onesr = sb(es, "onesr", [1, 128], F32)
        onesrb = sb(es, "onesrb", [1, 128], BF16)
        scalec = sb(es, "scalec", [128, 1], F32)
        bq = sb(es, "bq_s", [128, H], F32)
        bk = sb(es, "bk_s", [128, KV], F32)
        bvb = sb(es, "bvb", [128, KVD], BF16)
        bob = sb(es, "bob", [128, C], BF16)
        km2 = sb(es, "km2", [1, KV + H + 8], F32)
        sx_all = sb(es, "sx_all", [128, NT], F32)
        s2g = sb(es, "s2g", [128, 4], F32)
        stat = sb(es, "stat", [128, 14], F32)
        sigr = sb(es, "sigr", [1, 8], F32)

        RT_ = lambda hh: consts[:, hh * D:(hh + 1) * D]
        IDENT = consts[:, H * D:H * D + 128]
        ONESC = consts[:, H * D + 128:H * D + 129]
        ONESF = consts[:, H * D + 129:H * D + 257]
        ONESB = consts[:, H * D + 257:H * D + 385]

        pe = Sem(es, nc, "pe")
        ve = Sem(es, nc, "ve")
        ac = Sem(es, nc, "ac")
        bar = Sem(es, nc, "bar")
        P.selfsem = {"vector": ve, "scalar": ac}
        _s = lambda nm: Sem(es, nc, nm)
        ldi = _s("ldi")
        lrt = _s("lrt")
        ldb = _s("ldb")
        lwk = _s("lwk")
        lxb = [_s("lxb0"), _s("lxb1")]
        lwq = [_s("lwq0"), _s("lwq1")]
        lwo = _s("lwo")
        lxr = [_s("lxr0"), _s("lxr1")]
        lw1 = _s("lw1")
        sp_w1 = _s("sp_w1")
        lw2 = _s("lw2")
        sp_w2 = _s("sp_w2")
        lw2t = _s("lw2t")
        sp_w2t = _s("sp_w2t")
        sp_w2u = _s("sp_w2u")
        lxq = _s("lxq")
        lw1t = [_s("lw1t0"), _s("lw1t1")]
        lw1s = [_s("lw1s0"), _s("lw1s1")]
        lw2tb = _s("lw2tb")
        lwr = _s("lwr")
        sp_h = _s("sp_h")
        lh = _s("lh")
        lwf_ = [_s("lwf"), _s("lwf1"), _s("lwf2"), _s("lwf3")]
        lx2 = [_s("lx20"), _s("lx21")]
        sp_x2 = _s("sp_x2")
        sp_xq = _s("sp_xq")
        so_ = [_s("so0"), _s("so1")]
        sod = _s("sod")
        pcxx = [_s(f"pcxx{i}") for i in range(cfg.NG)]
        pcxq = [_s(f"pcxq{i}") for i in range((cfg.H + 3) // 4)]
        pcxw = _s("pcxw")
        lxc = _s("lxc")

        def barrier():
            for eng in Prog.ENGS:
                P.sem_inc(eng, bar, 1)
            t = bar.n
            for eng in Prog.ENGS:
                P.wait(eng, bar, t)

        n_rts = P.op("gpsimd", "dma_start",
                     out=consts[:, :H * D].rearrange("p (h d) -> p h d", h=H),
                     in_=rts_d.rearrange("(h p) d -> p h d", p=D),
                     inc=(lrt, 16))
        P.op("sync", "dma_start", out=IDENT, in_=ident_d, inc=(ldi, 16))
        P.op("sync", "dma_start", out=ONESC, in_=onesc_d, inc=(ldi, 16))
        P.op("sync", "dma_start", out=mbase[:], in_=mbase_d, inc=(ldi, 16))
        P.op("sync", "dma_start", out=beta[:], in_=beta_d, inc=(ldi, 16))
        P.op("sync", "dma_start", out=onesr[:], in_=onesr_d, inc=(ldi, 16))
        P.op("sync", "dma_start", out=onesrb[:], in_=onesrb_d, inc=(ldi, 16))
        P.op("sync", "dma_start", out=ONESF, in_=onesf_d, inc=(ldi, 16))
        P.op("sync", "dma_start", out=ONESB, in_=onesb_d, inc=(ldi, 16))
        P.op("sync", "dma_start", out=scalec[:], in_=scalec_d, inc=(ldi, 16))
        P.op("sync", "dma_start", out=bq[:], in_=bqT_d, inc=(ldi, 16))
        n_ld_init = P.op("sync", "dma_start", out=bk[:], in_=bkT_d,
                         inc=(ldi, 16))

        esc = contextlib.ExitStack()
        arena = sb(esc, "arena", [128, 32768], BF16)
        A_XTB, A_WQB = 0, NC * TG
        A_QH = A_WQB + 2 * NC * 128
        A_QPRE = A_QH + 2 * TG
        A_PT = A_QPRE + TG
        A_PTS = A_PT + 3 * TG
        A_TREE = A_PTS + TG
        assert A_TREE + 5 * TG <= 32768, (A_TREE, TG)
        xTb = arena[:, A_XTB:A_XTB + NC * TG]
        sig = sb(esc, "sig", [128, NS], F32)
        y1 = sb(esc, "y1", [128, C], F32)
        recipb = y1[:, 0:TG]
        recro = recipb[0:1, 0:TG]
        xres = sb(esc, "xres", [128, C], F32)
        sgrow = xres[0:1, 0:TG]
        x2t = sb(esc, "x2t", [128, 1024], BF16)
        xqt = sb(esc, "xqt", [128, 1024], BF16)
        sgbb_bufs = [x2t[:, 0:TG], xqt[:, 0:TG]]
        i32q = sb(esc, "i32q", [128, 512], I32)
        w1f = sb(esc, "w1f", [128, C], F32)
        w1qb = sb(esc, "w1qb", [128, 1024], BF16)
        attn = sb(esc, "attn", [128, H * TG], BF16)
        kpre = attn[:, 0:1024]
        assert 2 * NC * 1024 + 2 * NC * KVD <= 32768 or True
        wkb = arena[:, NC * 1024:NC * 1024 + NC * KVD]
        wvb = arena[:, NC * 1024 + NC * KVD:NC * 1024 + 2 * NC * KVD]
        k_fm = sb(esc, "k_fm", [128, KV * S], BF16)
        v_tm = sb(esc, "v_tm", [128, NS * KVD], BF16)

        pA = esc.enter_context(nc.psum_tensor("pA", [128, 1024], F32))
        pB = esc.enter_context(nc.psum_tensor("pB", [128, 1024], F32))
        pC = esc.enter_context(nc.psum_tensor("pC", [128, 1024], F32))
        pV = esc.enter_context(nc.psum_tensor("pV", [128, 1024], F32))
        PSC = [pA, pB, pC]

        n_bv_ld = P.op("sync", "dma_start", out=y1[0:1, 0:KVD], in_=bv_d,
                       inc=(ldb, 16))

        def bcast_row(width, wait_ld, dst, psum):
            nv = 0
            prev = {}
            for o in range(0, width, 512):
                w = min(512, width - o)
                slot = (o // 512) % 2
                wts = [(ldb, wait_ld), (ldi, n_ld_init)] if o == 0 else []
                if slot in prev:
                    wts.append((ve, prev[slot]))
                nm = P.op("tensor", "matmul", psum[:, slot * 512:slot * 512 + w],
                          onesr[:], y1[0:1, o:o + w], start=True, stop=True,
                          waits=wts, inc=(pe, 1))
                nv = P.op("vector", "tensor_copy", dst[:, o:o + w],
                          psum[:, slot * 512:slot * 512 + w], waits=[(pe, nm)],
                          inc=(ve, 1))
                prev[slot] = nv
            return nv

        n_bv = bcast_row(KVD, n_bv_ld, bvb, pA)
        n_bo_ld = P.op("sync", "dma_start", out=y1[0:1, 0:C], in_=bo_d,
                       waits=[(ve, n_bv)], inc=(ldb, 16))
        n_bo = bcast_row(C, n_bo_ld, bob, pB)

        n_wk = P.op("gpsimd", "dma_start",
                    out=wkb.rearrange("p (c o) -> p c o", o=KVD),
                    in_=wkT_d.rearrange("(c p) o -> p c o", p=128),
                    inc=(lwk, 16))
        n_wv = P.op("gpsimd", "dma_start",
                    out=wvb.rearrange("p (c o) -> p c o", o=KVD),
                    in_=wvT_d.rearrange("(c p) o -> p c o", p=128),
                    inc=(lwk, 16))

        NPWQ = (H + 3) // 4
        pcx_x = [0] * NG
        pcx_wo = [0]
        pcx_wq = [0] * NPWQ

        def _pcx_g(gg):
            def f():
                pcx_x[gg] = P.op("gpsimd", "dma_start",
                                 out=xTbf_d[:, gg * TG:(gg + 1) * TG],
                                 in_=xT_d[:, gg * TG:(gg + 1) * TG],
                                 inc=(pcxx[gg], 16))
            return f

        def _pcx_wo(po_):
            def f():
                pcx_wo[0] = P.op("gpsimd", "dma_start",
                                 out=woTbf_d[po_ * (C // 2):
                                             (po_ + 1) * (C // 2), :],
                                 in_=woT_d[po_ * (C // 2):
                                           (po_ + 1) * (C // 2), :],
                                 inc=(pcxw, 16))
            return f

        def _pcx_wq(pp):
            h0, h1 = pp * 4, min(H, pp * 4 + 4)

            def f():
                pcx_wq[pp] = P.op(
                    "gpsimd", "dma_start",
                    out=wqbf_d[h0:h1].rearrange("h c d -> (h c) d"),
                    in_=wqT_d[h0:h1].rearrange("h c d -> (h c) d"),
                    inc=(pcxq[pp], 16))
            return f

        precasts = [_pcx_g(0), _pcx_wq(0)]
        precasts += [_pcx_g(gg) for gg in range(1, NG)]
        precasts += [_pcx_wq(pp) for pp in range(1, NPWQ)]
        precasts += [_pcx_wo(0), _pcx_wo(1)]

        xb_reg = [arena[:, A_XTB:A_XTB + NC * 1024]] * 2
        xtb_cnt = [0] * NSG
        b_pe_done = [0] * NSG
        evA = [n_bo, n_bo]
        acA = [0, 0]
        acB = 0
        veC = [n_bv, n_bv]
        for sg in range(NSG):
            xb = xb_reg[sg % 2]
            w = []
            if sg >= 1:
                w.append((pe, b_pe_done[sg - 1]))
            xtb_cnt[sg] = P.op("gpsimd", "dma_start",
                               out=xb.rearrange("p (c s) -> p c s", s=1024),
                               in_=xT_d[:, sg * 1024:(sg + 1) * 1024]
                               .rearrange("(c p) s -> p c s", p=128),
                               waits=w, inc=(lxb[0], 16))
            if precasts:
                precasts.pop(0)()
            for kv in range(KV):
                for th in range(2):
                    wts = [(lxb[0], xtb_cnt[sg]), (ldi, n_ld_init),
                           (lwk, n_wv)]
                    if sg == 0 and kv == 0 and th == 0:
                        wts.append((ve, n_bo))
                    wts.append((ac, acA[th]))
                    n_mm = 0
                    for c in range(NC):
                        n_mm = P.op(
                            "tensor", "matmul", pA[:, th * 512:th * 512 + 512],
                            wkb[:, c * KVD + kv * 128:c * KVD + kv * 128 + 128],
                            xb[:, c * 1024 + th * 512:c * 1024 + th * 512 + 512],
                            start=(c == 0), stop=(c == NC - 1),
                            waits=wts if c == 0 else (),
                            inc=(pe, 1) if c == NC - 1 else None)
                    acA[th] = P.op("scalar", "activation",
                                   kpre[:, th * 512:th * 512 + 512],
                                   pA[:, th * 512:th * 512 + 512], AF.Identity,
                                   bias=bk[:, kv:kv + 1], scale=1.0,
                                   waits=[(pe, n_mm)], inc=(ac, 1))
                n_r = 0
                for th in range(2):
                    n_r = P.op("tensor", "matmul",
                               pB[:, th * 512:th * 512 + 512], RT_(kv),
                               kpre[:, th * 512:th * 512 + 512],
                               start=True, stop=True,
                               waits=[(ac, acA[1]), (ac, acB),
                                      (ldi, n_ld_init), (lrt, 16),
                                      (ve, n_bo if (sg == 0 and kv == 0) else 0)]
                               if th == 0 else (), inc=(pe, 1))
                acB = P.op("scalar", "activation",
                           k_fm[:, kv * S + sg * 1024:kv * S + (sg + 1) * 1024],
                           pB[:, 0:1024], AF.Identity, bias=0.0,
                           scale=scalec[:], waits=[(pe, n_r)], inc=(ac, 1))
            for st in range(8):
                po = pC[:, (st % 2) * 512:(st % 2) * 512 + KVD]
                wts = [(ve, veC[st % 2]), (lxb[0], xtb_cnt[sg])]
                n_mm = 0
                for c in range(NC):
                    n_mm = P.op(
                        "tensor", "matmul", po,
                        xb[:, c * 1024 + st * 128:c * 1024 + st * 128 + 128],
                        wvb[:, c * KVD:(c + 1) * KVD],
                        start=(c == 0), stop=(c == NC - 1),
                        waits=wts if c == 0 else (),
                        inc=(pe, 1) if c == NC - 1 else None)
                veC[st % 2] = P.op(
                    "vector", "scalar_tensor_tensor",
                    v_tm[:, (sg * 8 + st) * KVD:(sg * 8 + st + 1) * KVD],
                    po, 1.0, bvb[:], ALU.mult, ALU.add,
                    waits=[(pe, n_mm)], inc=(ve, 1))
            b_pe_done[sg] = pe.n
        while precasts:
            precasts.pop(0)()

        nsl = S // 512
        n_kmax_pe = 0
        km_red = [0, 0]
        for kv in range(KV):
            nparts = min(8, nsl)
            for sl in range(nsl):
                wts = ([(pe, b_pe_done[NSG - 1])] if (kv == 0 and sl == 0)
                       else [(pe, n_kmax_pe)])
                nq = P.op("scalar", "activation", kpre[:, 0:512],
                          k_fm[:, kv * S + sl * 512:kv * S + (sl + 1) * 512],
                          AF.Square, bias=0.0, scale=1.0, waits=wts,
                          inc=(ac, 1))
                n_kmax_pe = P.op("tensor", "matmul",
                                 pA[0:1, (sl % 2) * 512:(sl % 2) * 512 + 512],
                                 ONESC, kpre[:, 0:512], start=True, stop=True,
                                 waits=[(ac, nq), (ac, acB),
                                        (ve, km_red[sl % 2])], inc=(pe, 1))
                km_red[sl % 2] = P.op("vector", "reduce_max",
                     km2[0:1, KV + H + (sl % 8):KV + H + (sl % 8) + 1],
                     pA[0:1, (sl % 2) * 512:(sl % 2) * 512 + 512], axis=AX.X,
                     waits=[(pe, n_kmax_pe)], inc=(ve, 1))
            P.op("vector", "reduce_max", km2[0:1, kv:kv + 1],
                 km2[0:1, KV + H:KV + H + nparts], axis=AX.X,
                 waits=[(ve, ve.n)], inc=(ve, 1))
        kmi = km2.ap() if False else km2
        for kv2 in range(KV):
            c0 = KV + H + kv2
            P.op("vector", "tensor_scalar",
                 km2[0:1, c0:c0 + 1].bitcast(I32),
                 km2[0:1, kv2:kv2 + 1].bitcast(I32), 1, None,
                 ALU.arith_shift_right, inc=(ve, 1))
            P.op("vector", "tensor_scalar",
                 km2[0:1, c0:c0 + 1].bitcast(I32),
                 km2[0:1, c0:c0 + 1].bitcast(I32), 0x1FBD1DF5, None,
                 ALU.add, inc=(ve, 1))
        for hh in range(H):
            P.op("vector", "tensor_scalar", km2[0:1, KV + hh:KV + hh + 1],
                 km2[0:1, KV + H + hh // G:KV + H + hh // G + 1], -1.1, None,
                 ALU.mult, inc=(ve, 1))
        n_km2 = ve.n

        run_C = cfg.phases in ("all", "BC", "BCD1")
        run_D1 = cfg.phases in ("all", "BCD1")
        run_D2 = cfg.phases == "all"
        w1s = {"rb": 0, "st": 0, "ve": 0, "ac": 0, "sp": 0, "ld": 0, "nrc": 0}
        i32g = {"ve": 0}

        def w1_step():
            rb = w1s["rb"]
            if rb >= NW1 or not run_C:
                return
            st_ = w1s["st"]
            if st_ == 0:
                w1s["ld"] = P.op("sync", "dma_start", out=w1f[:],
                                 in_=w1_d[rb * 128:(rb + 1) * 128, :],
                                 waits=[(ac, w1s["ac"]), (ve, w1s["ve"])]
                                 if rb else (), inc=(lw1, 16))
            elif st_ == 1:
                P.op("vector", "reduce_max", stat[:, 8:9], w1f[:], axis=AX.X,
                     apply_absolute_value=True,
                     waits=[(lw1, w1s["ld"]), (ac, w1s["ac"])], inc=(ve, 1))
                P.op("vector", "tensor_scalar", stat[:, 9:10], stat[:, 8:9],
                     1.0 / 127.0, 1e-8, ALU.mult, ALU.max, inc=(ve, 1))
                w1s["nrc"] = P.op("vector", "reciprocal", stat[:, 10:11],
                                  stat[:, 9:10], inc=(ve, 1))
            else:
                nh = max(1, (C // 512) // 2)
                h0 = 0 if st_ == 2 else nh
                for hf in range(h0, h0 + nh):
                    na = P.op("scalar", "activation", i32q[:],
                              w1f[:, hf * 512:(hf + 1) * 512], AF.Identity,
                              bias=0.0, scale=stat[:, 10:11],
                              waits=[(ve, w1s["nrc"]), (sp_w1, w1s["sp"]),
                                     (ve, i32g["ve"])], inc=(ac, 1))
                    w1s["ac"] = na
                    w1s["ve"] = P.op("vector", "tensor_scalar",
                                     w1qb[:, (hf % 2) * 512:(hf % 2) * 512 + 512],
                                     i32q[:], stat[:, 9:10], None, ALU.mult,
                                     waits=[(ac, na)], inc=(ve, 1))
                    i32g["ve"] = w1s["ve"]
                w1s["sp"] = P.op("gpsimd", "dma_start",
                                 out=w1q_d[rb * 128:(rb + 1) * 128,
                                           h0 * 512:(h0 + nh) * 512],
                                 in_=w1qb[:, (h0 % 2) * 512:
                                          (h0 % 2) * 512 + nh * 512],
                                 waits=[(ve, w1s["ve"])],
                                 inc=(sp_w1, 16))
            w1s["st"] += 1
            if w1s["st"] == 4:
                w1s["st"] = 0
                w1s["rb"] += 1


        exp_slot2 = [acB, acB]
        pt_guard0 = [[], [], []]
        at_cnt = n_bv
        ve_recip = n_km2
        ve_recb = n_bo
        ve_sgc = [n_km2]
        n_woload = 0
        last_arena_pe = n_kmax_pe
        spill_x2 = [0]
        spill_xq = 0
        n_qmm_pc = [0, 0]
        wq_cnt = [0, 0]
        nxtb = [0]
        ve_y1 = [at_cnt, at_cnt]
        mix_nrd = [0]
        dbg_at_off = 0
        PSC2 = [pA, pB]
        J_PRE = NS - 14 if NS >= 20 else max(2, NS - 6)

        def emit_wq_load(h, wq_w):
            wqb = arena[:, A_WQB + (h % 2) * NC * 128:
                        A_WQB + (h % 2 + 1) * NC * 128]
            wq_cnt[h % 2] = P.op("sync", "dma_start",
                                 out=wqb.rearrange("p (c d) -> p c d", d=128),
                                 in_=wqbf_d[h]
                                 .rearrange("(c p) d -> p c d", p=128),
                                 waits=list(wq_w) + [(pcxq[h // 4],
                                                      pcx_wq[h // 4])],
                                 inc=(lwq[h % 2], 16))

        def emit_q_chain(g, h, q_extra=(), boundary=False):
            wqb = arena[:, A_WQB + (h % 2) * NC * 128:
                        A_WQB + (h % 2 + 1) * NC * 128]
            qpre = arena[:, A_QPRE:A_QPRE + TG]
            qh = arena[:, A_QH + (h % 2) * TG:A_QH + (h % 2 + 1) * TG]
            dg = [(sod, sod.n)] if cfg.debug else []
            nq = 0
            for th in range(TH):
                wts = []
                if th == 0:
                    wts = [(lwq[h % 2], wq_cnt[h % 2]), (lxc, nxtb[0]),
                           (ve, ve_sgc[0])] + list(q_extra)
                for c in range(NC):
                    nq = P.op("tensor", "matmul",
                              pC[:, th * 512:th * 512 + 512],
                              wqb[:, c * 128:(c + 1) * 128],
                              xTb[:, c * TG + th * 512:c * TG + th * 512 + 512],
                              start=(c == 0), stop=(c == NC - 1),
                              waits=wts if c == 0 else (),
                              inc=(pe, 1) if c == NC - 1 else None)
            n_qmm_pc[h % 2] = nq
            nqpre = P.op("scalar", "activation", qpre, pC[:, 0:TG],
                         AF.Identity, bias=bq[:, h:h + 1], scale=1.0,
                         waits=[(pe, nq)] + dg, inc=(ac, 1))
            yield (nqpre, 0, 0)
            n_r = 0
            for th in range(TH):
                n_r = P.op("tensor", "matmul",
                           pC[:, th * 512:th * 512 + 512], RT_(h),
                           qpre[:, th * 512:th * 512 + 512],
                           start=True, stop=True,
                           waits=[(ac, nqpre)] if th == 0 else (),
                           inc=(pe, 1))
            nqh = P.op("scalar", "activation", qh, pC[:, 0:TG], AF.Identity,
                       bias=0.0, scale=scalec[:], waits=[(pe, n_r)] + dg,
                       inc=(ac, 1))
            nsq = P.op("scalar", "activation", qpre, qh, AF.Square,
                       bias=0.0, scale=1.0, inc=(ac, 1))
            yield (nqpre, nqh, 0)
            nsm = 0
            for th in range(TH):
                nsm = P.op("tensor", "matmul",
                           pC[:, th * 512:th * 512 + 512], ONESB,
                           qpre[:, th * 512:th * 512 + 512],
                           start=True, stop=True,
                           waits=[(ac, nsq), (ve, ve_sgc[0])]
                           if th == 0 else (), inc=(pe, 1))
            sg_w = [(pe, nsm)]
            if boundary:
                sg_w += [(lxr[0], lxr[0].n)]
            nsgc = P.op("vector", "tensor_copy", sgrow, pC[0:1, 0:TG],
                        waits=sg_w + dg, inc=(ve, 1))
            ve_sgc[0] = nsgc
            P.op("vector", "tensor_scalar", sgrow.bitcast(I32),
                 sgrow.bitcast(I32), 1, None, ALU.arith_shift_right,
                 inc=(ve, 1))
            P.op("vector", "tensor_scalar", sgrow.bitcast(I32),
                 sgrow.bitcast(I32), 0x1FBD1DF5, None, ALU.add, inc=(ve, 1))
            P.op("vector", "tensor_scalar", sgrow, sgrow,
                 km2[0:1, KV + h:KV + h + 1], None, ALU.mult, inc=(ve, 1))
            sgb = attn[0:1, h * TG:(h + 1) * TG]
            nrow = P.op("vector", "tensor_copy", sgb, sgrow, inc=(ve, 1))
            nbc = 0
            for th in range(TH):
                nbc = P.op("tensor", "matmul",
                           pC[:, th * 512:th * 512 + 512], onesrb[:],
                           sgb[0:1, th * 512:th * 512 + 512],
                           start=True, stop=True,
                           waits=[(ve, nrow), (ve, nsgc)] if th == 0 else (),
                           inc=(pe, 1))
            sgbb = sgbb_bufs[h % 2]
            ev_w = [(pe, nbc), (sp_x2, spill_x2[0]), (sp_xq, spill_xq)]
            nsig = P.op("vector", "tensor_copy", sgbb, pC[:, 0:TG],
                        waits=ev_w, inc=(ve, 1))
            ve_sgc[0] = nsig
            yield (nqpre, nqh, nsig)

        for g in range(NG if run_C else 0):
            nxtb[0] = P.op("sync", "dma_start",
                           out=xTb.rearrange("p (c s) -> p c s", s=TG),
                           in_=xTbf_d[:, g * TG:(g + 1) * TG]
                           .rearrange("(c p) s -> p c s", p=128),
                           waits=[(pe, last_arena_pe), (pcxx[g], pcx_x[g])],
                           inc=(lxc, 16))
            wq_w0 = ([(pe, b_pe_done[NSG - 1])] if g == 0
                     else [(pe, last_arena_pe)])
            emit_wq_load(0, wq_w0)
            if H > 1:
                emit_wq_load(1, wq_w0)
            bx = ([(ve, n_km2)] if g == 0 else
                  [(ve, ve_y1[0]), (ve, ve_y1[1])])
            bchain = emit_q_chain(g, 0, q_extra=bx, boundary=(g > 0))
            next(bchain)
            next(bchain)
            nqpre, nqh, nsigdone = next(bchain)
            sgbb_cur = sgbb_bufs[0]
            if cfg.debug and g == 0:
                P.op("gpsimd", "dma_start", out=dbg["q"],
                     in_=arena[:, A_QH:A_QH + TG],
                     waits=[(ac, nqh)], inc=(sod, 16))
                P.op("gpsimd", "dma_start", out=dbg["sg"], in_=sgrow,
                     waits=[(ve, nsigdone)], inc=(sod, 16))
            for h in range(H):
                kv = h // G
                qh = arena[:, A_QH + (h % 2) * TG:A_QH + (h % 2 + 1) * TG]
                chain = None
                chain_res = None
                pt_sod = {0: 0, 1: 0, 2: 0}
                tree = TreeSum(P, ve, [
                    arena[:, A_TREE + k * TG:A_TREE + (k + 1) * TG]
                    for k in range(5)])
                rd = [None] * NS
                npv = [0] * NS
                treed = [0] * NS
                ptr = [None] * NS
                for j in range(NS):
                    slot2 = j % 2
                    slot3 = j % 3
                    psc = PSC2[slot2]
                    n_sc = 0
                    for th in range(TH):
                        wts = []
                        if th == 0:
                            wts = [(ac, exp_slot2[slot2])]
                            if j == 0:
                                wts += [(ve, nsigdone), (ve, ve_recip)]
                                if h == 0 and g > 0:
                                    wts += [(ve, ve_y1[0])]
                            if j == 1:
                                wts += [(ve, ve_recb)]
                                if h == 0 and g > 0:
                                    wts += [(ve, ve_y1[1])]
                        P.op("tensor", "matmul",
                             psc[:, th * 512:th * 512 + 512],
                             k_fm[:, kv * S + j * 128:kv * S + j * 128 + 128],
                             qh[:, th * 512:th * 512 + 512],
                             start=True, stop=False,
                             waits=wts, inc=None)
                        n_sc = P.op("tensor", "matmul",
                                    psc[:, th * 512:th * 512 + 512],
                                    ONESF,
                                    sgbb_cur[:, th * 512:th * 512 + 512],
                                    start=False, stop=True, inc=(pe, 1))
                    ptreg = arena[:, A_PT + slot3 * TG:A_PT + (slot3 + 1) * TG]
                    ptr[j] = ptreg
                    pg = ([(pe, npv[j - 3]), (ve, treed[j - 3])]
                          if j >= 3 else pt_guard0[slot3])
                    dg = [(sod, pt_sod[slot3])] if (cfg.debug and
                                                    pt_sod[slot3]) else []
                    mixed = (g * NMIX <= j < (g + 1) * NMIX)
                    if mixed:
                        pts = arena[:, A_PTS:A_PTS + TG]
                        nex = P.op("scalar", "activation", pts, psc[:, 0:TG],
                                   AF.Exp, bias=beta[:, g * NS + j:g * NS + j + 1],
                                   scale=1.0,
                                   waits=[(pe, n_sc), (ve, mix_nrd[0])],
                                   inc=(ac, 1))
                        off = (NMIX - 1 - (j - g * NMIX)) * 128
                        nrd = P.op("vector", "tensor_tensor", ptreg, pts,
                                   mbase[:, off:off + TG], ALU.mult,
                                   waits=[(ac, nex)] + pg + dg,
                                   inc=(ve, 1))
                        mix_nrd[0] = nrd
                        ready = (ve, nrd)
                    else:
                        nex = P.op("scalar", "activation", ptreg, psc[:, 0:TG],
                                   AF.Exp, bias=beta[:, g * NS + j:g * NS + j + 1],
                                   scale=1.0,
                                   waits=[(pe, n_sc)] + pg + dg, inc=(ac, 1))
                        ready = (ac, nex)
                    exp_slot2[slot2] = ac.n
                    rd[j] = ready
                    if cfg.debug and g == 0 and h == 0 and j < 4:
                        nd = P.op("gpsimd", "dma_start",
                                  out=dbg["pt"][:, j * TG:(j + 1) * TG],
                                  in_=ptreg, waits=[ready], inc=(sod, 16))
                        pt_sod[slot3] = nd
                    if j >= 1:
                        jp = j - 1
                        for th in range(TH):
                            wts = []
                            if th == 0:
                                wts = [rd[jp]]
                                if jp == 0:
                                    wts.append((ve, at_cnt))
                            npv[jp] = P.op(
                                "tensor", "matmul",
                                pV[:, th * 512:th * 512 + 512],
                                v_tm[:, jp * KVD + kv * 128:
                                     jp * KVD + kv * 128 + 128],
                                ptr[jp][:, th * 512:th * 512 + 512],
                                start=(jp == 0), stop=False,
                                waits=wts, inc=(pe, 1))
                    if j % 2 == 1:
                        tree.feed_pair(ptr[j - 1], rd[j - 1], ptreg, ready)
                        treed[j - 1] = treed[j] = ve.n
                        w1_step()
                    if h + 1 < H:
                        if j == J_PRE:
                            chain = emit_q_chain(g, h + 1)
                            chain_res = next(chain)
                        elif j == J_PRE + 1:
                            chain_res = next(chain)
                        elif j == J_PRE + 2:
                            chain_res = next(chain)
                    if j == 6 and h + 2 < H:
                        emit_wq_load(h + 2, [(pe, n_qmm_pc[h % 2])])
                jp = NS - 1
                for th in range(TH):
                    npv[jp] = P.op(
                        "tensor", "matmul", pV[:, th * 512:th * 512 + 512],
                        v_tm[:, jp * KVD + kv * 128:jp * KVD + kv * 128 + 128],
                        ptr[jp][:, th * 512:th * 512 + 512],
                        start=False, stop=True,
                        waits=[rd[jp]] if th == 0 else (), inc=(pe, 1))
                npv_last = npv[jp]
                for s3 in range(3):
                    jl = max(jj for jj in range(NS) if jj % 3 == s3)
                    pt_guard0[s3] = [(pe, npv[jl]), (ve, treed[jl])]
                root_ap, root_rdy = tree.root()
                if cfg.debug and g == 0 and h == 0:
                    P.op("gpsimd", "dma_start", out=dbg["sm"], in_=root_ap,
                         waits=[root_rdy], inc=(sod, 16))
                n_sums = 0
                for th in range(TH):
                    n_sums = P.op("tensor", "matmul",
                                  pA[:, th * 512:th * 512 + 512], ONESB,
                                  root_ap[:, th * 512:th * 512 + 512],
                                  start=True, stop=True,
                                  waits=[root_rdy, (ac, exp_slot2[0])]
                                  if th == 0 else (), inc=(pe, 1))
                if C >= 2 * TG:
                    sums_s = xres[:, 0:TG]
                    praw = xres[:, TG:2 * TG]
                    ve_recip = P.op("vector", "tensor_copy", sums_s,
                                    pA[:, 0:TG], waits=[(pe, n_sums)],
                                    inc=(ve, 1))
                    ve_recb = ve_recip
                    at_cnt = P.op("vector", "tensor_copy", praw, pV[:, 0:TG],
                                  waits=[(pe, npv_last)], inc=(ve, 1))
                    P.op("vector", "reciprocal", recipb[:], sums_s,
                         inc=(ve, 1))
                    at_fin = P.op("vector", "tensor_tensor",
                                  attn[:, h * TG:(h + 1) * TG], praw,
                                  recipb[:], ALU.mult, inc=(ve, 1))
                else:
                    ve_recip = P.op("vector", "reciprocal", recipb[:],
                                    pA[:, 0:TG], waits=[(pe, n_sums)],
                                    inc=(ve, 1))
                    ve_recb = ve_recip
                    at_cnt = P.op("vector", "tensor_tensor",
                                  attn[:, h * TG:(h + 1) * TG], pV[:, 0:TG],
                                  recipb[:], ALU.mult,
                                  waits=[(pe, npv_last), (ve, ve_recip)],
                                  inc=(ve, 1))
                    at_fin = at_cnt
                if chain_res is not None:
                    nqpre, nqh, nsigdone = chain_res
                    sgbb_cur = sgbb_bufs[(h + 1) % 2]
            last_arena_pe = pe.n
            at_cnt = at_fin
            if cfg.debug:
                P.op("gpsimd", "dma_start",
                     out=dbg["at"][:, dbg_at_off:dbg_at_off + H * TG],
                     in_=attn[:], waits=[(ve, at_cnt)], inc=(sod, 16))
                dbg_at_off += H * TG
            n_woload = P.op("sync", "dma_start",
                            out=arena[:, 0:NC * C]
                            .rearrange("p (c o) -> p c o", o=C),
                            in_=woTbf_d.rearrange("(c p) o -> p c o", p=128),
                            waits=[(pe, last_arena_pe), (ve, at_cnt),
                                   (ac, ac.n), (pcxw, pcx_wo[0])],
                            inc=(lwo, 16))
            PSY = [(pA, pB), (pC, pV)]
            ve_y1 = [at_cnt, at_cnt]
            last_nx2 = [0]
            for t in range(NTG):
                tt = g * NTG + t
                nx = P.op("sync", "dma_start", out=xres[:],
                          in_=xown_d[tt * 128:(tt + 1) * 128, :],
                          waits=[(ve, ve_y1[0]), (ve, ve_y1[1])] +
                          ([(ve, at_cnt)] if t == 0 else []),
                          inc=(lxr[0], 16))
                nxbo = P.op("vector", "tensor_tensor", xres[:], xres[:],
                            bob[:], ALU.add, waits=[(lxr[0], nx)],
                            inc=(ve, 1))
                pY0, pY1 = PSY[t % 2]
                n_wo = 0
                for o in range(C // 512):
                    pY = pY0 if o < 2 else pY1
                    po = pY[:, (o % 2) * 512:(o % 2) * 512 + 512]
                    for c2 in range(H):
                        wts = []
                        if c2 == 0:
                            wts = [(lwo, n_woload), (ve, ve_y1[t % 2])]
                        n_wo = P.op(
                            "tensor", "matmul", po,
                            attn[:, c2 * TG + t * 128:c2 * TG + t * 128 + 128],
                            arena[:, c2 * C + o * 512:c2 * C + o * 512 + 512],
                            start=(c2 == 0), stop=(c2 == H - 1),
                            waits=wts if c2 == 0 else (),
                            inc=(pe, 1) if c2 == H - 1 else None)
                nv = 0
                for half in range(max(1, C // 1024)):
                    pY = pY0 if half == 0 else pY1
                    nv = P.op("vector", "tensor_tensor",
                              y1[:, half * 1024:(half + 1) * 1024],
                              pY[:, 0:1024],
                              xres[:, half * 1024:half * 1024 + 1024],
                              ALU.add,
                              waits=[(pe, n_wo), (ve, nxbo),
                                     (ac, last_nx2[0])],
                              inc=(ve, 1))
                ve_y1[t % 2] = nv
                nsq2 = 0
                for hf in range(C // 1024):
                    nsq2 = P.op("scalar", "activation", xqt[:],
                                y1[:, hf * 1024:(hf + 1) * 1024], AF.Square,
                                bias=0.0, scale=1.0,
                                accum_out=stat[:, hf * 13:hf * 13 + 1],
                                waits=[(ve, nv), (sp_xq, spill_xq)],
                                inc=(ac, 1))
                if C > 1024:
                    P.op("vector", "tensor_tensor", stat[:, 0:1],
                         stat[:, 0:1], stat[:, 13:14], ALU.add,
                         waits=[(ac, nsq2)], inc=(ve, 1))
                P.op("vector", "reduce_sum", stat[:, 1:2], y1[:], axis=AX.X,
                     waits=[(ve, nv)], inc=(ve, 1))
                P.op("vector", "tensor_scalar", stat[:, 2:3], stat[:, 1:2],
                     1.0 / C, None, ALU.mult, inc=(ve, 1))
                P.op("vector", "tensor_tensor", stat[:, 3:4], stat[:, 2:3],
                     stat[:, 2:3], ALU.mult, inc=(ve, 1))
                P.op("vector", "tensor_scalar", stat[:, 4:5], stat[:, 0:1],
                     1.0 / C, None, ALU.mult, waits=[(ac, nsq2)],
                     inc=(ve, 1))
                P.op("vector", "tensor_tensor", stat[:, 5:6],
                     stat[:, 4:5], stat[:, 3:4], ALU.subtract, inc=(ve, 1))
                nvar = P.op("vector", "tensor_scalar", stat[:, 5:6],
                            stat[:, 5:6], 1e-5, None, ALU.add, inc=(ve, 1))
                nsd = P.op("scalar", "activation", stat[:, 6:7], stat[:, 5:6],
                           AF.Sqrt, bias=0.0, scale=1.0,
                           waits=[(ve, nvar)], inc=(ac, 1))
                nrstd = P.op("vector", "reciprocal", stat[:, 7:8],
                             stat[:, 6:7], waits=[(ac, nsd)], inc=(ve, 1))
                nnb = P.op("vector", "scalar_tensor_tensor", stat[:, 12:13],
                           stat[:, 2:3], -1.0, stat[:, 7:8], ALU.mult,
                           ALU.mult, inc=(ve, 1))
                for hf in range(C // 1024):
                    nx2 = P.op("scalar", "activation", x2t[:],
                               y1[:, hf * 1024:(hf + 1) * 1024], AF.Identity,
                               bias=stat[:, 12:13], scale=stat[:, 7:8],
                               waits=[(ve, nnb), (sp_x2, spill_x2[0])],
                               inc=(ac, 1))
                    last_nx2[0] = nx2
                    spill_x2[0] = P.op("gpsimd", "dma_start",
                                       out=x2_d[tt * 128:(tt + 1) * 128,
                                                hf * 1024:(hf + 1) * 1024],
                                       in_=x2t[:], waits=[(ac, nx2)],
                                       inc=(sp_x2, 16))
                P.op("vector", "reduce_max", stat[:, 8:9], y1[:], axis=AX.X,
                     waits=[(ve, nv)], inc=(ve, 1))
                P.op("vector", "reduce_max", stat[:, 3:4], y1[:], axis=AX.X,
                     negate=True, inc=(ve, 1))
                P.op("vector", "tensor_scalar", stat[:, 8:9], stat[:, 8:9],
                     stat[:, 7:8], None, ALU.mult, inc=(ve, 1))
                P.op("vector", "tensor_tensor", stat[:, 8:9], stat[:, 8:9],
                     stat[:, 12:13], ALU.add, inc=(ve, 1))
                P.op("vector", "tensor_scalar", stat[:, 3:4], stat[:, 3:4],
                     stat[:, 7:8], None, ALU.mult, inc=(ve, 1))
                P.op("vector", "tensor_tensor", stat[:, 3:4], stat[:, 3:4],
                     stat[:, 12:13], ALU.subtract, inc=(ve, 1))
                P.op("vector", "tensor_tensor", stat[:, 8:9], stat[:, 8:9],
                     stat[:, 3:4], ALU.max, inc=(ve, 1))
                P.op("vector", "tensor_scalar", stat[:, 9:10], stat[:, 8:9],
                     1.0 / 127.0, 1e-8, ALU.mult, ALU.max, inc=(ve, 1))
                P.op("vector", "tensor_copy", sx_all[:, tt:tt + 1],
                     stat[:, 9:10], inc=(ve, 1))
                nrq = P.op("vector", "reciprocal", stat[:, 10:11],
                           stat[:, 9:10], inc=(ve, 1))
                P.op("vector", "tensor_scalar", stat[:, 4:5], stat[:, 7:8],
                     stat[:, 10:11], None, ALU.mult, inc=(ve, 1))
                nrq = P.op("vector", "tensor_scalar", stat[:, 5:6],
                           stat[:, 12:13], stat[:, 10:11], None, ALU.mult,
                           inc=(ve, 1))
                nxq = 0
                for hf in range(C // 512):
                    na = P.op("scalar", "activation", i32q[:],
                              y1[:, hf * 512:(hf + 1) * 512], AF.Identity,
                              bias=stat[:, 5:6], scale=stat[:, 4:5],
                              waits=[(ve, nrq), (ve, i32g["ve"]),
                                     (sp_xq, spill_xq)], inc=(ac, 1))
                    nxq = P.op("vector", "tensor_copy",
                               xqt[:, (hf % 2) * 512:(hf % 2) * 512 + 512],
                               i32q[:], waits=[(ac, na)], inc=(ve, 1))
                    i32g["ve"] = nxq
                    if hf % 2 == 1:
                        spill_xq = P.op("gpsimd", "dma_start",
                                        out=xq_d[tt * 128:(tt + 1) * 128,
                                                 (hf - 1) * 512:(hf + 1) * 512],
                                        in_=xqt[:], waits=[(ve, nxq)],
                                        inc=(sp_xq, 16))
                last_nx2[0] = max(last_nx2[0], ac.n)
            last_arena_pe = pe.n
        while run_C and w1s["rb"] < NW1:
            w1_step()
        if cfg.debug:
            P.op("gpsimd", "dma_start", out=dbg["k"], in_=k_fm[:],
                 waits=[(ac, acB)], inc=(sod, 16))
            P.op("gpsimd", "dma_start", out=dbg["v"], in_=v_tm[:],
                 waits=[(ve, ve.n)], inc=(sod, 16))

        P.wait("gpsimd", sp_w1, sp_w1.n)
        P.wait("gpsimd", sp_x2, sp_x2.n)
        P.wait("gpsimd", sp_xq, sp_xq.n)
        esc.close()
        barrier()

        if not run_D1:
            NTx, NW2x, NOPx = 0, 0, 0
        else:
            NTx, NW2x, NOPx = NT, NW2, I // 512
        es1 = contextlib.ExitStack()
        xq_fm = sb(es1, "xq_fm", [128, NC * T], BF16)
        w1t = sb(es1, "w1t", [128, 4 * NC * 512], BF16)
        w2f = sb(es1, "w2f", [128, I], F32)
        w2qb = sb(es1, "w2qb", [128, I], BF16)
        w2ts = sb(es1, "w2ts", [128, C], BF16)
        w2ts2 = sb(es1, "w2ts2", [128, C], BF16)
        xqrow = sb(es1, "xqrow", [128, C], BF16)
        i32d = sb(es1, "i32d", [128, I // 2], I32)
        htile = sb(es1, "htile", [128, 512], F32)
        utile = sb(es1, "utile", [128, 512], BF16)
        pG = es1.enter_context(nc.psum_tensor("pG", [128, 1024], F32))
        pH = es1.enter_context(nc.psum_tensor("pH", [128, 1024], F32))
        pT = es1.enter_context(nc.psum_tensor("pT", [128, 1024], BF16))
        pX = es1.enter_context(nc.psum_tensor("pX", [128, 512], F32))

        xq_state = {"tt": 0, "evac": 0}
        xqfm_t = [0] * max(NT, 1)

        def xq_tile():
            tt = xq_state["tt"]
            if tt >= NTx:
                return
            xq_state["tt"] += 1
            nld = P.op("gpsimd", "dma_start", out=xqrow[:],
                       in_=xq_d[tt * 128:(tt + 1) * 128, :],
                       waits=[(ve, xq_state["evac"])] if tt else (),
                       inc=(lxq, 16))
            for cq in range(NC // 8):
                n_tr = 0
                for k in range(8):
                    c = cq * 8 + k
                    n_tr = P.op("tensor", "transpose",
                                pT[:, k * 128:(k + 1) * 128],
                                xqrow[:, c * 128:(c + 1) * 128], IDENT,
                                waits=[(lxq, nld), (ve, xq_state["evac"])]
                                if k == 0 else (), inc=(pe, 1))
                dst = xq_fm[:, cq * 8 * T + tt * 128:]
                dst = bass.AP(dst.tensor, dst.offset,
                              [[dst.ap[0][0], 128], [T, 8], [1, 128]])
                xq_state["evac"] = P.op("vector", "tensor_copy", dst,
                                        pT[:, 0:1024]
                                        .rearrange("p (k c) -> p k c", k=8),
                                        waits=[(pe, n_tr)], inc=(ve, 1))
            xqfm_t[tt] = xq_state["evac"]

        w2s = {"rb": 0, "sp": 0, "ve": 0, "i": 0, "tsp2": [0, 0]}

        def w2_block():
            rb = w2s["rb"]
            if rb >= NW2x:
                return
            w2s["rb"] += 1
            nl = P.op("scalar", "dma_start", out=w2f[:],
                      in_=w2_d[rb * 128:(rb + 1) * 128, :],
                      waits=[(ve, w2s["ve"])] if rb else (), inc=(lw2, 16))
            P.op("vector", "reduce_max", stat[:, 8:9], w2f[:], axis=AX.X,
                 apply_absolute_value=True, waits=[(lw2, nl)], inc=(ve, 1))
            P.op("vector", "tensor_scalar", stat[:, 9:10], stat[:, 8:9],
                 1.0 / 127.0, 1e-8, ALU.mult, ALU.max, inc=(ve, 1))
            nrc = P.op("vector", "reciprocal", stat[:, 10:11], stat[:, 9:10],
                       inc=(ve, 1))
            nqv = 0
            for hf in range(2):
                na = P.op("scalar", "activation", i32d[:],
                          w2f[:, hf * (I // 2):(hf + 1) * (I // 2)],
                          AF.Identity, bias=0.0, scale=stat[:, 10:11],
                          waits=[(ve, nrc if hf == 0 else nqv)], inc=(ac, 1))
                nqv = P.op("vector", "tensor_scalar",
                           w2qb[:, hf * (I // 2):(hf + 1) * (I // 2)],
                           i32d[:], stat[:, 9:10], None, ALU.mult,
                           waits=[(ac, na), (sp_w2, w2s["sp"])], inc=(ve, 1))
            w2s["ve"] = nqv
            w2s["sp"] = P.op("gpsimd", "dma_start",
                             out=w2q_d[rb * 128:(rb + 1) * 128, :],
                             in_=w2qb[:], waits=[(ve, nqv)], inc=(sp_w2, 16))

        def w2_respill(k):
            while k > 0 and w2s["rb"] >= NW2x and w2s["i"] < NI:
                k -= 1
                i = w2s["i"]
                w2s["i"] += 1
                buf = w2ts2 if i % 2 else w2ts
                sem = lw2tb if i % 2 else lw2t
                ntr = P.op("sync", "dma_start",
                           out=buf[:],
                           in_=w2q_d[:, i * 128:(i + 1) * 128],
                           transpose=True,
                           waits=[(sp_w2, w2s["sp"]),
                                  (sp_w2u if i % 2 else sp_w2t,
                                   w2s["tsp2"][i % 2])],
                           inc=(sem, 16))
                for ob in range(NOB):
                    w2s["tsp2"][i % 2] = P.op(
                        "gpsimd", "dma_start",
                        out=w2qT_d[ob * I + i * 128:ob * I + (i + 1) * 128, :],
                        in_=buf[:, ob * 512:(ob + 1) * 512],
                        waits=[(sem, ntr)],
                        inc=(sp_w2u if i % 2 else sp_w2t, 16))

        NOP = I // 512
        w1rdy = {0: [], 1: []}

        def emit_w1t_load(op, wts=()):
            buf = op % 2
            base = buf * 2 * NC * 512
            first = True
            nn = 0
            for gu in range(2):
                rows = op * 512 + gu * I
                for c in range(NC):
                    sem = lw1t[buf] if c % 2 == 0 else lw1s[buf]
                    nn = P.op("sync", "dma_start",
                              out=w1t[:, base + gu * NC * 512 + c * 512:
                                      base + gu * NC * 512 + (c + 1) * 512],
                              in_=w1q_d[rows:rows + 512,
                                        c * 128:(c + 1) * 128],
                              transpose=True,
                              waits=wts if first else (),
                              inc=(sem, 16))
                    first = False
            w1rdy[buf] = [(lw1t[buf], lw1t[buf].n), (lw1s[buf], lw1s[buf].n)]

        w1pe = {"evac": 0}

        def w1pe_rowchunk(op, idx, wts_first=()):
            buf = op % 2
            base = buf * 2 * NC * 512
            gu, rc = divmod(idx, 4)
            rows = op * 512 + gu * I + rc * 128
            wl = list(wts_first) if wts_first else [(ve, w1pe["evac"])]
            nld = P.op("sync", "dma_start", out=xqrow[:],
                       in_=w1q_d[rows:rows + 128, :],
                       waits=wl, inc=(lwr, 16))
            for hc in range(NC // 8):
                n_tr = 0
                for k in range(8):
                    c = hc * 8 + k
                    n_tr = P.op("tensor", "transpose",
                                pT[:, k * 128:(k + 1) * 128],
                                xqrow[:, c * 128:(c + 1) * 128], IDENT,
                                waits=[(lwr, nld), (ve, w1pe["evac"])]
                                if k == 0 else (), inc=(pe, 1))
                dst = w1t[:, base + gu * NC * 512 + hc * 8 * 512 + rc * 128:]
                dst = bass.AP(dst.tensor, dst.offset,
                              [[dst.ap[0][0], 128], [512, 8], [1, 128]])
                w1pe["evac"] = P.op("vector", "tensor_copy", dst,
                                    pT[:, 0:1024]
                                    .rearrange("p (k c) -> p k c", k=8),
                                    waits=[(pe, n_tr)], inc=(ve, 1))
            if idx == 2 * 4 - 1:
                w1rdy[buf] = [(ve, w1pe["evac"])]

        if run_D1:
            emit_w1t_load(0)
            if NOPx > 1:
                emit_w1t_load(1)
            xq_tile()
            xq_tile()
        ac_sil = [0, 0]
        ve_hml = [0, 0]
        sph_n = 0
        for op in range(NOPx):
            buf = op % 2
            base = buf * 2 * NC * 512
            wrdy = list(w1rdy[buf])
            pe_mark = pe.n
            w1pe_idx = 0
            for t in range(NT):
                pGU = pG if t % 2 == 0 else pH
                n_g = 0
                for gu in range(2):
                    for c in range(NC):
                        wts2 = []
                        if gu == 0 and c == 0:
                            wts2 = wrdy + [
                                (ve, xqfm_t[t] if op == 0
                                 else xqfm_t[NT - 1]),
                                (ac, ac_sil[t % 2]), (ve, ve_hml[t % 2])]
                        n_g = P.op(
                            "tensor", "matmul",
                            pGU[:, gu * 512:gu * 512 + 512],
                            xq_fm[:, c * T + t * 128:c * T + t * 128 + 128],
                            w1t[:, base + gu * NC * 512 + c * 512:
                                base + gu * NC * 512 + (c + 1) * 512],
                            start=(c == 0), stop=(c == NC - 1),
                            waits=wts2 if (gu == 0 and c == 0) or c == 0 else (),
                            inc=(pe, 1) if c == NC - 1 else None)
                ac_sil[t % 2] = P.op("scalar", "activation", htile[:],
                                     pGU[:, 0:512], AF.Silu, bias=0.0,
                                     scale=sx_all[:, t:t + 1],
                                     waits=[(pe, n_g), (sp_h, sph_n)],
                                     inc=(ac, 1))
                nu = P.op("vector", "tensor_scalar", utile[:],
                          pGU[:, 512:1024], sx_all[:, t:t + 1], None,
                          ALU.mult, waits=[(pe, n_g)], inc=(ve, 1))
                ve_hml[t % 2] = P.op("vector", "tensor_tensor", htile[:],
                                     htile[:], utile[:], ALU.mult,
                                     waits=[(ac, ac_sil[t % 2]), (ve, nu)],
                                     inc=(ve, 1))
                sph_n = P.op("gpsimd", "dma_start",
                             out=h_d[t * 128:(t + 1) * 128,
                                     op * 512:(op + 1) * 512],
                             in_=htile[:], waits=[(ve, ve_hml[t % 2])],
                             inc=(sp_h, 16))
                if op == 0:
                    xq_tile()
                elif op + 1 < NOPx and t % 2 == 0 and w1pe_idx < 8:
                    wf = ()
                    if w1pe_idx == 0:
                        wf = [(pe, pe_mark), (ve, xq_state["evac"])
                              ] if op == 1 else [(pe, pe_mark),
                                                 (ve, w1pe["evac"])]
                    w1pe_rowchunk(op + 1, w1pe_idx, wf)
                    w1pe_idx += 1
            while op + 1 < NOPx and 0 < w1pe_idx < 8:
                w1pe_rowchunk(op + 1, w1pe_idx)
                w1pe_idx += 1
            w2_block()
            w2_block()
            w2_block()
            w2_respill(9)
        while run_D1 and xq_state["tt"] < NTx:
            xq_tile()
        while w2s["rb"] < NW2x:
            w2_block()
        while run_D1 and w2s["i"] < NI:
            w2_respill(4)

        P.wait("gpsimd", sp_h, sp_h.n)
        P.wait("gpsimd", sp_w2t, sp_w2t.n)
        P.wait("gpsimd", sp_w2u, sp_w2u.n)
        es1.close()
        barrier()

        es2 = contextlib.ExitStack()
        w2full = sb(es2, "w2full", [128, NI * 512], BF16)
        hq_fm = sb(es2, "hq_fm", [128, NI * 512], BF16)
        hrow = sb(es2, "hrow", [128, I], F32)
        hqrow = sb(es2, "hqrow", [128, I], BF16)
        i32h = sb(es2, "i32h", [128, I // 2], I32)
        y2g = sb(es2, "y2g", [128, 4 * C], F32)
        y2 = sb(es2, "y2", [128, C], F32)
        x2res = sb(es2, "x2res", [128, 2 * C], BF16)
        outt = sb(es2, "outt", [128, C], F32)
        sq2 = sb(es2, "sq2", [128, C], BF16)
        pD = es2.enter_context(nc.psum_tensor("pD", [128, 1024], F32))
        pE = es2.enter_context(nc.psum_tensor("pE", [128, 1024], F32))
        pT2 = es2.enter_context(nc.psum_tensor("pT2", [128, 1024], BF16))

        NT4 = (NT // 4) if run_D2 else 0
        CH = max(1, (NI + 3) // 4)
        n_out = [0, 0]
        last_nfin = 0
        pe_hq_read = 0
        ve_dn = [0, 0]
        ac_dn = [0, 0]
        hqs = {"tt": 0, "tr": 0, "i32": 0, "hq": 0}
        ve_hq_all = [0] * max(NT, 1)

        def hq_tile():
            tt = hqs["tt"]
            if tt >= 4 * NT4:
                return
            hqs["tt"] += 1
            nlh = P.op("sync", "dma_start", out=hrow[:],
                       in_=h_d[tt * 128:(tt + 1) * 128, :],
                       waits=[(pe, pe_hq_read), (ac, hqs["i32"])]
                       if tt % 4 == 0 else [(ac, hqs["i32"])],
                       inc=(lh, 16))
            P.op("vector", "reduce_max", stat[:, 8:9], hrow[:], axis=AX.X,
                 apply_absolute_value=True, waits=[(lh, nlh)],
                 inc=(ve, 1))
            P.op("vector", "tensor_scalar", stat[:, 9:10], stat[:, 8:9],
                 1.0 / 127.0, 1e-8, ALU.mult, ALU.max, inc=(ve, 1))
            P.op("vector", "tensor_copy", s2g[:, tt % 4:tt % 4 + 1],
                 stat[:, 9:10],
                 waits=[(ac, ac_dn[0]), (ac, ac_dn[1])], inc=(ve, 1))
            nrc = P.op("vector", "reciprocal", stat[:, 10:11],
                       stat[:, 9:10], inc=(ve, 1))
            nhq = 0
            for hf in range(2):
                na = P.op("scalar", "activation", i32h[:],
                          hrow[:, hf * (I // 2):(hf + 1) * (I // 2)],
                          AF.Identity, bias=0.0, scale=stat[:, 10:11],
                          waits=[(ve, nrc if hf == 0 else nhq)],
                          inc=(ac, 1))
                hqs["i32"] = na
                nhq = P.op("vector", "tensor_copy",
                           hqrow[:, hf * (I // 2):(hf + 1) * (I // 2)],
                           i32h[:], waits=[(ac, na),
                                           (pe, hqs["tr"])],
                           inc=(ve, 1))
            for iq in range((NI + 7) // 8):
                k8 = min(8, NI - iq * 8)
                n_tr = 0
                for k in range(k8):
                    i = iq * 8 + k
                    n_tr = P.op("tensor", "transpose",
                                pT2[:, k * 128:(k + 1) * 128],
                                hqrow[:, i * 128:(i + 1) * 128], IDENT,
                                waits=[(ve, nhq), (ve, hqs["hq"])]
                                if k == 0 else (), inc=(pe, 1))
                hqs["tr"] = n_tr
                dst = hq_fm[:, iq * 8 * 512 + (tt % 4) * 128:]
                dst = bass.AP(dst.tensor, dst.offset,
                              [[dst.ap[0][0], 128], [512, k8], [1, 128]])
                hqs["hq"] = P.op("vector", "tensor_copy", dst,
                                 pT2[:, 0:k8 * 128]
                                 .rearrange("p (k c) -> p k c", k=k8),
                                 waits=[(pe, n_tr)], inc=(ve, 1))
            ve_hq_all[tt] = hqs["hq"]

        for tg in range(NT4):
            while hqs["tt"] < tg * 4 + 2:
                hq_tile()
            for ob in range(NOB):
                nw2c = [0] * 4
                for chk in range(4):
                    i0 = chk * CH
                    if i0 >= NI:
                        break
                    i1 = min(NI, i0 + CH)
                    nw2c[chk] = P.op(
                        "sync", "dma_start",
                        out=w2full[:, i0 * 512:i1 * 512]
                        .rearrange("p (i o) -> p i o", i=i1 - i0),
                        in_=w2qT_d[ob * I + i0 * 128:ob * I + i1 * 128, :]
                        .rearrange("(i p) o -> p i o", p=128),
                        waits=[(ac, ac_dn[0]), (ac, ac_dn[1])]
                        if chk == 0 else (),
                        inc=(lwf_[chk], 16))
                for t4 in range(4):
                    if ob == 0 and t4 < 2:
                        hq_tile()
                    pDD = pD if t4 % 2 == 0 else pE
                    po = pDD[:, (ob % 2) * 512:(ob % 2) * 512 + 512]
                    n_d = 0
                    for i in range(NI):
                        wts3 = ()
                        if i == 0:
                            wts3 = [(lwf_[0], nw2c[0]),
                                    (ve, ve_hq_all[tg * 4 + t4]),
                                    (ac, ac_dn[t4 % 2])]
                        elif i % CH == 0:
                            wts3 = [(lwf_[i // CH], nw2c[i // CH])]
                        n_d = P.op(
                            "tensor", "matmul", po,
                            hq_fm[:, i * 512 + t4 * 128:i * 512 + t4 * 128 + 128],
                            w2full[:, i * 512:(i + 1) * 512],
                            start=(i == 0), stop=(i == NI - 1),
                            waits=wts3, inc=(pe, 1) if i == NI - 1 else None)
                    ac_dn[t4 % 2] = P.op(
                        "scalar", "activation",
                        y2g[:, t4 * C + ob * 512:t4 * C + (ob + 1) * 512],
                        po, AF.Identity, bias=0.0, scale=s2g[:, t4:t4 + 1],
                        waits=[(pe, n_d), (ve, ve_dn[t4 % 2])], inc=(ac, 1))
            pe_hq_read = pe.n
            if tg + 1 < NT4:
                hq_tile()
                hq_tile()
            for t4 in range(4):
                tt = tg * 4 + t4
                nx2l = P.op("sync", "dma_start",
                            out=x2res[:, (t4 % 2) * C:(t4 % 2 + 1) * C],
                            in_=x2_d[tt * 128:(tt + 1) * 128, :],
                            waits=[(ve, ve_dn[t4 % 2])],
                            inc=(lx2[t4 % 2], 16))
                nv2 = P.op("vector", "tensor_tensor", y2[:],
                           y2g[:, t4 * C:(t4 + 1) * C],
                           x2res[:, (t4 % 2) * C:(t4 % 2 + 1) * C], ALU.add,
                           waits=[(lx2[t4 % 2], nx2l), (ac, ac_dn[t4 % 2]),
                                  (so_[t4 % 2], n_out[t4 % 2]),
                                  (ac, last_nfin)], inc=(ve, 1))
                ve_dn[t4 % 2] = nv2
                nsq2 = P.op("scalar", "activation", sq2[:], y2[:], AF.Square,
                            bias=0.0, scale=1.0, accum_out=stat[:, 0:1],
                            waits=[(ve, nv2)], inc=(ac, 1))
                P.op("vector", "reduce_sum", stat[:, 1:2], y2[:], axis=AX.X,
                     waits=[(ve, nv2)], inc=(ve, 1))
                P.op("vector", "tensor_scalar", stat[:, 2:3], stat[:, 1:2],
                     1.0 / C, None, ALU.mult, inc=(ve, 1))
                P.op("vector", "tensor_tensor", stat[:, 3:4], stat[:, 2:3],
                     stat[:, 2:3], ALU.mult, inc=(ve, 1))
                P.op("vector", "tensor_scalar", stat[:, 4:5], stat[:, 0:1],
                     1.0 / C, None, ALU.mult, waits=[(ac, nsq2)], inc=(ve, 1))
                P.op("vector", "tensor_tensor", stat[:, 5:6],
                     stat[:, 4:5], stat[:, 3:4], ALU.subtract, inc=(ve, 1))
                nvar = P.op("vector", "tensor_scalar", stat[:, 5:6],
                            stat[:, 5:6], 1e-5, None, ALU.add, inc=(ve, 1))
                nsd = P.op("scalar", "activation", stat[:, 6:7], stat[:, 5:6],
                           AF.Sqrt, bias=0.0, scale=1.0, waits=[(ve, nvar)],
                           inc=(ac, 1))
                P.op("vector", "reciprocal", stat[:, 7:8], stat[:, 6:7],
                     waits=[(ac, nsd)], inc=(ve, 1))
                nnb = P.op("vector", "scalar_tensor_tensor", stat[:, 12:13],
                           stat[:, 2:3], -1.0, stat[:, 7:8], ALU.mult,
                           ALU.mult, inc=(ve, 1))
                nfin = P.op("scalar", "activation", outt[:], y2[:],
                            AF.Identity, bias=stat[:, 12:13],
                            scale=stat[:, 7:8],
                            waits=[(ve, nnb), (so_[0], n_out[0]),
                                   (so_[1], n_out[1])], inc=(ac, 1))
                last_nfin = nfin
                n_out[t4 % 2] = P.op("gpsimd", "dma_start",
                             out=out_d[tt * 128:(tt + 1) * 128, :],
                             in_=outt[:], waits=[(ac, nfin)],
                             inc=(so_[t4 % 2], 16))
        P.wait("gpsimd", so_[0], so_[0].n)
        P.wait("gpsimd", so_[1], so_[1].n)
        if cfg.debug:
            P.wait("gpsimd", sod, sod.n)
        es2.close()

        blk = es.enter_context(nc.Block())

        @blk.sync
        def _(e):
            P.replay("sync", e, nc, collect, fix)

        @blk.tensor
        def _(e):
            P.replay("tensor", e, nc, collect, fix)

        @blk.vector
        def _(e):
            P.replay("vector", e, nc, collect, fix)

        @blk.scalar
        def _(e):
            P.replay("scalar", e, nc, collect, fix)

        @blk.gpsimd
        def _(e):
            P.replay("gpsimd", e, nc, collect, fix)

    return nc


def build(cfg):
    m = {}
    _build(cfg, collect=m)
    return _build(cfg, fix=m)


def make_in_maps(cfg, x, wq, bq_, wk, bk_, wv, bv_, wo, bo_, w1, w2,
                 n_cores=8):
    consts = host_consts(cfg)
    wqT_b = np.ascontiguousarray(
        np.asarray(wq, np.float32).T.reshape(cfg.C, cfg.H, cfg.D)
        .transpose(1, 0, 2))
    shared = dict(
        wqT=wqT_b,
        wkT=np.ascontiguousarray(np.asarray(wk, np.float32).T),
        wvT=np.ascontiguousarray(np.asarray(wv, np.float32).T),
        woT=np.ascontiguousarray(np.asarray(wo, np.float32).T),
        w1=np.ascontiguousarray(np.asarray(w1, np.float32)),
        w2=np.ascontiguousarray(np.asarray(w2, np.float32)),
        bqT=np.ascontiguousarray(
            np.asarray(bq_, np.float32).reshape(cfg.H, cfg.D).T),
        bkT=np.ascontiguousarray(
            np.asarray(bk_, np.float32).reshape(cfg.KV, cfg.D).T),
        bv_r=np.ascontiguousarray(np.asarray(bv_, np.float32).reshape(1, -1)),
        bo_r=np.ascontiguousarray(np.asarray(bo_, np.float32).reshape(1, -1)),
        rts=consts["rts"], mbase=consts["mbase"], ident=consts["ident"],
        ones_col=consts["ones_col"], ones_row=consts["ones_row"],
        ones_rowb=consts["ones_rowb"], ones_f=consts["ones_f"],
        ones_b=consts["ones_b"], scale_col=consts["scale_col"])
    in_maps = []
    for c in range(n_cores):
        b, half = divmod(c, 2)
        r = half * cfg.T
        xrot = np.roll(x[b], -r, axis=0)
        in_maps.append(dict(shared,
                            xT=np.ascontiguousarray(xrot.T),
                            x_own=np.ascontiguousarray(x[b, r:r + cfg.T]),
                            beta=host_beta(cfg, half)))
    return in_maps


def kernel(L0_attn_x, L0_attn_wq, L0_attn_bq, L0_attn_wk, L0_attn_bk,
           L0_attn_wv, L0_attn_bv, L0_attn_wo, L0_attn_bo,
           L0_ffn_w_gate_up, L0_ffn_w_down):
    cfg = Cfg()
    x = np.ascontiguousarray(np.asarray(L0_attn_x, dtype=np.float32))
    in_maps = make_in_maps(cfg, x, L0_attn_wq, L0_attn_bq, L0_attn_wk,
                           L0_attn_bk, L0_attn_wv, L0_attn_bv, L0_attn_wo,
                           L0_attn_bo, L0_ffn_w_gate_up, L0_ffn_w_down)
    nc = build(cfg)
    res = run_bass_kernel_spmd(nc, in_maps, core_ids=list(range(8)))
    B = x.shape[0]
    outs = np.empty((B, cfg.S, cfg.C), dtype=np.float32)
    for c in range(8):
        b, half = divmod(c, 2)
        outs[b, half * cfg.T:(half + 1) * cfg.T] = res.results[c]["out"]
    return outs


# version 17
# speedup vs baseline: 1.0721x; 1.0721x over previous
import math
import contextlib
import numpy as np
import ml_dtypes
import concourse.bass as bass
import concourse.mybir as mybir
from concourse.bass_utils import run_bass_kernel_spmd

F32 = mybir.dt.float32
BF16 = mybir.dt.bfloat16
I32 = mybir.dt.int32
AF = mybir.ActivationFunctionType
ALU = mybir.AluOpType
AX = mybir.AxisListType
BF = ml_dtypes.bfloat16


class Cfg:
    def __init__(self, S=4096, C=2048, I=5632, H=16, KV=4, TG=1024, debug=False,
                 phases="all"):
        self.S, self.C, self.I, self.H, self.KV, self.TG = S, C, I, H, KV, TG
        self.phases = phases
        self.T = S // 2
        self.D = 128
        self.debug = debug
        assert C == H * self.D and C % 1024 == 0 and S % 1024 == 0
        assert TG % 512 == 0 and self.T % TG == 0 and I % 512 == 0
        self.NC = C // 128
        self.NS = S // 128
        self.NG = self.T // TG
        self.NMIX = TG // 128
        self.NI = I // 128
        self.NT = self.T // 128
        self.NTG = TG // 128
        self.G = H // KV
        self.NOB = C // 512


def host_consts(cfg):
    D, H, TG, NMIX = cfg.D, cfg.H, cfg.TG, cfg.NMIX
    half = D // 2
    inv_freq = 1.0 / 10000.0 ** (np.arange(0, D, 2, dtype=np.float64) / D)
    pos = np.arange(H, dtype=np.float64)
    freqs = np.einsum("i,j->ij", pos, inv_freq)
    emb = np.concatenate([freqs, freqs], axis=-1)
    cos, sin = np.cos(emb), np.sin(emb)
    RT = np.zeros((H, D, D), dtype=np.float64)
    for h in range(H):
        R = np.zeros((D, D))
        for d in range(D):
            R[d, d] = cos[h, d]
        for d in range(half):
            R[d, d + half] = -sin[h, d]
            R[d + half, d] = sin[h, d + half]
        RT[h] = R.T
    rts = np.ascontiguousarray(RT.reshape(H * D, D)).astype(np.float32)
    W = TG + 128 * (NMIX - 1)
    u = np.arange(W)[None, :]
    p = np.arange(128)[:, None]
    base = np.exp((u >= (NMIX - 1) * 128 + p).astype(np.float64)).astype(BF)
    return dict(rts=rts, mbase=np.ascontiguousarray(base),
                ident=np.eye(128, dtype=np.float32).astype(BF),
                ones_col=np.ones((128, 1), dtype=np.float32).astype(BF),
                ones_row=np.ones((1, 128), dtype=np.float32),
                ones_rowb=np.ones((1, 128), dtype=np.float32).astype(BF),
                ones_f=np.full((128, 128), 1.0 / 128, dtype=np.float32)
                .astype(BF),
                ones_b=np.ones((128, 128), dtype=np.float32).astype(BF),
                scale_col=np.full((128, 1), D ** -0.25, dtype=np.float32))


def host_beta(cfg, core_half):
    S, TG, NG, NS, T = cfg.S, cfg.TG, cfg.NG, cfg.NS, cfg.T
    r = core_half * T
    beta = np.zeros((128, NG * NS), dtype=np.float32)
    for g in range(NG):
        t_lo = r + g * TG
        for j in range(NS):
            srot0 = 128 * j
            if g * TG <= srot0 < (g + 1) * TG:
                continue
            for p in range(128):
                s_orig = (srot0 + p + r) % S
                beta[p, g * NS + j] = 1.0 if s_orig <= t_lo else 0.0
    return beta


class Sem:
    __slots__ = ("h", "n", "name", "log_seen")

    def __init__(self, es, nc, name):
        self.h = es.enter_context(nc.semaphore(name))
        self.n = 0
        self.name = name
        self.log_seen = 0

    def will(self, k=1):
        self.n += k
        assert self.n < 2 ** 31
        return self.n


class Prog:
    ENGS = ("sync", "tensor", "vector", "scalar", "gpsimd")

    def __init__(self):
        self.ops = {e: [] for e in self.ENGS}
        self.selfsem = {}

    def op(self, eng, meth, *args, waits=(), inc=None, **kw):
        ss = self.selfsem.get(eng)
        if ss is not None and ss.n > 0:
            waits = tuple(waits) + ((ss, ss.n),)
        tot = None
        if inc is not None:
            tot = inc[0].will(inc[1])
        self.ops[eng].append((meth, args, kw, tuple(waits), inc))
        return tot

    def wait(self, eng, sem, val):
        if val > 0:
            self.ops[eng].append(("__wait__", (sem, val), {}, (), None))

    def sem_inc(self, eng, sem, k=1):
        tot = sem.will(k)
        self.ops[eng].append(("__inc__", (sem, k), {}, (), None))
        return tot

    def replay(self, eng, e, nc=None, collect=None, fix=None):
        def xlate(s, v):
            if fix is not None:
                return fix.get(s.h.__class__ and getattr(s, 'name', ''), {}).get(v, v)
            return v

        for meth, args, kw, waits, inc in self.ops[eng]:
            if meth == "__wait__":
                s, v = args
                e.wait_ge(s.h, self._fx(fix, s, v))
                continue
            if meth == "__inc__":
                e.sem_inc(args[0].h, args[1])
                continue
            for s, v in waits:
                if v > 0:
                    e.wait_ge(s.h, self._fx(fix, s, v))
            if meth == "dma_start" and (collect is not None) and inc is not None:
                n0 = nc.n_instructions()
                ins = getattr(e, meth)(*args, **kw)
                ndma = nc.n_instructions() - n0
                ins.then_inc(inc[0].h, inc[1])
                s = inc[0]
                st = collect.setdefault(s.name, {"act": 0, "map": {0: 0}})
                st["act"] += 16 * max(1, ndma)
                st["map"][s.log_seen + inc[1]] = st["act"]
                s.log_seen += inc[1]
            else:
                ins = getattr(e, meth)(*args, **kw)
                if inc is not None:
                    ins.then_inc(inc[0].h, inc[1])
        return None

    @staticmethod
    def _fx(fix, s, v):
        if fix is None:
            return v
        m = fix.get(s.name)
        if m is None:
            return v
        return m["map"].get(v, v)


class TreeSum:

    def __init__(self, P, ve, pool):
        self.P, self.ve = P, ve
        self.pool = list(pool)
        self.levels = []

    def feed_pair(self, ap_a, rdy_a, ap_b, rdy_b):
        out = self.pool.pop()
        n = self.P.op("vector", "tensor_tensor", out, ap_a, ap_b, ALU.add,
                      waits=[rdy_a, rdy_b], inc=(self.ve, 1))
        carry = (out, (self.ve, n))
        lvl = 0
        while lvl < len(self.levels) and self.levels[lvl] is not None:
            a, ra = self.levels[lvl]
            n = self.P.op("vector", "tensor_tensor", a, a, carry[0], ALU.add,
                          waits=[ra, carry[1]], inc=(self.ve, 1))
            self.pool.append(carry[0])
            carry = (a, (self.ve, n))
            self.levels[lvl] = None
            lvl += 1
        if lvl == len(self.levels):
            self.levels.append(None)
        self.levels[lvl] = carry

    def root(self):
        occ = [x for x in self.levels if x is not None]
        assert occ
        ap, rdy = occ[0]
        for ap2, rdy2 in occ[1:]:
            n = self.P.op("vector", "tensor_tensor", ap, ap, ap2, ALU.add,
                          waits=[rdy, rdy2], inc=(self.ve, 1))
            rdy = (self.ve, n)
        return ap, rdy


def _build(cfg, collect=None, fix=None):
    nc = bass.Bass("TRN2", target_bir_lowering=False)
    S, T, C, I, H, KV, D, TG = (cfg.S, cfg.T, cfg.C, cfg.I, cfg.H, cfg.KV,
                                cfg.D, cfg.TG)
    NC, NS, NG, NMIX, NI, NT, NTG, NOB, G = (cfg.NC, cfg.NS, cfg.NG, cfg.NMIX,
                                             cfg.NI, cfg.NT, cfg.NTG, cfg.NOB,
                                             cfg.G)
    KVD = KV * D
    I2 = 2 * I
    MW = TG + 128 * (NMIX - 1)
    NW1 = I2 // 128
    NW2 = C // 128
    TH = TG // 512
    NSG = S // 1024

    dt = nc.dram_tensor
    xT_d = dt("xT", [C, S], F32, kind="ExternalInput").ap()
    xown_d = dt("x_own", [T, C], F32, kind="ExternalInput").ap()
    wqT_d = dt("wqT", [H, C, D], F32, kind="ExternalInput").ap()
    wkT_d = dt("wkT", [C, KVD], F32, kind="ExternalInput").ap()
    wvT_d = dt("wvT", [C, KVD], F32, kind="ExternalInput").ap()
    woT_d = dt("woT", [C, C], F32, kind="ExternalInput").ap()
    w1_d = dt("w1", [I2, C], F32, kind="ExternalInput").ap()
    w2_d = dt("w2", [C, I], F32, kind="ExternalInput").ap()
    bqT_d = dt("bqT", [128, H], F32, kind="ExternalInput").ap()
    bkT_d = dt("bkT", [128, KV], F32, kind="ExternalInput").ap()
    bv_d = dt("bv_r", [1, KVD], F32, kind="ExternalInput").ap()
    bo_d = dt("bo_r", [1, C], F32, kind="ExternalInput").ap()
    beta_d = dt("beta", [128, NG * NS], F32, kind="ExternalInput").ap()
    rts_d = dt("rts", [H * D, D], F32, kind="ExternalInput").ap()
    mbase_d = dt("mbase", [128, MW], BF16, kind="ExternalInput").ap()
    ident_d = dt("ident", [128, 128], BF16, kind="ExternalInput").ap()
    onesc_d = dt("ones_col", [128, 1], BF16, kind="ExternalInput").ap()
    onesr_d = dt("ones_row", [1, 128], F32, kind="ExternalInput").ap()
    onesrb_d = dt("ones_rowb", [1, 128], BF16, kind="ExternalInput").ap()
    onesf_d = dt("ones_f", [128, 128], BF16, kind="ExternalInput").ap()
    onesb_d = dt("ones_b", [128, 128], BF16, kind="ExternalInput").ap()
    scalec_d = dt("scale_col", [128, 1], F32, kind="ExternalInput").ap()

    out_d = dt("out", [T, C], F32, kind="ExternalOutput").ap()

    spk = "ExternalOutput" if cfg.debug else "Internal"
    w1q_d = dt("w1q_d", [I2, C], BF16, kind=spk).ap()
    w2q_d = dt("w2q_d", [C, I], BF16, kind="Internal").ap()
    xTbf_d = dt("xTbf_d", [C, T], BF16, kind="Internal").ap()
    woTbf_d = dt("woTbf_d", [C, C], BF16, kind="Internal").ap()
    wqbf_d = dt("wqbf_d", [H, C, D], BF16, kind="Internal").ap()
    w2qT_d = dt("w2qT_d", [NOB * I, 512], BF16, kind="Internal").ap()
    x2_d = dt("x2_d", [T, C], BF16, kind=spk).ap()
    xq_d = dt("xq_d", [T, C], BF16, kind=spk).ap()
    h_d = dt("h_d", [T, I], F32, kind=spk).ap()

    dbg = {}
    if cfg.debug:
        dbg["k"] = dt("dbg_k", [128, KV * S], F32, kind="ExternalOutput").ap()
        dbg["v"] = dt("dbg_v", [128, NS * KVD], F32, kind="ExternalOutput").ap()
        dbg["at"] = dt("dbg_at", [128, NG * H * TG], F32,
                       kind="ExternalOutput").ap()
        dbg["q"] = dt("dbg_q", [128, TG], F32, kind="ExternalOutput").ap()
        dbg["sg"] = dt("dbg_sg", [1, TG], F32, kind="ExternalOutput").ap()
        dbg["pt"] = dt("dbg_pt", [128, 4 * TG], F32, kind="ExternalOutput").ap()
        dbg["sm"] = dt("dbg_sm", [128, TG], F32, kind="ExternalOutput").ap()

    P = Prog()
    es = contextlib.ExitStack()

    def sb(stack, name, shape, dtp):
        return stack.enter_context(nc.sbuf_tensor(name, shape, dtp))

    with es:
        consts = sb(es, "consts", [128, H * D + 385], BF16)
        mbase = sb(es, "mbase_s", [128, MW], BF16)
        beta = sb(es, "beta_s", [128, NG * NS], F32)
        onesr = sb(es, "onesr", [1, 128], F32)
        onesrb = sb(es, "onesrb", [1, 128], BF16)
        scalec = sb(es, "scalec", [128, 1], F32)
        bq = sb(es, "bq_s", [128, H], F32)
        bk = sb(es, "bk_s", [128, KV], F32)
        bvb = sb(es, "bvb", [128, KVD], BF16)
        bob = sb(es, "bob", [128, C], BF16)
        km2 = sb(es, "km2", [1, KV + H + 8], F32)
        sx_all = sb(es, "sx_all", [128, NT], F32)
        s2g = sb(es, "s2g", [128, 4], F32)
        stat = sb(es, "stat", [128, 14], F32)
        sigr = sb(es, "sigr", [1, 8], F32)

        RT_ = lambda hh: consts[:, hh * D:(hh + 1) * D]
        IDENT = consts[:, H * D:H * D + 128]
        ONESC = consts[:, H * D + 128:H * D + 129]
        ONESF = consts[:, H * D + 129:H * D + 257]
        ONESB = consts[:, H * D + 257:H * D + 385]

        pe = Sem(es, nc, "pe")
        ve = Sem(es, nc, "ve")
        ac = Sem(es, nc, "ac")
        bar = Sem(es, nc, "bar")
        P.selfsem = {"vector": ve, "scalar": ac}
        _s = lambda nm: Sem(es, nc, nm)
        ldi = _s("ldi")
        lrt = _s("lrt")
        ldb = _s("ldb")
        lwk = _s("lwk")
        lxb = [_s("lxb0"), _s("lxb1")]
        lwq = [_s("lwq0"), _s("lwq1")]
        lwo = _s("lwo")
        lxr = [_s("lxr0"), _s("lxr1")]
        lw1 = _s("lw1")
        sp_w1 = _s("sp_w1")
        lw2 = _s("lw2")
        sp_w2 = _s("sp_w2")
        lw2t = _s("lw2t")
        sp_w2t = _s("sp_w2t")
        sp_w2u = _s("sp_w2u")
        lxq = _s("lxq")
        lw1t = [_s("lw1t0"), _s("lw1t1")]
        lw1s = [_s("lw1s0"), _s("lw1s1")]
        lw2tb = _s("lw2tb")
        lwr = _s("lwr")
        sp_h = _s("sp_h")
        lh = _s("lh")
        lwf_ = [_s("lwf"), _s("lwf1"), _s("lwf2"), _s("lwf3")]
        lx2 = [_s("lx20"), _s("lx21")]
        sp_x2 = _s("sp_x2")
        sp_xq = _s("sp_xq")
        so_ = [_s("so0"), _s("so1")]
        sod = _s("sod")
        pcxx = [_s(f"pcxx{i}") for i in range(cfg.NG)]
        pcxq = [_s(f"pcxq{i}") for i in range((cfg.H + 3) // 4)]
        pcxw = _s("pcxw")
        lxc = _s("lxc")

        def barrier():
            for eng in Prog.ENGS:
                P.sem_inc(eng, bar, 1)
            t = bar.n
            for eng in Prog.ENGS:
                P.wait(eng, bar, t)

        n_rts = P.op("gpsimd", "dma_start",
                     out=consts[:, :H * D].rearrange("p (h d) -> p h d", h=H),
                     in_=rts_d.rearrange("(h p) d -> p h d", p=D),
                     inc=(lrt, 16))
        P.op("sync", "dma_start", out=IDENT, in_=ident_d, inc=(ldi, 16))
        P.op("sync", "dma_start", out=ONESC, in_=onesc_d, inc=(ldi, 16))
        P.op("sync", "dma_start", out=mbase[:], in_=mbase_d, inc=(ldi, 16))
        P.op("sync", "dma_start", out=beta[:], in_=beta_d, inc=(ldi, 16))
        P.op("sync", "dma_start", out=onesr[:], in_=onesr_d, inc=(ldi, 16))
        P.op("sync", "dma_start", out=onesrb[:], in_=onesrb_d, inc=(ldi, 16))
        P.op("sync", "dma_start", out=ONESF, in_=onesf_d, inc=(ldi, 16))
        P.op("sync", "dma_start", out=ONESB, in_=onesb_d, inc=(ldi, 16))
        P.op("sync", "dma_start", out=scalec[:], in_=scalec_d, inc=(ldi, 16))
        P.op("sync", "dma_start", out=bq[:], in_=bqT_d, inc=(ldi, 16))
        n_ld_init = P.op("sync", "dma_start", out=bk[:], in_=bkT_d,
                         inc=(ldi, 16))

        esc = contextlib.ExitStack()
        arena = sb(esc, "arena", [128, 32768], BF16)
        A_XTB, A_WQB = 0, NC * TG
        A_QH = A_WQB + 2 * NC * 128
        A_QPRE = A_QH + 2 * TG
        A_PT = A_QPRE + TG
        A_PTS = A_PT + 3 * TG
        A_TREE = A_PTS + TG
        assert A_TREE + 5 * TG <= 32768, (A_TREE, TG)
        xTb = arena[:, A_XTB:A_XTB + NC * TG]
        sig = sb(esc, "sig", [128, NS], F32)
        y1 = sb(esc, "y1", [128, C], F32)
        recipb = y1[:, 0:TG]
        recro = recipb[0:1, 0:TG]
        xres = sb(esc, "xres", [128, C], F32)
        sgrow = xres[0:1, 0:TG]
        x2t = sb(esc, "x2t", [128, 1024], BF16)
        xqt = sb(esc, "xqt", [128, 1024], BF16)
        sgbb_bufs = [x2t[:, 0:TG], xqt[:, 0:TG]]
        i32q = sb(esc, "i32q", [128, 512], I32)
        w1f = sb(esc, "w1f", [128, C], F32)
        w1qb = sb(esc, "w1qb", [128, 1024], BF16)
        attn = sb(esc, "attn", [128, H * TG], BF16)
        kpre = attn[:, 0:1024]
        assert 2 * NC * 1024 + 2 * NC * KVD <= 32768 or True
        wkb = arena[:, NC * 1024:NC * 1024 + NC * KVD]
        wvb = arena[:, NC * 1024 + NC * KVD:NC * 1024 + 2 * NC * KVD]
        k_fm = sb(esc, "k_fm", [128, KV * S], BF16)
        v_tm = sb(esc, "v_tm", [128, NS * KVD], BF16)

        pA = esc.enter_context(nc.psum_tensor("pA", [128, 1024], F32))
        pB = esc.enter_context(nc.psum_tensor("pB", [128, 1024], F32))
        pC = esc.enter_context(nc.psum_tensor("pC", [128, 1024], F32))
        pV = esc.enter_context(nc.psum_tensor("pV", [128, 1024], F32))
        PSC = [pA, pB, pC]

        n_bv_ld = P.op("sync", "dma_start", out=y1[0:1, 0:KVD], in_=bv_d,
                       inc=(ldb, 16))

        def bcast_row(width, wait_ld, dst, psum):
            nv = 0
            prev = {}
            for o in range(0, width, 512):
                w = min(512, width - o)
                slot = (o // 512) % 2
                wts = [(ldb, wait_ld), (ldi, n_ld_init)] if o == 0 else []
                if slot in prev:
                    wts.append((ve, prev[slot]))
                nm = P.op("tensor", "matmul", psum[:, slot * 512:slot * 512 + w],
                          onesr[:], y1[0:1, o:o + w], start=True, stop=True,
                          waits=wts, inc=(pe, 1))
                nv = P.op("vector", "tensor_copy", dst[:, o:o + w],
                          psum[:, slot * 512:slot * 512 + w], waits=[(pe, nm)],
                          inc=(ve, 1))
                prev[slot] = nv
            return nv

        n_bv = bcast_row(KVD, n_bv_ld, bvb, pA)
        n_bo_ld = P.op("sync", "dma_start", out=y1[0:1, 0:C], in_=bo_d,
                       waits=[(ve, n_bv)], inc=(ldb, 16))
        n_bo = bcast_row(C, n_bo_ld, bob, pB)

        n_wk = P.op("gpsimd", "dma_start",
                    out=wkb.rearrange("p (c o) -> p c o", o=KVD),
                    in_=wkT_d.rearrange("(c p) o -> p c o", p=128),
                    inc=(lwk, 16))
        n_wv = P.op("gpsimd", "dma_start",
                    out=wvb.rearrange("p (c o) -> p c o", o=KVD),
                    in_=wvT_d.rearrange("(c p) o -> p c o", p=128),
                    inc=(lwk, 16))

        NPWQ = (H + 3) // 4
        pcx_x = [0] * NG
        pcx_wo = [0]
        pcx_wq = [0] * NPWQ

        def _pcx_g(gg):
            def f():
                pcx_x[gg] = P.op("gpsimd", "dma_start",
                                 out=xTbf_d[:, gg * TG:(gg + 1) * TG],
                                 in_=xT_d[:, gg * TG:(gg + 1) * TG],
                                 inc=(pcxx[gg], 16))
            return f

        def _pcx_wo(po_):
            def f():
                pcx_wo[0] = P.op("gpsimd", "dma_start",
                                 out=woTbf_d[po_ * (C // 2):
                                             (po_ + 1) * (C // 2), :],
                                 in_=woT_d[po_ * (C // 2):
                                           (po_ + 1) * (C // 2), :],
                                 inc=(pcxw, 16))
            return f

        def _pcx_wq(pp):
            h0, h1 = pp * 4, min(H, pp * 4 + 4)

            def f():
                pcx_wq[pp] = P.op(
                    "gpsimd", "dma_start",
                    out=wqbf_d[h0:h1].rearrange("h c d -> (h c) d"),
                    in_=wqT_d[h0:h1].rearrange("h c d -> (h c) d"),
                    inc=(pcxq[pp], 16))
            return f

        precasts = [_pcx_g(0), _pcx_wq(0)]
        precasts += [_pcx_g(gg) for gg in range(1, NG)]
        precasts += [_pcx_wq(pp) for pp in range(1, NPWQ)]
        precasts += [_pcx_wo(0), _pcx_wo(1)]

        xb_reg = [arena[:, A_XTB:A_XTB + NC * 1024]] * 2
        xtb_cnt = [0] * NSG
        b_pe_done = [0] * NSG
        evA = [n_bo, n_bo]
        acA = [0, 0]
        acB = 0
        veC = [n_bv, n_bv]
        for sg in range(NSG):
            xb = xb_reg[sg % 2]
            w = []
            if sg >= 1:
                w.append((pe, b_pe_done[sg - 1]))
            xtb_cnt[sg] = P.op("gpsimd", "dma_start",
                               out=xb.rearrange("p (c s) -> p c s", s=1024),
                               in_=xT_d[:, sg * 1024:(sg + 1) * 1024]
                               .rearrange("(c p) s -> p c s", p=128),
                               waits=w, inc=(lxb[0], 16))
            if precasts:
                precasts.pop(0)()
            for kv in range(KV):
                for th in range(2):
                    wts = [(lxb[0], xtb_cnt[sg]), (ldi, n_ld_init),
                           (lwk, n_wv)]
                    if sg == 0 and kv == 0 and th == 0:
                        wts.append((ve, n_bo))
                    wts.append((ac, acA[th]))
                    n_mm = 0
                    for c in range(NC):
                        n_mm = P.op(
                            "tensor", "matmul", pA[:, th * 512:th * 512 + 512],
                            wkb[:, c * KVD + kv * 128:c * KVD + kv * 128 + 128],
                            xb[:, c * 1024 + th * 512:c * 1024 + th * 512 + 512],
                            start=(c == 0), stop=(c == NC - 1),
                            waits=wts if c == 0 else (),
                            inc=(pe, 1) if c == NC - 1 else None)
                    acA[th] = P.op("scalar", "activation",
                                   kpre[:, th * 512:th * 512 + 512],
                                   pA[:, th * 512:th * 512 + 512], AF.Identity,
                                   bias=bk[:, kv:kv + 1], scale=1.0,
                                   waits=[(pe, n_mm)], inc=(ac, 1))
                n_r = 0
                for th in range(2):
                    n_r = P.op("tensor", "matmul",
                               pB[:, th * 512:th * 512 + 512], RT_(kv),
                               kpre[:, th * 512:th * 512 + 512],
                               start=True, stop=True,
                               waits=[(ac, acA[1]), (ac, acB),
                                      (ldi, n_ld_init), (lrt, 16),
                                      (ve, n_bo if (sg == 0 and kv == 0) else 0)]
                               if th == 0 else (), inc=(pe, 1))
                acB = P.op("scalar", "activation",
                           k_fm[:, kv * S + sg * 1024:kv * S + (sg + 1) * 1024],
                           pB[:, 0:1024], AF.Identity, bias=0.0,
                           scale=scalec[:], waits=[(pe, n_r)], inc=(ac, 1))
            for st in range(8):
                po = pC[:, (st % 2) * 512:(st % 2) * 512 + KVD]
                wts = [(ve, veC[st % 2]), (lxb[0], xtb_cnt[sg])]
                n_mm = 0
                for c in range(NC):
                    n_mm = P.op(
                        "tensor", "matmul", po,
                        xb[:, c * 1024 + st * 128:c * 1024 + st * 128 + 128],
                        wvb[:, c * KVD:(c + 1) * KVD],
                        start=(c == 0), stop=(c == NC - 1),
                        waits=wts if c == 0 else (),
                        inc=(pe, 1) if c == NC - 1 else None)
                veC[st % 2] = P.op(
                    "vector", "scalar_tensor_tensor",
                    v_tm[:, (sg * 8 + st) * KVD:(sg * 8 + st + 1) * KVD],
                    po, 1.0, bvb[:], ALU.mult, ALU.add,
                    waits=[(pe, n_mm)], inc=(ve, 1))
            b_pe_done[sg] = pe.n
        while precasts:
            precasts.pop(0)()

        nsl = S // 512
        n_kmax_pe = 0
        km_red = [0, 0]
        for kv in range(KV):
            nparts = min(8, nsl)
            for sl in range(nsl):
                wts = ([(pe, b_pe_done[NSG - 1])] if (kv == 0 and sl == 0)
                       else [(pe, n_kmax_pe)])
                nq = P.op("scalar", "activation", kpre[:, 0:512],
                          k_fm[:, kv * S + sl * 512:kv * S + (sl + 1) * 512],
                          AF.Square, bias=0.0, scale=1.0, waits=wts,
                          inc=(ac, 1))
                n_kmax_pe = P.op("tensor", "matmul",
                                 pA[0:1, (sl % 2) * 512:(sl % 2) * 512 + 512],
                                 ONESC, kpre[:, 0:512], start=True, stop=True,
                                 waits=[(ac, nq), (ac, acB),
                                        (ve, km_red[sl % 2])], inc=(pe, 1))
                km_red[sl % 2] = P.op("vector", "reduce_max",
                     km2[0:1, KV + H + (sl % 8):KV + H + (sl % 8) + 1],
                     pA[0:1, (sl % 2) * 512:(sl % 2) * 512 + 512], axis=AX.X,
                     waits=[(pe, n_kmax_pe)], inc=(ve, 1))
            P.op("vector", "reduce_max", km2[0:1, kv:kv + 1],
                 km2[0:1, KV + H:KV + H + nparts], axis=AX.X,
                 waits=[(ve, ve.n)], inc=(ve, 1))
        kmi = km2.ap() if False else km2
        for kv2 in range(KV):
            c0 = KV + H + kv2
            P.op("vector", "tensor_scalar",
                 km2[0:1, c0:c0 + 1].bitcast(I32),
                 km2[0:1, kv2:kv2 + 1].bitcast(I32), 1, None,
                 ALU.arith_shift_right, inc=(ve, 1))
            P.op("vector", "tensor_scalar",
                 km2[0:1, c0:c0 + 1].bitcast(I32),
                 km2[0:1, c0:c0 + 1].bitcast(I32), 0x1FBD1DF5, None,
                 ALU.add, inc=(ve, 1))
        for hh in range(H):
            P.op("vector", "tensor_scalar", km2[0:1, KV + hh:KV + hh + 1],
                 km2[0:1, KV + H + hh // G:KV + H + hh // G + 1], -1.1, None,
                 ALU.mult, inc=(ve, 1))
        n_km2 = ve.n

        run_C = cfg.phases in ("all", "BC", "BCD1")
        run_D1 = cfg.phases in ("all", "BCD1")
        run_D2 = cfg.phases == "all"
        w1s = {"rb": 0, "st": 0, "ve": 0, "ac": 0, "sp": 0, "ld": 0, "nrc": 0}
        i32g = {"ve": 0}

        def w1_step():
            rb = w1s["rb"]
            if rb >= NW1 or not run_C:
                return
            st_ = w1s["st"]
            if st_ == 0:
                w1s["ld"] = P.op("sync", "dma_start", out=w1f[:],
                                 in_=w1_d[rb * 128:(rb + 1) * 128, :],
                                 waits=[(ac, w1s["ac"]), (ve, w1s["ve"])]
                                 if rb else (), inc=(lw1, 16))
            elif st_ == 1:
                P.op("vector", "reduce_max", stat[:, 8:9], w1f[:], axis=AX.X,
                     apply_absolute_value=True,
                     waits=[(lw1, w1s["ld"]), (ac, w1s["ac"])], inc=(ve, 1))
                P.op("vector", "tensor_scalar", stat[:, 9:10], stat[:, 8:9],
                     1.0 / 127.0, 1e-8, ALU.mult, ALU.max, inc=(ve, 1))
                w1s["nrc"] = P.op("vector", "reciprocal", stat[:, 10:11],
                                  stat[:, 9:10], inc=(ve, 1))
            else:
                nh = max(1, (C // 512) // 2)
                h0 = 0 if st_ == 2 else nh
                for hf in range(h0, h0 + nh):
                    na = P.op("scalar", "activation", i32q[:],
                              w1f[:, hf * 512:(hf + 1) * 512], AF.Identity,
                              bias=0.0, scale=stat[:, 10:11],
                              waits=[(ve, w1s["nrc"]), (sp_w1, w1s["sp"]),
                                     (ve, i32g["ve"])], inc=(ac, 1))
                    w1s["ac"] = na
                    w1s["ve"] = P.op("vector", "tensor_scalar",
                                     w1qb[:, (hf % 2) * 512:(hf % 2) * 512 + 512],
                                     i32q[:], stat[:, 9:10], None, ALU.mult,
                                     waits=[(ac, na)], inc=(ve, 1))
                    i32g["ve"] = w1s["ve"]
                w1s["sp"] = P.op("gpsimd", "dma_start",
                                 out=w1q_d[rb * 128:(rb + 1) * 128,
                                           h0 * 512:(h0 + nh) * 512],
                                 in_=w1qb[:, (h0 % 2) * 512:
                                          (h0 % 2) * 512 + nh * 512],
                                 waits=[(ve, w1s["ve"])],
                                 inc=(sp_w1, 16))
            w1s["st"] += 1
            if w1s["st"] == 4:
                w1s["st"] = 0
                w1s["rb"] += 1


        exp_slot2 = [acB, acB]
        pt_guard0 = [[], [], []]
        at_cnt = n_bv
        ve_recip = n_km2
        ve_recb = n_bo
        ve_sgc = [n_km2]
        n_woload = 0
        last_arena_pe = n_kmax_pe
        spill_x2 = [0]
        spill_xq = 0
        n_qmm_pc = [0, 0]
        wq_cnt = [0, 0]
        nxtb = [0]
        ve_y1 = [at_cnt, at_cnt]
        mix_nrd = [0]
        dbg_at_off = 0
        PSC2 = [pA, pB]
        J_PRE = NS - 14 if NS >= 20 else max(2, NS - 6)

        def emit_wq_load(h, wq_w):
            wqb = arena[:, A_WQB + (h % 2) * NC * 128:
                        A_WQB + (h % 2 + 1) * NC * 128]
            wq_cnt[h % 2] = P.op("sync", "dma_start",
                                 out=wqb.rearrange("p (c d) -> p c d", d=128),
                                 in_=wqbf_d[h]
                                 .rearrange("(c p) d -> p c d", p=128),
                                 waits=list(wq_w) + [(pcxq[h // 4],
                                                      pcx_wq[h // 4])],
                                 inc=(lwq[h % 2], 16))

        def emit_q_chain(g, h, q_extra=(), boundary=False):
            wqb = arena[:, A_WQB + (h % 2) * NC * 128:
                        A_WQB + (h % 2 + 1) * NC * 128]
            qpre = arena[:, A_QPRE:A_QPRE + TG]
            qh = arena[:, A_QH + (h % 2) * TG:A_QH + (h % 2 + 1) * TG]
            dg = [(sod, sod.n)] if cfg.debug else []
            nq = 0
            for th in range(TH):
                wts = []
                if th == 0:
                    wts = [(lwq[h % 2], wq_cnt[h % 2]), (lxc, nxtb[0]),
                           (ve, ve_sgc[0])] + list(q_extra)
                for c in range(NC):
                    nq = P.op("tensor", "matmul",
                              pC[:, th * 512:th * 512 + 512],
                              wqb[:, c * 128:(c + 1) * 128],
                              xTb[:, c * TG + th * 512:c * TG + th * 512 + 512],
                              start=(c == 0), stop=(c == NC - 1),
                              waits=wts if c == 0 else (),
                              inc=(pe, 1) if c == NC - 1 else None)
            n_qmm_pc[h % 2] = nq
            nqpre = P.op("scalar", "activation", qpre, pC[:, 0:TG],
                         AF.Identity, bias=bq[:, h:h + 1], scale=1.0,
                         waits=[(pe, nq)] + dg, inc=(ac, 1))
            yield (nqpre, 0, 0)
            n_r = 0
            for th in range(TH):
                n_r = P.op("tensor", "matmul",
                           pC[:, th * 512:th * 512 + 512], RT_(h),
                           qpre[:, th * 512:th * 512 + 512],
                           start=True, stop=True,
                           waits=[(ac, nqpre)] if th == 0 else (),
                           inc=(pe, 1))
            nqh = P.op("scalar", "activation", qh, pC[:, 0:TG], AF.Identity,
                       bias=0.0, scale=scalec[:], waits=[(pe, n_r)] + dg,
                       inc=(ac, 1))
            nsq = P.op("scalar", "activation", qpre, qh, AF.Square,
                       bias=0.0, scale=1.0, inc=(ac, 1))
            yield (nqpre, nqh, 0)
            nsm = 0
            for th in range(TH):
                nsm = P.op("tensor", "matmul",
                           pC[:, th * 512:th * 512 + 512], ONESB,
                           qpre[:, th * 512:th * 512 + 512],
                           start=True, stop=True,
                           waits=[(ac, nsq), (ve, ve_sgc[0])]
                           if th == 0 else (), inc=(pe, 1))
            sg_w = [(pe, nsm)]
            if boundary:
                sg_w += [(lxr[0], lxr[0].n)]
            nsgc = P.op("vector", "tensor_copy", sgrow, pC[0:1, 0:TG],
                        waits=sg_w + dg, inc=(ve, 1))
            ve_sgc[0] = nsgc
            P.op("vector", "tensor_scalar", sgrow.bitcast(I32),
                 sgrow.bitcast(I32), 1, None, ALU.arith_shift_right,
                 inc=(ve, 1))
            P.op("vector", "tensor_scalar", sgrow.bitcast(I32),
                 sgrow.bitcast(I32), 0x1FBD1DF5, None, ALU.add, inc=(ve, 1))
            P.op("vector", "tensor_scalar", sgrow, sgrow,
                 km2[0:1, KV + h:KV + h + 1], None, ALU.mult, inc=(ve, 1))
            sgb = attn[0:1, h * TG:(h + 1) * TG]
            nrow = P.op("vector", "tensor_copy", sgb, sgrow, inc=(ve, 1))
            nbc = 0
            for th in range(TH):
                nbc = P.op("tensor", "matmul",
                           pC[:, th * 512:th * 512 + 512], onesrb[:],
                           sgb[0:1, th * 512:th * 512 + 512],
                           start=True, stop=True,
                           waits=[(ve, nrow), (ve, nsgc)] if th == 0 else (),
                           inc=(pe, 1))
            sgbb = sgbb_bufs[h % 2]
            ev_w = [(pe, nbc), (sp_x2, spill_x2[0]), (sp_xq, spill_xq)]
            nsig = P.op("vector", "tensor_copy", sgbb, pC[:, 0:TG],
                        waits=ev_w, inc=(ve, 1))
            ve_sgc[0] = nsig
            yield (nqpre, nqh, nsig)

        for g in range(NG if run_C else 0):
            nxtb[0] = P.op("sync", "dma_start",
                           out=xTb.rearrange("p (c s) -> p c s", s=TG),
                           in_=xTbf_d[:, g * TG:(g + 1) * TG]
                           .rearrange("(c p) s -> p c s", p=128),
                           waits=[(pe, last_arena_pe), (pcxx[g], pcx_x[g])],
                           inc=(lxc, 16))
            wq_w0 = ([(pe, b_pe_done[NSG - 1])] if g == 0
                     else [(pe, last_arena_pe)])
            emit_wq_load(0, wq_w0)
            if H > 1:
                emit_wq_load(1, wq_w0)
            bx = ([(ve, n_km2)] if g == 0 else
                  [(ve, ve_y1[0]), (ve, ve_y1[1])])
            bchain = emit_q_chain(g, 0, q_extra=bx, boundary=(g > 0))
            next(bchain)
            next(bchain)
            nqpre, nqh, nsigdone = next(bchain)
            sgbb_cur = sgbb_bufs[0]
            if cfg.debug and g == 0:
                P.op("gpsimd", "dma_start", out=dbg["q"],
                     in_=arena[:, A_QH:A_QH + TG],
                     waits=[(ac, nqh)], inc=(sod, 16))
                P.op("gpsimd", "dma_start", out=dbg["sg"], in_=sgrow,
                     waits=[(ve, nsigdone)], inc=(sod, 16))
            for h in range(H):
                kv = h // G
                qh = arena[:, A_QH + (h % 2) * TG:A_QH + (h % 2 + 1) * TG]
                chain = None
                chain_res = None
                pt_sod = {0: 0, 1: 0, 2: 0}
                tree = TreeSum(P, ve, [
                    arena[:, A_TREE + k * TG:A_TREE + (k + 1) * TG]
                    for k in range(5)])
                rd = [None] * NS
                npv = [0] * NS
                treed = [0] * NS
                ptr = [None] * NS
                for j in range(NS):
                    slot2 = j % 2
                    slot3 = j % 3
                    psc = PSC2[slot2]
                    n_sc = 0
                    for th in range(TH):
                        wts = []
                        if th == 0:
                            wts = [(ac, exp_slot2[slot2])]
                            if j == 0:
                                wts += [(ve, nsigdone), (ve, ve_recip)]
                                if h == 0 and g > 0:
                                    wts += [(ve, ve_y1[0])]
                            if j == 1:
                                wts += [(ve, ve_recb)]
                                if h == 0 and g > 0:
                                    wts += [(ve, ve_y1[1])]
                        P.op("tensor", "matmul",
                             psc[:, th * 512:th * 512 + 512],
                             k_fm[:, kv * S + j * 128:kv * S + j * 128 + 128],
                             qh[:, th * 512:th * 512 + 512],
                             start=True, stop=False,
                             waits=wts, inc=None)
                        n_sc = P.op("tensor", "matmul",
                                    psc[:, th * 512:th * 512 + 512],
                                    ONESF,
                                    sgbb_cur[:, th * 512:th * 512 + 512],
                                    start=False, stop=True, inc=(pe, 1))
                    ptreg = arena[:, A_PT + slot3 * TG:A_PT + (slot3 + 1) * TG]
                    ptr[j] = ptreg
                    pg = ([(pe, npv[j - 3]), (ve, treed[j - 3])]
                          if j >= 3 else pt_guard0[slot3])
                    dg = [(sod, pt_sod[slot3])] if (cfg.debug and
                                                    pt_sod[slot3]) else []
                    mixed = (g * NMIX <= j < (g + 1) * NMIX)
                    if mixed:
                        pts = arena[:, A_PTS:A_PTS + TG]
                        nex = P.op("scalar", "activation", pts, psc[:, 0:TG],
                                   AF.Exp, bias=beta[:, g * NS + j:g * NS + j + 1],
                                   scale=1.0,
                                   waits=[(pe, n_sc), (ve, mix_nrd[0])],
                                   inc=(ac, 1))
                        off = (NMIX - 1 - (j - g * NMIX)) * 128
                        nrd = P.op("vector", "tensor_tensor", ptreg, pts,
                                   mbase[:, off:off + TG], ALU.mult,
                                   waits=[(ac, nex)] + pg + dg,
                                   inc=(ve, 1))
                        mix_nrd[0] = nrd
                        ready = (ve, nrd)
                    else:
                        nex = P.op("scalar", "activation", ptreg, psc[:, 0:TG],
                                   AF.Exp, bias=beta[:, g * NS + j:g * NS + j + 1],
                                   scale=1.0,
                                   waits=[(pe, n_sc)] + pg + dg, inc=(ac, 1))
                        ready = (ac, nex)
                    exp_slot2[slot2] = ac.n
                    rd[j] = ready
                    if cfg.debug and g == 0 and h == 0 and j < 4:
                        nd = P.op("gpsimd", "dma_start",
                                  out=dbg["pt"][:, j * TG:(j + 1) * TG],
                                  in_=ptreg, waits=[ready], inc=(sod, 16))
                        pt_sod[slot3] = nd
                    if j >= 1:
                        jp = j - 1
                        for th in range(TH):
                            wts = []
                            if th == 0:
                                wts = [rd[jp]]
                                if jp == 0:
                                    wts.append((ve, at_cnt))
                            npv[jp] = P.op(
                                "tensor", "matmul",
                                pV[:, th * 512:th * 512 + 512],
                                v_tm[:, jp * KVD + kv * 128:
                                     jp * KVD + kv * 128 + 128],
                                ptr[jp][:, th * 512:th * 512 + 512],
                                start=(jp == 0), stop=False,
                                waits=wts, inc=(pe, 1))
                    if j % 2 == 1:
                        tree.feed_pair(ptr[j - 1], rd[j - 1], ptreg, ready)
                        treed[j - 1] = treed[j] = ve.n
                        w1_step()
                    if h + 1 < H:
                        if j == J_PRE:
                            chain = emit_q_chain(g, h + 1)
                            chain_res = next(chain)
                        elif j == J_PRE + 1:
                            chain_res = next(chain)
                        elif j == J_PRE + 2:
                            chain_res = next(chain)
                    if j == 6 and h + 2 < H:
                        emit_wq_load(h + 2, [(pe, n_qmm_pc[h % 2])])
                jp = NS - 1
                for th in range(TH):
                    npv[jp] = P.op(
                        "tensor", "matmul", pV[:, th * 512:th * 512 + 512],
                        v_tm[:, jp * KVD + kv * 128:jp * KVD + kv * 128 + 128],
                        ptr[jp][:, th * 512:th * 512 + 512],
                        start=False, stop=True,
                        waits=[rd[jp]] if th == 0 else (), inc=(pe, 1))
                npv_last = npv[jp]
                for s3 in range(3):
                    jl = max(jj for jj in range(NS) if jj % 3 == s3)
                    pt_guard0[s3] = [(pe, npv[jl]), (ve, treed[jl])]
                root_ap, root_rdy = tree.root()
                if cfg.debug and g == 0 and h == 0:
                    P.op("gpsimd", "dma_start", out=dbg["sm"], in_=root_ap,
                         waits=[root_rdy], inc=(sod, 16))
                n_sums = 0
                for th in range(TH):
                    n_sums = P.op("tensor", "matmul",
                                  pA[:, th * 512:th * 512 + 512], ONESB,
                                  root_ap[:, th * 512:th * 512 + 512],
                                  start=True, stop=True,
                                  waits=[root_rdy, (ac, exp_slot2[0])]
                                  if th == 0 else (), inc=(pe, 1))
                if C >= 2 * TG:
                    sums_s = xres[:, 0:TG]
                    praw = xres[:, TG:2 * TG]
                    ve_recip = P.op("vector", "tensor_copy", sums_s,
                                    pA[:, 0:TG], waits=[(pe, n_sums)],
                                    inc=(ve, 1))
                    ve_recb = ve_recip
                    at_cnt = P.op("vector", "tensor_copy", praw, pV[:, 0:TG],
                                  waits=[(pe, npv_last)], inc=(ve, 1))
                    P.op("vector", "reciprocal", recipb[:], sums_s,
                         inc=(ve, 1))
                    at_fin = P.op("vector", "tensor_tensor",
                                  attn[:, h * TG:(h + 1) * TG], praw,
                                  recipb[:], ALU.mult, inc=(ve, 1))
                else:
                    ve_recip = P.op("vector", "reciprocal", recipb[:],
                                    pA[:, 0:TG], waits=[(pe, n_sums)],
                                    inc=(ve, 1))
                    ve_recb = ve_recip
                    at_cnt = P.op("vector", "tensor_tensor",
                                  attn[:, h * TG:(h + 1) * TG], pV[:, 0:TG],
                                  recipb[:], ALU.mult,
                                  waits=[(pe, npv_last), (ve, ve_recip)],
                                  inc=(ve, 1))
                    at_fin = at_cnt
                if chain_res is not None:
                    nqpre, nqh, nsigdone = chain_res
                    sgbb_cur = sgbb_bufs[(h + 1) % 2]
            last_arena_pe = pe.n
            at_cnt = at_fin
            if cfg.debug:
                P.op("gpsimd", "dma_start",
                     out=dbg["at"][:, dbg_at_off:dbg_at_off + H * TG],
                     in_=attn[:], waits=[(ve, at_cnt)], inc=(sod, 16))
                dbg_at_off += H * TG
            n_woload = P.op("sync", "dma_start",
                            out=arena[:, 0:NC * C]
                            .rearrange("p (c o) -> p c o", o=C),
                            in_=woTbf_d.rearrange("(c p) o -> p c o", p=128),
                            waits=[(pe, last_arena_pe), (ve, at_cnt),
                                   (ac, ac.n), (pcxw, pcx_wo[0])],
                            inc=(lwo, 16))
            PSY = [(pA, pB), (pC, pV)]
            ve_y1 = [at_cnt, at_cnt]
            last_nx2 = [0]
            for t in range(NTG):
                tt = g * NTG + t
                nx = P.op("sync", "dma_start", out=xres[:],
                          in_=xown_d[tt * 128:(tt + 1) * 128, :],
                          waits=[(ve, ve_y1[0]), (ve, ve_y1[1])] +
                          ([(ve, at_cnt)] if t == 0 else []),
                          inc=(lxr[0], 16))
                nxbo = P.op("vector", "tensor_tensor", xres[:], xres[:],
                            bob[:], ALU.add, waits=[(lxr[0], nx)],
                            inc=(ve, 1))
                pY0, pY1 = PSY[t % 2]
                n_wo = 0
                for o in range(C // 512):
                    pY = pY0 if o < 2 else pY1
                    po = pY[:, (o % 2) * 512:(o % 2) * 512 + 512]
                    for c2 in range(H):
                        wts = []
                        if c2 == 0:
                            wts = [(lwo, n_woload), (ve, ve_y1[t % 2])]
                        n_wo = P.op(
                            "tensor", "matmul", po,
                            attn[:, c2 * TG + t * 128:c2 * TG + t * 128 + 128],
                            arena[:, c2 * C + o * 512:c2 * C + o * 512 + 512],
                            start=(c2 == 0), stop=(c2 == H - 1),
                            waits=wts if c2 == 0 else (),
                            inc=(pe, 1) if c2 == H - 1 else None)
                nv = 0
                for half in range(max(1, C // 1024)):
                    pY = pY0 if half == 0 else pY1
                    nv = P.op("vector", "tensor_tensor",
                              y1[:, half * 1024:(half + 1) * 1024],
                              pY[:, 0:1024],
                              xres[:, half * 1024:half * 1024 + 1024],
                              ALU.add,
                              waits=[(pe, n_wo), (ve, nxbo),
                                     (ac, last_nx2[0])],
                              inc=(ve, 1))
                ve_y1[t % 2] = nv
                nsq2 = 0
                for hf in range(C // 1024):
                    nsq2 = P.op("scalar", "activation", xqt[:],
                                y1[:, hf * 1024:(hf + 1) * 1024], AF.Square,
                                bias=0.0, scale=1.0,
                                accum_out=stat[:, hf * 13:hf * 13 + 1],
                                waits=[(ve, nv), (sp_xq, spill_xq)],
                                inc=(ac, 1))
                if C > 1024:
                    P.op("vector", "tensor_tensor", stat[:, 0:1],
                         stat[:, 0:1], stat[:, 13:14], ALU.add,
                         waits=[(ac, nsq2)], inc=(ve, 1))
                P.op("vector", "reduce_sum", stat[:, 1:2], y1[:], axis=AX.X,
                     waits=[(ve, nv)], inc=(ve, 1))
                P.op("vector", "tensor_scalar", stat[:, 2:3], stat[:, 1:2],
                     1.0 / C, None, ALU.mult, inc=(ve, 1))
                P.op("vector", "tensor_tensor", stat[:, 3:4], stat[:, 2:3],
                     stat[:, 2:3], ALU.mult, inc=(ve, 1))
                P.op("vector", "tensor_scalar", stat[:, 4:5], stat[:, 0:1],
                     1.0 / C, None, ALU.mult, waits=[(ac, nsq2)],
                     inc=(ve, 1))
                P.op("vector", "tensor_tensor", stat[:, 5:6],
                     stat[:, 4:5], stat[:, 3:4], ALU.subtract, inc=(ve, 1))
                nvar = P.op("vector", "tensor_scalar", stat[:, 5:6],
                            stat[:, 5:6], 1e-5, None, ALU.add, inc=(ve, 1))
                nsd = P.op("scalar", "activation", stat[:, 6:7], stat[:, 5:6],
                           AF.Sqrt, bias=0.0, scale=1.0,
                           waits=[(ve, nvar)], inc=(ac, 1))
                nrstd = P.op("vector", "reciprocal", stat[:, 7:8],
                             stat[:, 6:7], waits=[(ac, nsd)], inc=(ve, 1))
                nnb = P.op("vector", "scalar_tensor_tensor", stat[:, 12:13],
                           stat[:, 2:3], -1.0, stat[:, 7:8], ALU.mult,
                           ALU.mult, inc=(ve, 1))
                for hf in range(C // 1024):
                    nx2 = P.op("scalar", "activation", x2t[:],
                               y1[:, hf * 1024:(hf + 1) * 1024], AF.Identity,
                               bias=stat[:, 12:13], scale=stat[:, 7:8],
                               waits=[(ve, nnb), (sp_x2, spill_x2[0])],
                               inc=(ac, 1))
                    last_nx2[0] = nx2
                    spill_x2[0] = P.op("gpsimd", "dma_start",
                                       out=x2_d[tt * 128:(tt + 1) * 128,
                                                hf * 1024:(hf + 1) * 1024],
                                       in_=x2t[:], waits=[(ac, nx2)],
                                       inc=(sp_x2, 16))
                P.op("vector", "reduce_max", stat[:, 8:9], y1[:], axis=AX.X,
                     waits=[(ve, nv)], inc=(ve, 1))
                P.op("vector", "reduce_max", stat[:, 3:4], y1[:], axis=AX.X,
                     negate=True, inc=(ve, 1))
                P.op("vector", "tensor_scalar", stat[:, 8:9], stat[:, 8:9],
                     stat[:, 7:8], None, ALU.mult, inc=(ve, 1))
                P.op("vector", "tensor_tensor", stat[:, 8:9], stat[:, 8:9],
                     stat[:, 12:13], ALU.add, inc=(ve, 1))
                P.op("vector", "tensor_scalar", stat[:, 3:4], stat[:, 3:4],
                     stat[:, 7:8], None, ALU.mult, inc=(ve, 1))
                P.op("vector", "tensor_tensor", stat[:, 3:4], stat[:, 3:4],
                     stat[:, 12:13], ALU.subtract, inc=(ve, 1))
                P.op("vector", "tensor_tensor", stat[:, 8:9], stat[:, 8:9],
                     stat[:, 3:4], ALU.max, inc=(ve, 1))
                P.op("vector", "tensor_scalar", stat[:, 9:10], stat[:, 8:9],
                     1.0 / 127.0, 1e-8, ALU.mult, ALU.max, inc=(ve, 1))
                P.op("vector", "tensor_copy", sx_all[:, tt:tt + 1],
                     stat[:, 9:10], inc=(ve, 1))
                nrq = P.op("vector", "reciprocal", stat[:, 10:11],
                           stat[:, 9:10], inc=(ve, 1))
                P.op("vector", "tensor_scalar", stat[:, 4:5], stat[:, 7:8],
                     stat[:, 10:11], None, ALU.mult, inc=(ve, 1))
                nrq = P.op("vector", "tensor_scalar", stat[:, 5:6],
                           stat[:, 12:13], stat[:, 10:11], None, ALU.mult,
                           inc=(ve, 1))
                nxq = 0
                for hf in range(C // 512):
                    na = P.op("scalar", "activation", i32q[:],
                              y1[:, hf * 512:(hf + 1) * 512], AF.Identity,
                              bias=stat[:, 5:6], scale=stat[:, 4:5],
                              waits=[(ve, nrq), (ve, i32g["ve"]),
                                     (sp_xq, spill_xq)], inc=(ac, 1))
                    nxq = P.op("vector", "tensor_copy",
                               xqt[:, (hf % 2) * 512:(hf % 2) * 512 + 512],
                               i32q[:], waits=[(ac, na)], inc=(ve, 1))
                    i32g["ve"] = nxq
                    if hf % 2 == 1:
                        spill_xq = P.op("gpsimd", "dma_start",
                                        out=xq_d[tt * 128:(tt + 1) * 128,
                                                 (hf - 1) * 512:(hf + 1) * 512],
                                        in_=xqt[:], waits=[(ve, nxq)],
                                        inc=(sp_xq, 16))
                last_nx2[0] = max(last_nx2[0], ac.n)
            last_arena_pe = pe.n
        while run_C and w1s["rb"] < NW1:
            w1_step()
        if cfg.debug:
            P.op("gpsimd", "dma_start", out=dbg["k"], in_=k_fm[:],
                 waits=[(ac, acB)], inc=(sod, 16))
            P.op("gpsimd", "dma_start", out=dbg["v"], in_=v_tm[:],
                 waits=[(ve, ve.n)], inc=(sod, 16))

        P.wait("gpsimd", sp_w1, sp_w1.n)
        P.wait("gpsimd", sp_x2, sp_x2.n)
        P.wait("gpsimd", sp_xq, sp_xq.n)
        esc.close()
        barrier()

        if not run_D1:
            NTx, NW2x, NOPx = 0, 0, 0
        else:
            NTx, NW2x, NOPx = NT, NW2, I // 512
        es1 = contextlib.ExitStack()
        xq_fm = sb(es1, "xq_fm", [128, NC * T], BF16)
        w1t = sb(es1, "w1t", [128, 4 * NC * 512], BF16)
        w2f = sb(es1, "w2f", [128, I], F32)
        w2qb = sb(es1, "w2qb", [128, I], BF16)
        w2ts = sb(es1, "w2ts", [128, C], BF16)
        w2ts2 = sb(es1, "w2ts2", [128, C], BF16)
        xqrow = sb(es1, "xqrow", [128, C], BF16)
        i32d = sb(es1, "i32d", [128, I // 2], I32)
        htile = sb(es1, "htile", [128, 512], F32)
        utile = sb(es1, "utile", [128, 512], BF16)
        pG = es1.enter_context(nc.psum_tensor("pG", [128, 1024], F32))
        pH = es1.enter_context(nc.psum_tensor("pH", [128, 1024], F32))
        pT = es1.enter_context(nc.psum_tensor("pT", [128, 1024], BF16))
        pX = es1.enter_context(nc.psum_tensor("pX", [128, 512], F32))

        xq_state = {"tt": 0, "evac": 0}
        xqfm_t = [0] * max(NT, 1)

        def xq_tile():
            tt = xq_state["tt"]
            if tt >= NTx:
                return
            xq_state["tt"] += 1
            nld = P.op("gpsimd", "dma_start", out=xqrow[:],
                       in_=xq_d[tt * 128:(tt + 1) * 128, :],
                       waits=[(ve, xq_state["evac"])] if tt else (),
                       inc=(lxq, 16))
            for cq in range(NC // 8):
                n_tr = 0
                for k in range(8):
                    c = cq * 8 + k
                    n_tr = P.op("tensor", "transpose",
                                pT[:, k * 128:(k + 1) * 128],
                                xqrow[:, c * 128:(c + 1) * 128], IDENT,
                                waits=[(lxq, nld), (ve, xq_state["evac"])]
                                if k == 0 else (), inc=(pe, 1))
                dst = xq_fm[:, cq * 8 * T + tt * 128:]
                dst = bass.AP(dst.tensor, dst.offset,
                              [[dst.ap[0][0], 128], [T, 8], [1, 128]])
                xq_state["evac"] = P.op("vector", "tensor_copy", dst,
                                        pT[:, 0:1024]
                                        .rearrange("p (k c) -> p k c", k=8),
                                        waits=[(pe, n_tr)], inc=(ve, 1))
            xqfm_t[tt] = xq_state["evac"]

        w2s = {"rb": 0, "st": 0, "sp": 0, "ve": 0, "ac": 0, "ld": 0,
               "nrc": 0, "i": 0, "tsp2": [0, 0]}

        def w2_step():
            rb = w2s["rb"]
            if rb >= NW2x:
                return
            st_ = w2s["st"]
            if st_ == 0:
                w2s["ld"] = P.op("scalar", "dma_start", out=w2f[:],
                                 in_=w2_d[rb * 128:(rb + 1) * 128, :],
                                 waits=[(ac, w2s["ac"]), (ve, w2s["ve"])]
                                 if rb else (), inc=(lw2, 16))
            elif st_ == 1:
                P.op("vector", "reduce_max", stat[:, 8:9], w2f[:], axis=AX.X,
                     apply_absolute_value=True,
                     waits=[(lw2, w2s["ld"]), (ac, w2s["ac"])], inc=(ve, 1))
                P.op("vector", "tensor_scalar", stat[:, 9:10], stat[:, 8:9],
                     1.0 / 127.0, 1e-8, ALU.mult, ALU.max, inc=(ve, 1))
                w2s["nrc"] = P.op("vector", "reciprocal", stat[:, 10:11],
                                  stat[:, 9:10], inc=(ve, 1))
            else:
                hf = st_ - 2
                na = P.op("scalar", "activation", i32d[:],
                          w2f[:, hf * (I // 2):(hf + 1) * (I // 2)],
                          AF.Identity, bias=0.0, scale=stat[:, 10:11],
                          waits=[(ve, w2s["nrc"]), (sp_w2, w2s["sp"])],
                          inc=(ac, 1))
                w2s["ac"] = na
                w2s["ve"] = P.op("vector", "tensor_scalar",
                                 w2qb[:, hf * (I // 2):(hf + 1) * (I // 2)],
                                 i32d[:], stat[:, 9:10], None, ALU.mult,
                                 waits=[(ac, na), (sp_w2, w2s["sp"])],
                                 inc=(ve, 1))
                w2s["sp"] = P.op(
                    "gpsimd", "dma_start",
                    out=w2q_d[rb * 128:(rb + 1) * 128,
                              hf * (I // 2):(hf + 1) * (I // 2)],
                    in_=w2qb[:, hf * (I // 2):(hf + 1) * (I // 2)],
                    waits=[(ve, w2s["ve"])], inc=(sp_w2, 16))
            w2s["st"] += 1
            if w2s["st"] == 4:
                w2s["st"] = 0
                w2s["rb"] += 1

        def w2_respill(k):
            while k > 0 and w2s["rb"] >= NW2x and w2s["i"] < NI:
                k -= 1
                i = w2s["i"]
                w2s["i"] += 1
                buf = w2ts2 if i % 2 else w2ts
                sem = lw2tb if i % 2 else lw2t
                ntr = P.op("sync", "dma_start",
                           out=buf[:],
                           in_=w2q_d[:, i * 128:(i + 1) * 128],
                           transpose=True,
                           waits=[(sp_w2, w2s["sp"]),
                                  (sp_w2u if i % 2 else sp_w2t,
                                   w2s["tsp2"][i % 2])],
                           inc=(sem, 16))
                for ob in range(NOB):
                    w2s["tsp2"][i % 2] = P.op(
                        "gpsimd", "dma_start",
                        out=w2qT_d[ob * I + i * 128:ob * I + (i + 1) * 128, :],
                        in_=buf[:, ob * 512:(ob + 1) * 512],
                        waits=[(sem, ntr)],
                        inc=(sp_w2u if i % 2 else sp_w2t, 16))

        NOP = I // 512
        w1rdy = {0: [], 1: []}

        def emit_w1t_load(op, wts=()):
            buf = op % 2
            base = buf * 2 * NC * 512
            first = True
            nn = 0
            for gu in range(2):
                rows = op * 512 + gu * I
                for c in range(NC):
                    sem = lw1t[buf] if c % 2 == 0 else lw1s[buf]
                    nn = P.op("sync", "dma_start",
                              out=w1t[:, base + gu * NC * 512 + c * 512:
                                      base + gu * NC * 512 + (c + 1) * 512],
                              in_=w1q_d[rows:rows + 512,
                                        c * 128:(c + 1) * 128],
                              transpose=True,
                              waits=wts if first else (),
                              inc=(sem, 16))
                    first = False
            w1rdy[buf] = [(lw1t[buf], lw1t[buf].n), (lw1s[buf], lw1s[buf].n)]

        w1pe = {"evac": 0}

        def w1pe_rowchunk(op, idx, wts_first=()):
            buf = op % 2
            base = buf * 2 * NC * 512
            gu, rc = divmod(idx, 4)
            rows = op * 512 + gu * I + rc * 128
            wl = list(wts_first) if wts_first else [(ve, w1pe["evac"])]
            nld = P.op("sync", "dma_start", out=xqrow[:],
                       in_=w1q_d[rows:rows + 128, :],
                       waits=wl, inc=(lwr, 16))
            for hc in range(NC // 8):
                n_tr = 0
                for k in range(8):
                    c = hc * 8 + k
                    n_tr = P.op("tensor", "transpose",
                                pT[:, k * 128:(k + 1) * 128],
                                xqrow[:, c * 128:(c + 1) * 128], IDENT,
                                waits=[(lwr, nld), (ve, w1pe["evac"])]
                                if k == 0 else (), inc=(pe, 1))
                dst = w1t[:, base + gu * NC * 512 + hc * 8 * 512 + rc * 128:]
                dst = bass.AP(dst.tensor, dst.offset,
                              [[dst.ap[0][0], 128], [512, 8], [1, 128]])
                w1pe["evac"] = P.op("vector", "tensor_copy", dst,
                                    pT[:, 0:1024]
                                    .rearrange("p (k c) -> p k c", k=8),
                                    waits=[(pe, n_tr)], inc=(ve, 1))
            if idx == 2 * 4 - 1:
                w1rdy[buf] = [(ve, w1pe["evac"])]

        if run_D1:
            emit_w1t_load(0)
            if NOPx > 1:
                emit_w1t_load(1)
            xq_tile()
            xq_tile()
        ac_sil = [0, 0]
        ve_hml = [0, 0]
        sph_n = 0
        for op in range(NOPx):
            buf = op % 2
            base = buf * 2 * NC * 512
            wrdy = list(w1rdy[buf])
            pe_mark = pe.n
            w1pe_idx = 0
            for t in range(NT):
                pGU = pG if t % 2 == 0 else pH
                n_g = 0
                for gu in range(2):
                    for c in range(NC):
                        wts2 = []
                        if gu == 0 and c == 0:
                            wts2 = wrdy + [
                                (ve, xqfm_t[t] if op == 0
                                 else xqfm_t[NT - 1]),
                                (ac, ac_sil[t % 2]), (ve, ve_hml[t % 2])]
                        n_g = P.op(
                            "tensor", "matmul",
                            pGU[:, gu * 512:gu * 512 + 512],
                            xq_fm[:, c * T + t * 128:c * T + t * 128 + 128],
                            w1t[:, base + gu * NC * 512 + c * 512:
                                base + gu * NC * 512 + (c + 1) * 512],
                            start=(c == 0), stop=(c == NC - 1),
                            waits=wts2 if (gu == 0 and c == 0) or c == 0 else (),
                            inc=(pe, 1) if c == NC - 1 else None)
                ac_sil[t % 2] = P.op("scalar", "activation", htile[:],
                                     pGU[:, 0:512], AF.Silu, bias=0.0,
                                     scale=sx_all[:, t:t + 1],
                                     waits=[(pe, n_g), (sp_h, sph_n)],
                                     inc=(ac, 1))
                nu = P.op("vector", "tensor_scalar", utile[:],
                          pGU[:, 512:1024], sx_all[:, t:t + 1], None,
                          ALU.mult, waits=[(pe, n_g)], inc=(ve, 1))
                ve_hml[t % 2] = P.op("vector", "tensor_tensor", htile[:],
                                     htile[:], utile[:], ALU.mult,
                                     waits=[(ac, ac_sil[t % 2]), (ve, nu)],
                                     inc=(ve, 1))
                sph_n = P.op("gpsimd", "dma_start",
                             out=h_d[t * 128:(t + 1) * 128,
                                     op * 512:(op + 1) * 512],
                             in_=htile[:], waits=[(ve, ve_hml[t % 2])],
                             inc=(sp_h, 16))
                w2_step()
                if op == 0:
                    xq_tile()
                elif op + 1 < NOPx and t % 2 == 0 and w1pe_idx < 8:
                    wf = ()
                    if w1pe_idx == 0:
                        wf = [(pe, pe_mark), (ve, xq_state["evac"])
                              ] if op == 1 else [(pe, pe_mark),
                                                 (ve, w1pe["evac"])]
                    w1pe_rowchunk(op + 1, w1pe_idx, wf)
                    w1pe_idx += 1
            while op + 1 < NOPx and 0 < w1pe_idx < 8:
                w1pe_rowchunk(op + 1, w1pe_idx)
                w1pe_idx += 1
            w2_respill(9)
        while run_D1 and xq_state["tt"] < NTx:
            xq_tile()
        while w2s["rb"] < NW2x:
            w2_step()
        while run_D1 and w2s["i"] < NI:
            w2_respill(4)

        P.wait("gpsimd", sp_h, sp_h.n)
        P.wait("gpsimd", sp_w2t, sp_w2t.n)
        P.wait("gpsimd", sp_w2u, sp_w2u.n)
        es1.close()
        barrier()

        es2 = contextlib.ExitStack()
        w2full = sb(es2, "w2full", [128, NI * 512], BF16)
        hq_fm = sb(es2, "hq_fm", [128, NI * 512], BF16)
        hrow = sb(es2, "hrow", [128, I], F32)
        hqrow = sb(es2, "hqrow", [128, I], BF16)
        i32h = sb(es2, "i32h", [128, I // 2], I32)
        y2g = sb(es2, "y2g", [128, 4 * C], F32)
        y2 = sb(es2, "y2", [128, C], F32)
        x2res = sb(es2, "x2res", [128, 2 * C], BF16)
        outt = sb(es2, "outt", [128, C], F32)
        sq2 = sb(es2, "sq2", [128, C], BF16)
        pD = es2.enter_context(nc.psum_tensor("pD", [128, 1024], F32))
        pE = es2.enter_context(nc.psum_tensor("pE", [128, 1024], F32))
        pT2 = es2.enter_context(nc.psum_tensor("pT2", [128, 1024], BF16))

        NT4 = (NT // 4) if run_D2 else 0
        CH = max(1, (NI + 3) // 4)
        n_out = [0, 0]
        last_nfin = 0
        pe_hq_read = 0
        ve_dn = [0, 0]
        ac_dn = [0, 0]
        hqs = {"tq": 0, "tt": 0, "tr": 0, "i32": 0, "hq": 0}
        ve_hq_all = [0] * max(NT, 1)
        nhq_all = [0] * max(NT, 1)

        def hq_quant():
            tt = hqs["tq"]
            if tt >= 4 * NT4:
                return
            hqs["tq"] += 1
            nlh = P.op("sync", "dma_start", out=hrow[:],
                       in_=h_d[tt * 128:(tt + 1) * 128, :],
                       waits=[(ac, hqs["i32"])], inc=(lh, 16))
            P.op("vector", "reduce_max", stat[:, 8:9], hrow[:], axis=AX.X,
                 apply_absolute_value=True, waits=[(lh, nlh)],
                 inc=(ve, 1))
            P.op("vector", "tensor_scalar", stat[:, 9:10], stat[:, 8:9],
                 1.0 / 127.0, 1e-8, ALU.mult, ALU.max, inc=(ve, 1))
            P.op("vector", "tensor_copy", s2g[:, tt % 4:tt % 4 + 1],
                 stat[:, 9:10],
                 waits=[(ac, ac_dn[0]), (ac, ac_dn[1])], inc=(ve, 1))
            nrc = P.op("vector", "reciprocal", stat[:, 10:11],
                       stat[:, 9:10], inc=(ve, 1))
            nhq = 0
            for hf in range(2):
                na = P.op("scalar", "activation", i32h[:],
                          hrow[:, hf * (I // 2):(hf + 1) * (I // 2)],
                          AF.Identity, bias=0.0, scale=stat[:, 10:11],
                          waits=[(ve, nrc if hf == 0 else nhq)],
                          inc=(ac, 1))
                hqs["i32"] = na
                nhq = P.op("vector", "tensor_copy",
                           hqrow[:, hf * (I // 2):(hf + 1) * (I // 2)],
                           i32h[:], waits=[(ac, na),
                                           (pe, hqs["tr"])],
                           inc=(ve, 1))
            nhq_all[tt] = nhq

        def hq_tr():
            tt = hqs["tt"]
            if tt >= hqs["tq"]:
                return
            hqs["tt"] += 1
            nhq = nhq_all[tt]
            for iq in range((NI + 7) // 8):
                k8 = min(8, NI - iq * 8)
                n_tr = 0
                for k in range(k8):
                    i = iq * 8 + k
                    n_tr = P.op("tensor", "transpose",
                                pT2[:, k * 128:(k + 1) * 128],
                                hqrow[:, i * 128:(i + 1) * 128], IDENT,
                                waits=[(ve, nhq), (ve, hqs["hq"]),
                                       (pe, pe_hq_read)]
                                if (k == 0 and iq == 0) else
                                ([(ve, nhq), (ve, hqs["hq"])]
                                 if k == 0 else ()), inc=(pe, 1))
                hqs["tr"] = n_tr
                dst = hq_fm[:, iq * 8 * 512 + (tt % 4) * 128:]
                dst = bass.AP(dst.tensor, dst.offset,
                              [[dst.ap[0][0], 128], [512, k8], [1, 128]])
                hqs["hq"] = P.op("vector", "tensor_copy", dst,
                                 pT2[:, 0:k8 * 128]
                                 .rearrange("p (k c) -> p k c", k=k8),
                                 waits=[(pe, n_tr)], inc=(ve, 1))
            ve_hq_all[tt] = hqs["hq"]

        def hq_tile():
            hq_quant()
            hq_tr()

        for tg in range(NT4):
            while hqs["tt"] < tg * 4 + 2:
                hq_tile()
            for ob in range(NOB):
                nw2c = [0] * 4
                for chk in range(4):
                    i0 = chk * CH
                    if i0 >= NI:
                        break
                    i1 = min(NI, i0 + CH)
                    nw2c[chk] = P.op(
                        "sync", "dma_start",
                        out=w2full[:, i0 * 512:i1 * 512]
                        .rearrange("p (i o) -> p i o", i=i1 - i0),
                        in_=w2qT_d[ob * I + i0 * 128:ob * I + i1 * 128, :]
                        .rearrange("(i p) o -> p i o", p=128),
                        waits=[(ac, ac_dn[0]), (ac, ac_dn[1])]
                        if chk == 0 else (),
                        inc=(lwf_[chk], 16))
                for t4 in range(4):
                    if ob == 0 and t4 < 2:
                        hq_tile()
                    if ob == NOB - 1 and t4 == 2:
                        hq_quant()
                    pDD = pD if t4 % 2 == 0 else pE
                    po = pDD[:, (ob % 2) * 512:(ob % 2) * 512 + 512]
                    n_d = 0
                    for i in range(NI):
                        wts3 = ()
                        if i == 0:
                            wts3 = [(lwf_[0], nw2c[0]),
                                    (ve, ve_hq_all[tg * 4 + t4]),
                                    (ac, ac_dn[t4 % 2])]
                        elif i % CH == 0:
                            wts3 = [(lwf_[i // CH], nw2c[i // CH])]
                        n_d = P.op(
                            "tensor", "matmul", po,
                            hq_fm[:, i * 512 + t4 * 128:i * 512 + t4 * 128 + 128],
                            w2full[:, i * 512:(i + 1) * 512],
                            start=(i == 0), stop=(i == NI - 1),
                            waits=wts3, inc=(pe, 1) if i == NI - 1 else None)
                    ac_dn[t4 % 2] = P.op(
                        "scalar", "activation",
                        y2g[:, t4 * C + ob * 512:t4 * C + (ob + 1) * 512],
                        po, AF.Identity, bias=0.0, scale=s2g[:, t4:t4 + 1],
                        waits=[(pe, n_d), (ve, ve_dn[t4 % 2])], inc=(ac, 1))
            pe_hq_read = pe.n
            if tg + 1 < NT4:
                hq_tr()
                hq_tile()
            for t4 in range(4):
                tt = tg * 4 + t4
                nx2l = P.op("sync", "dma_start",
                            out=x2res[:, (t4 % 2) * C:(t4 % 2 + 1) * C],
                            in_=x2_d[tt * 128:(tt + 1) * 128, :],
                            waits=[(ve, ve_dn[t4 % 2])],
                            inc=(lx2[t4 % 2], 16))
                nv2 = P.op("vector", "tensor_tensor", y2[:],
                           y2g[:, t4 * C:(t4 + 1) * C],
                           x2res[:, (t4 % 2) * C:(t4 % 2 + 1) * C], ALU.add,
                           waits=[(lx2[t4 % 2], nx2l), (ac, ac_dn[t4 % 2]),
                                  (so_[t4 % 2], n_out[t4 % 2]),
                                  (ac, last_nfin)], inc=(ve, 1))
                ve_dn[t4 % 2] = nv2
                nsq2 = P.op("scalar", "activation", sq2[:], y2[:], AF.Square,
                            bias=0.0, scale=1.0, accum_out=stat[:, 0:1],
                            waits=[(ve, nv2)], inc=(ac, 1))
                P.op("vector", "reduce_sum", stat[:, 1:2], y2[:], axis=AX.X,
                     waits=[(ve, nv2)], inc=(ve, 1))
                P.op("vector", "tensor_scalar", stat[:, 2:3], stat[:, 1:2],
                     1.0 / C, None, ALU.mult, inc=(ve, 1))
                P.op("vector", "tensor_tensor", stat[:, 3:4], stat[:, 2:3],
                     stat[:, 2:3], ALU.mult, inc=(ve, 1))
                P.op("vector", "tensor_scalar", stat[:, 4:5], stat[:, 0:1],
                     1.0 / C, None, ALU.mult, waits=[(ac, nsq2)], inc=(ve, 1))
                P.op("vector", "tensor_tensor", stat[:, 5:6],
                     stat[:, 4:5], stat[:, 3:4], ALU.subtract, inc=(ve, 1))
                nvar = P.op("vector", "tensor_scalar", stat[:, 5:6],
                            stat[:, 5:6], 1e-5, None, ALU.add, inc=(ve, 1))
                nsd = P.op("scalar", "activation", stat[:, 6:7], stat[:, 5:6],
                           AF.Sqrt, bias=0.0, scale=1.0, waits=[(ve, nvar)],
                           inc=(ac, 1))
                P.op("vector", "reciprocal", stat[:, 7:8], stat[:, 6:7],
                     waits=[(ac, nsd)], inc=(ve, 1))
                nnb = P.op("vector", "scalar_tensor_tensor", stat[:, 12:13],
                           stat[:, 2:3], -1.0, stat[:, 7:8], ALU.mult,
                           ALU.mult, inc=(ve, 1))
                nfin = P.op("scalar", "activation", outt[:], y2[:],
                            AF.Identity, bias=stat[:, 12:13],
                            scale=stat[:, 7:8],
                            waits=[(ve, nnb), (so_[0], n_out[0]),
                                   (so_[1], n_out[1])], inc=(ac, 1))
                last_nfin = nfin
                n_out[t4 % 2] = P.op("gpsimd", "dma_start",
                             out=out_d[tt * 128:(tt + 1) * 128, :],
                             in_=outt[:], waits=[(ac, nfin)],
                             inc=(so_[t4 % 2], 16))
        P.wait("gpsimd", so_[0], so_[0].n)
        P.wait("gpsimd", so_[1], so_[1].n)
        if cfg.debug:
            P.wait("gpsimd", sod, sod.n)
        es2.close()

        blk = es.enter_context(nc.Block())

        @blk.sync
        def _(e):
            P.replay("sync", e, nc, collect, fix)

        @blk.tensor
        def _(e):
            P.replay("tensor", e, nc, collect, fix)

        @blk.vector
        def _(e):
            P.replay("vector", e, nc, collect, fix)

        @blk.scalar
        def _(e):
            P.replay("scalar", e, nc, collect, fix)

        @blk.gpsimd
        def _(e):
            P.replay("gpsimd", e, nc, collect, fix)

    return nc


def build(cfg):
    m = {}
    _build(cfg, collect=m)
    return _build(cfg, fix=m)


def make_in_maps(cfg, x, wq, bq_, wk, bk_, wv, bv_, wo, bo_, w1, w2,
                 n_cores=8):
    consts = host_consts(cfg)
    wqT_b = np.ascontiguousarray(
        np.asarray(wq, np.float32).T.reshape(cfg.C, cfg.H, cfg.D)
        .transpose(1, 0, 2))
    shared = dict(
        wqT=wqT_b,
        wkT=np.ascontiguousarray(np.asarray(wk, np.float32).T),
        wvT=np.ascontiguousarray(np.asarray(wv, np.float32).T),
        woT=np.ascontiguousarray(np.asarray(wo, np.float32).T),
        w1=np.ascontiguousarray(np.asarray(w1, np.float32)),
        w2=np.ascontiguousarray(np.asarray(w2, np.float32)),
        bqT=np.ascontiguousarray(
            np.asarray(bq_, np.float32).reshape(cfg.H, cfg.D).T),
        bkT=np.ascontiguousarray(
            np.asarray(bk_, np.float32).reshape(cfg.KV, cfg.D).T),
        bv_r=np.ascontiguousarray(np.asarray(bv_, np.float32).reshape(1, -1)),
        bo_r=np.ascontiguousarray(np.asarray(bo_, np.float32).reshape(1, -1)),
        rts=consts["rts"], mbase=consts["mbase"], ident=consts["ident"],
        ones_col=consts["ones_col"], ones_row=consts["ones_row"],
        ones_rowb=consts["ones_rowb"], ones_f=consts["ones_f"],
        ones_b=consts["ones_b"], scale_col=consts["scale_col"])
    in_maps = []
    for c in range(n_cores):
        b, half = divmod(c, 2)
        r = half * cfg.T
        xrot = np.roll(x[b], -r, axis=0)
        in_maps.append(dict(shared,
                            xT=np.ascontiguousarray(xrot.T),
                            x_own=np.ascontiguousarray(x[b, r:r + cfg.T]),
                            beta=host_beta(cfg, half)))
    return in_maps


def kernel(L0_attn_x, L0_attn_wq, L0_attn_bq, L0_attn_wk, L0_attn_bk,
           L0_attn_wv, L0_attn_bv, L0_attn_wo, L0_attn_bo,
           L0_ffn_w_gate_up, L0_ffn_w_down):
    cfg = Cfg()
    x = np.ascontiguousarray(np.asarray(L0_attn_x, dtype=np.float32))
    in_maps = make_in_maps(cfg, x, L0_attn_wq, L0_attn_bq, L0_attn_wk,
                           L0_attn_bk, L0_attn_wv, L0_attn_bv, L0_attn_wo,
                           L0_attn_bo, L0_ffn_w_gate_up, L0_ffn_w_down)
    nc = build(cfg)
    res = run_bass_kernel_spmd(nc, in_maps, core_ids=list(range(8)))
    B = x.shape[0]
    outs = np.empty((B, cfg.S, cfg.C), dtype=np.float32)
    for c in range(8):
        b, half = divmod(c, 2)
        outs[b, half * cfg.T:(half + 1) * cfg.T] = res.results[c]["out"]
    return outs


# version 18
# speedup vs baseline: 1.0749x; 1.0027x over previous
import math
import contextlib
import numpy as np
import ml_dtypes
import concourse.bass as bass
import concourse.mybir as mybir
from concourse.bass_utils import run_bass_kernel_spmd

F32 = mybir.dt.float32
BF16 = mybir.dt.bfloat16
I32 = mybir.dt.int32
AF = mybir.ActivationFunctionType
ALU = mybir.AluOpType
AX = mybir.AxisListType
BF = ml_dtypes.bfloat16


class Cfg:
    def __init__(self, S=4096, C=2048, I=5632, H=16, KV=4, TG=1024, debug=False,
                 phases="all"):
        self.S, self.C, self.I, self.H, self.KV, self.TG = S, C, I, H, KV, TG
        self.phases = phases
        self.T = S // 2
        self.D = 128
        self.debug = debug
        assert C == H * self.D and C % 1024 == 0 and S % 1024 == 0
        assert TG % 512 == 0 and self.T % TG == 0 and I % 512 == 0
        self.NC = C // 128
        self.NS = S // 128
        self.NG = self.T // TG
        self.NMIX = TG // 128
        self.NI = I // 128
        self.NT = self.T // 128
        self.NTG = TG // 128
        self.G = H // KV
        self.NOB = C // 512


def host_consts(cfg):
    D, H, TG, NMIX = cfg.D, cfg.H, cfg.TG, cfg.NMIX
    half = D // 2
    inv_freq = 1.0 / 10000.0 ** (np.arange(0, D, 2, dtype=np.float64) / D)
    pos = np.arange(H, dtype=np.float64)
    freqs = np.einsum("i,j->ij", pos, inv_freq)
    emb = np.concatenate([freqs, freqs], axis=-1)
    cos, sin = np.cos(emb), np.sin(emb)
    RT = np.zeros((H, D, D), dtype=np.float64)
    for h in range(H):
        R = np.zeros((D, D))
        for d in range(D):
            R[d, d] = cos[h, d]
        for d in range(half):
            R[d, d + half] = -sin[h, d]
            R[d + half, d] = sin[h, d + half]
        RT[h] = R.T
    rts = np.ascontiguousarray(RT.reshape(H * D, D)).astype(np.float32)
    W = TG + 128 * (NMIX - 1)
    u = np.arange(W)[None, :]
    p = np.arange(128)[:, None]
    base = np.exp((u >= (NMIX - 1) * 128 + p).astype(np.float64)).astype(BF)
    return dict(rts=rts, mbase=np.ascontiguousarray(base),
                ident=np.eye(128, dtype=np.float32).astype(BF),
                ones_col=np.ones((128, 1), dtype=np.float32).astype(BF),
                ones_row=np.ones((1, 128), dtype=np.float32),
                ones_rowb=np.ones((1, 128), dtype=np.float32).astype(BF),
                ones_f=np.full((128, 128), 1.0 / 128, dtype=np.float32)
                .astype(BF),
                ones_b=np.ones((128, 128), dtype=np.float32).astype(BF),
                scale_col=np.full((128, 1), D ** -0.25, dtype=np.float32))


def host_beta(cfg, core_half):
    S, TG, NG, NS, T = cfg.S, cfg.TG, cfg.NG, cfg.NS, cfg.T
    r = core_half * T
    beta = np.zeros((128, NG * NS), dtype=np.float32)
    for g in range(NG):
        t_lo = r + g * TG
        for j in range(NS):
            srot0 = 128 * j
            if g * TG <= srot0 < (g + 1) * TG:
                continue
            for p in range(128):
                s_orig = (srot0 + p + r) % S
                beta[p, g * NS + j] = 1.0 if s_orig <= t_lo else 0.0
    return beta


class Sem:
    __slots__ = ("h", "n", "name", "log_seen")

    def __init__(self, es, nc, name):
        self.h = es.enter_context(nc.semaphore(name))
        self.n = 0
        self.name = name
        self.log_seen = 0

    def will(self, k=1):
        self.n += k
        assert self.n < 2 ** 31
        return self.n


class Prog:
    ENGS = ("sync", "tensor", "vector", "scalar", "gpsimd")

    def __init__(self):
        self.ops = {e: [] for e in self.ENGS}
        self.selfsem = {}

    def op(self, eng, meth, *args, waits=(), inc=None, **kw):
        ss = self.selfsem.get(eng)
        if ss is not None and ss.n > 0:
            waits = tuple(waits) + ((ss, ss.n),)
        tot = None
        if inc is not None:
            tot = inc[0].will(inc[1])
        self.ops[eng].append((meth, args, kw, tuple(waits), inc))
        return tot

    def wait(self, eng, sem, val):
        if val > 0:
            self.ops[eng].append(("__wait__", (sem, val), {}, (), None))

    def sem_inc(self, eng, sem, k=1):
        tot = sem.will(k)
        self.ops[eng].append(("__inc__", (sem, k), {}, (), None))
        return tot

    def replay(self, eng, e, nc=None, collect=None, fix=None):
        def xlate(s, v):
            if fix is not None:
                return fix.get(s.h.__class__ and getattr(s, 'name', ''), {}).get(v, v)
            return v

        for meth, args, kw, waits, inc in self.ops[eng]:
            if meth == "__wait__":
                s, v = args
                e.wait_ge(s.h, self._fx(fix, s, v))
                continue
            if meth == "__inc__":
                e.sem_inc(args[0].h, args[1])
                continue
            for s, v in waits:
                if v > 0:
                    e.wait_ge(s.h, self._fx(fix, s, v))
            if meth == "dma_start" and (collect is not None) and inc is not None:
                n0 = nc.n_instructions()
                ins = getattr(e, meth)(*args, **kw)
                ndma = nc.n_instructions() - n0
                ins.then_inc(inc[0].h, inc[1])
                s = inc[0]
                st = collect.setdefault(s.name, {"act": 0, "map": {0: 0}})
                st["act"] += 16 * max(1, ndma)
                st["map"][s.log_seen + inc[1]] = st["act"]
                s.log_seen += inc[1]
            else:
                ins = getattr(e, meth)(*args, **kw)
                if inc is not None:
                    ins.then_inc(inc[0].h, inc[1])
        return None

    @staticmethod
    def _fx(fix, s, v):
        if fix is None:
            return v
        m = fix.get(s.name)
        if m is None:
            return v
        return m["map"].get(v, v)


class TreeSum:

    def __init__(self, P, ve, pool):
        self.P, self.ve = P, ve
        self.pool = list(pool)
        self.levels = []

    def feed_pair(self, ap_a, rdy_a, ap_b, rdy_b):
        out = self.pool.pop()
        n = self.P.op("vector", "tensor_tensor", out, ap_a, ap_b, ALU.add,
                      waits=[rdy_a, rdy_b], inc=(self.ve, 1))
        carry = (out, (self.ve, n))
        lvl = 0
        while lvl < len(self.levels) and self.levels[lvl] is not None:
            a, ra = self.levels[lvl]
            n = self.P.op("vector", "tensor_tensor", a, a, carry[0], ALU.add,
                          waits=[ra, carry[1]], inc=(self.ve, 1))
            self.pool.append(carry[0])
            carry = (a, (self.ve, n))
            self.levels[lvl] = None
            lvl += 1
        if lvl == len(self.levels):
            self.levels.append(None)
        self.levels[lvl] = carry

    def root(self):
        occ = [x for x in self.levels if x is not None]
        assert occ
        ap, rdy = occ[0]
        for ap2, rdy2 in occ[1:]:
            n = self.P.op("vector", "tensor_tensor", ap, ap, ap2, ALU.add,
                          waits=[rdy, rdy2], inc=(self.ve, 1))
            rdy = (self.ve, n)
        return ap, rdy


def _build(cfg, collect=None, fix=None):
    nc = bass.Bass("TRN2", target_bir_lowering=False)
    S, T, C, I, H, KV, D, TG = (cfg.S, cfg.T, cfg.C, cfg.I, cfg.H, cfg.KV,
                                cfg.D, cfg.TG)
    NC, NS, NG, NMIX, NI, NT, NTG, NOB, G = (cfg.NC, cfg.NS, cfg.NG, cfg.NMIX,
                                             cfg.NI, cfg.NT, cfg.NTG, cfg.NOB,
                                             cfg.G)
    KVD = KV * D
    I2 = 2 * I
    MW = TG + 128 * (NMIX - 1)
    NW1 = I2 // 128
    NW2 = C // 128
    TH = TG // 512
    NSG = S // 1024

    dt = nc.dram_tensor
    xT_d = dt("xT", [C, S], F32, kind="ExternalInput").ap()
    xown_d = dt("x_own", [T, C], F32, kind="ExternalInput").ap()
    wqT_d = dt("wqT", [H, C, D], F32, kind="ExternalInput").ap()
    wkT_d = dt("wkT", [C, KVD], F32, kind="ExternalInput").ap()
    wvT_d = dt("wvT", [C, KVD], F32, kind="ExternalInput").ap()
    woT_d = dt("woT", [C, C], F32, kind="ExternalInput").ap()
    w1_d = dt("w1", [I2, C], F32, kind="ExternalInput").ap()
    w2_d = dt("w2", [C, I], F32, kind="ExternalInput").ap()
    bqT_d = dt("bqT", [128, H], F32, kind="ExternalInput").ap()
    bkT_d = dt("bkT", [128, KV], F32, kind="ExternalInput").ap()
    bv_d = dt("bv_r", [1, KVD], F32, kind="ExternalInput").ap()
    bo_d = dt("bo_r", [1, C], F32, kind="ExternalInput").ap()
    beta_d = dt("beta", [128, NG * NS], F32, kind="ExternalInput").ap()
    rts_d = dt("rts", [H * D, D], F32, kind="ExternalInput").ap()
    mbase_d = dt("mbase", [128, MW], BF16, kind="ExternalInput").ap()
    ident_d = dt("ident", [128, 128], BF16, kind="ExternalInput").ap()
    onesc_d = dt("ones_col", [128, 1], BF16, kind="ExternalInput").ap()
    onesr_d = dt("ones_row", [1, 128], F32, kind="ExternalInput").ap()
    onesrb_d = dt("ones_rowb", [1, 128], BF16, kind="ExternalInput").ap()
    onesf_d = dt("ones_f", [128, 128], BF16, kind="ExternalInput").ap()
    onesb_d = dt("ones_b", [128, 128], BF16, kind="ExternalInput").ap()
    scalec_d = dt("scale_col", [128, 1], F32, kind="ExternalInput").ap()

    out_d = dt("out", [T, C], F32, kind="ExternalOutput").ap()

    spk = "ExternalOutput" if cfg.debug else "Internal"
    w1q_d = dt("w1q_d", [I2, C], BF16, kind=spk).ap()
    w2q_d = dt("w2q_d", [C, I], BF16, kind="Internal").ap()
    xTbf_d = dt("xTbf_d", [C, T], BF16, kind="Internal").ap()
    woTbf_d = dt("woTbf_d", [C, C], BF16, kind="Internal").ap()
    wqbf_d = dt("wqbf_d", [H, C, D], BF16, kind="Internal").ap()
    w2qT_d = dt("w2qT_d", [NOB * I, 512], BF16, kind="Internal").ap()
    x2_d = dt("x2_d", [T, C], BF16, kind=spk).ap()
    xq_d = dt("xq_d", [T, C], BF16, kind=spk).ap()
    h_d = dt("h_d", [T, I], F32, kind=spk).ap()

    dbg = {}
    if cfg.debug:
        dbg["k"] = dt("dbg_k", [128, KV * S], F32, kind="ExternalOutput").ap()
        dbg["v"] = dt("dbg_v", [128, NS * KVD], F32, kind="ExternalOutput").ap()
        dbg["at"] = dt("dbg_at", [128, NG * H * TG], F32,
                       kind="ExternalOutput").ap()
        dbg["q"] = dt("dbg_q", [128, TG], F32, kind="ExternalOutput").ap()
        dbg["sg"] = dt("dbg_sg", [1, TG], F32, kind="ExternalOutput").ap()
        dbg["pt"] = dt("dbg_pt", [128, 4 * TG], F32, kind="ExternalOutput").ap()
        dbg["sm"] = dt("dbg_sm", [128, TG], F32, kind="ExternalOutput").ap()

    P = Prog()
    es = contextlib.ExitStack()

    def sb(stack, name, shape, dtp):
        return stack.enter_context(nc.sbuf_tensor(name, shape, dtp))

    with es:
        consts = sb(es, "consts", [128, H * D + 385], BF16)
        mbase = sb(es, "mbase_s", [128, MW], BF16)
        beta = sb(es, "beta_s", [128, NG * NS], F32)
        onesr = sb(es, "onesr", [1, 128], F32)
        onesrb = sb(es, "onesrb", [1, 128], BF16)
        scalec = sb(es, "scalec", [128, 1], F32)
        bq = sb(es, "bq_s", [128, H], F32)
        bk = sb(es, "bk_s", [128, KV], F32)
        bvb = sb(es, "bvb", [128, KVD], BF16)
        bob = sb(es, "bob", [128, C], BF16)
        km2 = sb(es, "km2", [1, KV + H + 8], F32)
        sx_all = sb(es, "sx_all", [128, NT], F32)
        s2g = sb(es, "s2g", [128, 4], F32)
        stat = sb(es, "stat", [128, 14], F32)
        sigr = sb(es, "sigr", [1, 8], F32)

        RT_ = lambda hh: consts[:, hh * D:(hh + 1) * D]
        IDENT = consts[:, H * D:H * D + 128]
        ONESC = consts[:, H * D + 128:H * D + 129]
        ONESF = consts[:, H * D + 129:H * D + 257]
        ONESB = consts[:, H * D + 257:H * D + 385]

        pe = Sem(es, nc, "pe")
        ve = Sem(es, nc, "ve")
        ac = Sem(es, nc, "ac")
        bar = Sem(es, nc, "bar")
        P.selfsem = {"vector": ve, "scalar": ac}
        _s = lambda nm: Sem(es, nc, nm)
        ldi = _s("ldi")
        lrt = _s("lrt")
        ldb = _s("ldb")
        lwk = _s("lwk")
        lxb = [_s("lxb0"), _s("lxb1")]
        lwq = [_s("lwq0"), _s("lwq1")]
        lwo = _s("lwo")
        lxr = [_s("lxr0"), _s("lxr1")]
        lw1 = _s("lw1")
        sp_w1 = _s("sp_w1")
        lw2 = _s("lw2")
        sp_w2 = _s("sp_w2")
        lw2t = _s("lw2t")
        sp_w2t = _s("sp_w2t")
        sp_w2u = _s("sp_w2u")
        lxq = _s("lxq")
        lw1t = [_s("lw1t0"), _s("lw1t1")]
        lw1s = [_s("lw1s0"), _s("lw1s1")]
        lw2tb = _s("lw2tb")
        lwr = _s("lwr")
        sp_h = _s("sp_h")
        lh = _s("lh")
        lwf_ = [_s("lwf"), _s("lwf1"), _s("lwf2"), _s("lwf3")]
        lx2 = [_s("lx20"), _s("lx21")]
        sp_x2 = _s("sp_x2")
        sp_xq = _s("sp_xq")
        so_ = [_s("so0"), _s("so1")]
        sod = _s("sod")
        pcxx = [_s(f"pcxx{i}") for i in range(cfg.NG)]
        pcxq = [_s(f"pcxq{i}") for i in range((cfg.H + 3) // 4)]
        pcxw = _s("pcxw")
        lxc = _s("lxc")

        def barrier():
            for eng in Prog.ENGS:
                P.sem_inc(eng, bar, 1)
            t = bar.n
            for eng in Prog.ENGS:
                P.wait(eng, bar, t)

        n_rts = P.op("gpsimd", "dma_start",
                     out=consts[:, :H * D].rearrange("p (h d) -> p h d", h=H),
                     in_=rts_d.rearrange("(h p) d -> p h d", p=D),
                     inc=(lrt, 16))
        P.op("sync", "dma_start", out=IDENT, in_=ident_d, inc=(ldi, 16))
        P.op("sync", "dma_start", out=ONESC, in_=onesc_d, inc=(ldi, 16))
        P.op("sync", "dma_start", out=mbase[:], in_=mbase_d, inc=(ldi, 16))
        P.op("sync", "dma_start", out=beta[:], in_=beta_d, inc=(ldi, 16))
        P.op("sync", "dma_start", out=onesr[:], in_=onesr_d, inc=(ldi, 16))
        P.op("sync", "dma_start", out=onesrb[:], in_=onesrb_d, inc=(ldi, 16))
        P.op("sync", "dma_start", out=ONESF, in_=onesf_d, inc=(ldi, 16))
        P.op("sync", "dma_start", out=ONESB, in_=onesb_d, inc=(ldi, 16))
        P.op("sync", "dma_start", out=scalec[:], in_=scalec_d, inc=(ldi, 16))
        P.op("sync", "dma_start", out=bq[:], in_=bqT_d, inc=(ldi, 16))
        n_ld_init = P.op("sync", "dma_start", out=bk[:], in_=bkT_d,
                         inc=(ldi, 16))

        esc = contextlib.ExitStack()
        arena = sb(esc, "arena", [128, 32768], BF16)
        A_XTB, A_WQB = 0, NC * TG
        A_QH = A_WQB + 2 * NC * 128
        A_QPRE = A_QH + 2 * TG
        A_PT = A_QPRE + TG
        A_PTS = A_PT + 3 * TG
        A_TREE = A_PTS + TG
        assert A_TREE + 5 * TG <= 32768, (A_TREE, TG)
        xTb = arena[:, A_XTB:A_XTB + NC * TG]
        sig = sb(esc, "sig", [128, NS], F32)
        y1 = sb(esc, "y1", [128, C], F32)
        recipb = y1[:, 0:TG]
        recro = recipb[0:1, 0:TG]
        xres = sb(esc, "xres", [128, C], F32)
        sgrow = xres[0:1, 0:TG]
        x2t = sb(esc, "x2t", [128, 1024], BF16)
        xqt = sb(esc, "xqt", [128, 1024], BF16)
        sgbb_bufs = [x2t[:, 0:TG], xqt[:, 0:TG]]
        i32q = sb(esc, "i32q", [128, 512], I32)
        w1f = sb(esc, "w1f", [128, C], F32)
        w1qb = sb(esc, "w1qb", [128, 1024], BF16)
        attn = sb(esc, "attn", [128, H * TG], BF16)
        kpre = attn[:, 0:1024]
        assert 2 * NC * 1024 + 2 * NC * KVD <= 32768 or True
        wkb = arena[:, NC * 1024:NC * 1024 + NC * KVD]
        wvb = arena[:, NC * 1024 + NC * KVD:NC * 1024 + 2 * NC * KVD]
        k_fm = sb(esc, "k_fm", [128, KV * S], BF16)
        v_tm = sb(esc, "v_tm", [128, NS * KVD], BF16)

        pA = esc.enter_context(nc.psum_tensor("pA", [128, 1024], F32))
        pB = esc.enter_context(nc.psum_tensor("pB", [128, 1024], F32))
        pC = esc.enter_context(nc.psum_tensor("pC", [128, 1024], F32))
        pV = esc.enter_context(nc.psum_tensor("pV", [128, 1024], F32))
        PSC = [pA, pB, pC]

        n_bv_ld = P.op("sync", "dma_start", out=y1[0:1, 0:KVD], in_=bv_d,
                       inc=(ldb, 16))

        def bcast_row(width, wait_ld, dst, psum):
            nv = 0
            prev = {}
            for o in range(0, width, 512):
                w = min(512, width - o)
                slot = (o // 512) % 2
                wts = [(ldb, wait_ld), (ldi, n_ld_init)] if o == 0 else []
                if slot in prev:
                    wts.append((ve, prev[slot]))
                nm = P.op("tensor", "matmul", psum[:, slot * 512:slot * 512 + w],
                          onesr[:], y1[0:1, o:o + w], start=True, stop=True,
                          waits=wts, inc=(pe, 1))
                nv = P.op("vector", "tensor_copy", dst[:, o:o + w],
                          psum[:, slot * 512:slot * 512 + w], waits=[(pe, nm)],
                          inc=(ve, 1))
                prev[slot] = nv
            return nv

        n_bv = bcast_row(KVD, n_bv_ld, bvb, pA)
        n_bo_ld = P.op("sync", "dma_start", out=y1[0:1, 0:C], in_=bo_d,
                       waits=[(ve, n_bv)], inc=(ldb, 16))
        n_bo = bcast_row(C, n_bo_ld, bob, pB)

        n_wk = P.op("gpsimd", "dma_start",
                    out=wkb.rearrange("p (c o) -> p c o", o=KVD),
                    in_=wkT_d.rearrange("(c p) o -> p c o", p=128),
                    inc=(lwk, 16))
        n_wv = P.op("gpsimd", "dma_start",
                    out=wvb.rearrange("p (c o) -> p c o", o=KVD),
                    in_=wvT_d.rearrange("(c p) o -> p c o", p=128),
                    inc=(lwk, 16))

        NPWQ = (H + 3) // 4
        pcx_x = [0] * NG
        pcx_wo = [0]
        pcx_wq = [0] * NPWQ

        def _pcx_g(gg):
            def f():
                pcx_x[gg] = P.op("gpsimd", "dma_start",
                                 out=xTbf_d[:, gg * TG:(gg + 1) * TG],
                                 in_=xT_d[:, gg * TG:(gg + 1) * TG],
                                 inc=(pcxx[gg], 16))
            return f

        def _pcx_wo(po_):
            def f():
                pcx_wo[0] = P.op("gpsimd", "dma_start",
                                 out=woTbf_d[po_ * (C // 2):
                                             (po_ + 1) * (C // 2), :],
                                 in_=woT_d[po_ * (C // 2):
                                           (po_ + 1) * (C // 2), :],
                                 inc=(pcxw, 16))
            return f

        def _pcx_wq(pp):
            h0, h1 = pp * 4, min(H, pp * 4 + 4)

            def f():
                pcx_wq[pp] = P.op(
                    "gpsimd", "dma_start",
                    out=wqbf_d[h0:h1].rearrange("h c d -> (h c) d"),
                    in_=wqT_d[h0:h1].rearrange("h c d -> (h c) d"),
                    inc=(pcxq[pp], 16))
            return f

        precasts = [_pcx_g(0), _pcx_wq(0)]
        precasts += [_pcx_g(gg) for gg in range(1, NG)]
        precasts += [_pcx_wq(pp) for pp in range(1, NPWQ)]
        precasts += [_pcx_wo(0), _pcx_wo(1)]

        xb_reg = [arena[:, A_XTB:A_XTB + NC * 1024]] * 2
        xtb_cnt = [0] * NSG
        b_pe_done = [0] * NSG
        evA = [n_bo, n_bo]
        acA = [0, 0]
        acB = 0
        veC = [n_bv, n_bv]
        for sg in range(NSG):
            xb = xb_reg[sg % 2]
            w = []
            if sg >= 1:
                w.append((pe, b_pe_done[sg - 1]))
            xtb_cnt[sg] = P.op("gpsimd", "dma_start",
                               out=xb.rearrange("p (c s) -> p c s", s=1024),
                               in_=xT_d[:, sg * 1024:(sg + 1) * 1024]
                               .rearrange("(c p) s -> p c s", p=128),
                               waits=w, inc=(lxb[0], 16))
            if precasts:
                precasts.pop(0)()
            for kv in range(KV):
                for th in range(2):
                    wts = [(lxb[0], xtb_cnt[sg]), (ldi, n_ld_init),
                           (lwk, n_wv)]
                    if sg == 0 and kv == 0 and th == 0:
                        wts.append((ve, n_bo))
                    wts.append((ac, acA[th]))
                    n_mm = 0
                    for c in range(NC):
                        n_mm = P.op(
                            "tensor", "matmul", pA[:, th * 512:th * 512 + 512],
                            wkb[:, c * KVD + kv * 128:c * KVD + kv * 128 + 128],
                            xb[:, c * 1024 + th * 512:c * 1024 + th * 512 + 512],
                            start=(c == 0), stop=(c == NC - 1),
                            waits=wts if c == 0 else (),
                            inc=(pe, 1) if c == NC - 1 else None)
                    acA[th] = P.op("scalar", "activation",
                                   kpre[:, th * 512:th * 512 + 512],
                                   pA[:, th * 512:th * 512 + 512], AF.Identity,
                                   bias=bk[:, kv:kv + 1], scale=1.0,
                                   waits=[(pe, n_mm)], inc=(ac, 1))
                n_r = 0
                for th in range(2):
                    n_r = P.op("tensor", "matmul",
                               pB[:, th * 512:th * 512 + 512], RT_(kv),
                               kpre[:, th * 512:th * 512 + 512],
                               start=True, stop=True,
                               waits=[(ac, acA[1]), (ac, acB),
                                      (ldi, n_ld_init), (lrt, 16),
                                      (ve, n_bo if (sg == 0 and kv == 0) else 0)]
                               if th == 0 else (), inc=(pe, 1))
                acB = P.op("scalar", "activation",
                           k_fm[:, kv * S + sg * 1024:kv * S + (sg + 1) * 1024],
                           pB[:, 0:1024], AF.Identity, bias=0.0,
                           scale=scalec[:], waits=[(pe, n_r)], inc=(ac, 1))
            for st in range(8):
                po = pC[:, (st % 2) * 512:(st % 2) * 512 + KVD]
                wts = [(ve, veC[st % 2]), (lxb[0], xtb_cnt[sg])]
                n_mm = 0
                for c in range(NC):
                    n_mm = P.op(
                        "tensor", "matmul", po,
                        xb[:, c * 1024 + st * 128:c * 1024 + st * 128 + 128],
                        wvb[:, c * KVD:(c + 1) * KVD],
                        start=(c == 0), stop=(c == NC - 1),
                        waits=wts if c == 0 else (),
                        inc=(pe, 1) if c == NC - 1 else None)
                veC[st % 2] = P.op(
                    "vector", "scalar_tensor_tensor",
                    v_tm[:, (sg * 8 + st) * KVD:(sg * 8 + st + 1) * KVD],
                    po, 1.0, bvb[:], ALU.mult, ALU.add,
                    waits=[(pe, n_mm)], inc=(ve, 1))
            b_pe_done[sg] = pe.n
        while precasts:
            precasts.pop(0)()

        nsl = S // 512
        n_kmax_pe = 0
        km_red = [0, 0]
        for kv in range(KV):
            nparts = min(8, nsl)
            for sl in range(nsl):
                wts = ([(pe, b_pe_done[NSG - 1])] if (kv == 0 and sl == 0)
                       else [(pe, n_kmax_pe)])
                nq = P.op("scalar", "activation", kpre[:, 0:512],
                          k_fm[:, kv * S + sl * 512:kv * S + (sl + 1) * 512],
                          AF.Square, bias=0.0, scale=1.0, waits=wts,
                          inc=(ac, 1))
                n_kmax_pe = P.op("tensor", "matmul",
                                 pA[0:1, (sl % 2) * 512:(sl % 2) * 512 + 512],
                                 ONESC, kpre[:, 0:512], start=True, stop=True,
                                 waits=[(ac, nq), (ac, acB),
                                        (ve, km_red[sl % 2])], inc=(pe, 1))
                km_red[sl % 2] = P.op("vector", "reduce_max",
                     km2[0:1, KV + H + (sl % 8):KV + H + (sl % 8) + 1],
                     pA[0:1, (sl % 2) * 512:(sl % 2) * 512 + 512], axis=AX.X,
                     waits=[(pe, n_kmax_pe)], inc=(ve, 1))
            P.op("vector", "reduce_max", km2[0:1, kv:kv + 1],
                 km2[0:1, KV + H:KV + H + nparts], axis=AX.X,
                 waits=[(ve, ve.n)], inc=(ve, 1))
        kmi = km2.ap() if False else km2
        for kv2 in range(KV):
            c0 = KV + H + kv2
            P.op("vector", "tensor_scalar",
                 km2[0:1, c0:c0 + 1].bitcast(I32),
                 km2[0:1, kv2:kv2 + 1].bitcast(I32), 1, None,
                 ALU.arith_shift_right, inc=(ve, 1))
            P.op("vector", "tensor_scalar",
                 km2[0:1, c0:c0 + 1].bitcast(I32),
                 km2[0:1, c0:c0 + 1].bitcast(I32), 0x1FBD1DF5, None,
                 ALU.add, inc=(ve, 1))
        for hh in range(H):
            P.op("vector", "tensor_scalar", km2[0:1, KV + hh:KV + hh + 1],
                 km2[0:1, KV + H + hh // G:KV + H + hh // G + 1], -1.1, None,
                 ALU.mult, inc=(ve, 1))
        n_km2 = ve.n

        run_C = cfg.phases in ("all", "BC", "BCD1")
        run_D1 = cfg.phases in ("all", "BCD1")
        run_D2 = cfg.phases == "all"
        w1s = {"rb": 0, "st": 0, "ve": 0, "ac": 0, "sp": 0, "ld": 0, "nrc": 0}
        i32g = {"ve": 0}

        def w1_step():
            rb = w1s["rb"]
            if rb >= NW1 or not run_C:
                return
            st_ = w1s["st"]
            if st_ == 0:
                w1s["ld"] = P.op("sync", "dma_start", out=w1f[:],
                                 in_=w1_d[rb * 128:(rb + 1) * 128, :],
                                 waits=[(ac, w1s["ac"]), (ve, w1s["ve"])]
                                 if rb else (), inc=(lw1, 16))
            elif st_ == 1:
                P.op("vector", "reduce_max", stat[:, 8:9], w1f[:], axis=AX.X,
                     apply_absolute_value=True,
                     waits=[(lw1, w1s["ld"]), (ac, w1s["ac"])], inc=(ve, 1))
                P.op("vector", "tensor_scalar", stat[:, 9:10], stat[:, 8:9],
                     1.0 / 127.0, 1e-8, ALU.mult, ALU.max, inc=(ve, 1))
                w1s["nrc"] = P.op("vector", "reciprocal", stat[:, 10:11],
                                  stat[:, 9:10], inc=(ve, 1))
            else:
                nh = max(1, (C // 512) // 2)
                h0 = 0 if st_ == 2 else nh
                for hf in range(h0, h0 + nh):
                    na = P.op("scalar", "activation", i32q[:],
                              w1f[:, hf * 512:(hf + 1) * 512], AF.Identity,
                              bias=0.0, scale=stat[:, 10:11],
                              waits=[(ve, w1s["nrc"]), (sp_w1, w1s["sp"]),
                                     (ve, i32g["ve"])], inc=(ac, 1))
                    w1s["ac"] = na
                    w1s["ve"] = P.op("vector", "tensor_scalar",
                                     w1qb[:, (hf % 2) * 512:(hf % 2) * 512 + 512],
                                     i32q[:], stat[:, 9:10], None, ALU.mult,
                                     waits=[(ac, na)], inc=(ve, 1))
                    i32g["ve"] = w1s["ve"]
                w1s["sp"] = P.op("gpsimd", "dma_start",
                                 out=w1q_d[rb * 128:(rb + 1) * 128,
                                           h0 * 512:(h0 + nh) * 512],
                                 in_=w1qb[:, (h0 % 2) * 512:
                                          (h0 % 2) * 512 + nh * 512],
                                 waits=[(ve, w1s["ve"])],
                                 inc=(sp_w1, 16))
            w1s["st"] += 1
            if w1s["st"] == 4:
                w1s["st"] = 0
                w1s["rb"] += 1


        exp_slot2 = [acB, acB]
        pt_guard0 = [[], [], []]
        at_cnt = n_bv
        ve_recip = n_km2
        ve_recb = n_bo
        ve_sgc = [n_km2]
        n_woload = 0
        last_arena_pe = n_kmax_pe
        spill_x2 = [0]
        spill_xq = 0
        n_qmm_pc = [0, 0]
        wq_cnt = [0, 0]
        nxtb = [0]
        ve_y1 = [at_cnt, at_cnt]
        mix_nrd = [0]
        dbg_at_off = 0
        PSC2 = [pA, pB]
        J_PRE = NS - 14 if NS >= 20 else max(2, NS - 6)

        def emit_wq_load(h, wq_w):
            wqb = arena[:, A_WQB + (h % 2) * NC * 128:
                        A_WQB + (h % 2 + 1) * NC * 128]
            wq_cnt[h % 2] = P.op("sync", "dma_start",
                                 out=wqb.rearrange("p (c d) -> p c d", d=128),
                                 in_=wqbf_d[h]
                                 .rearrange("(c p) d -> p c d", p=128),
                                 waits=list(wq_w) + [(pcxq[h // 4],
                                                      pcx_wq[h // 4])],
                                 inc=(lwq[h % 2], 16))

        def emit_q_chain(g, h, q_extra=(), boundary=False):
            wqb = arena[:, A_WQB + (h % 2) * NC * 128:
                        A_WQB + (h % 2 + 1) * NC * 128]
            qpre = arena[:, A_QPRE:A_QPRE + TG]
            qh = arena[:, A_QH + (h % 2) * TG:A_QH + (h % 2 + 1) * TG]
            dg = [(sod, sod.n)] if cfg.debug else []
            nq = 0
            for th in range(TH):
                wts = []
                if th == 0:
                    wts = [(lwq[h % 2], wq_cnt[h % 2]), (lxc, nxtb[0]),
                           (ve, ve_sgc[0])] + list(q_extra)
                for c in range(NC):
                    nq = P.op("tensor", "matmul",
                              pC[:, th * 512:th * 512 + 512],
                              wqb[:, c * 128:(c + 1) * 128],
                              xTb[:, c * TG + th * 512:c * TG + th * 512 + 512],
                              start=(c == 0), stop=(c == NC - 1),
                              waits=wts if c == 0 else (),
                              inc=(pe, 1) if c == NC - 1 else None)
            n_qmm_pc[h % 2] = nq
            nqpre = P.op("scalar", "activation", qpre, pC[:, 0:TG],
                         AF.Identity, bias=bq[:, h:h + 1], scale=1.0,
                         waits=[(pe, nq)] + dg, inc=(ac, 1))
            yield (nqpre, 0, 0)
            n_r = 0
            for th in range(TH):
                n_r = P.op("tensor", "matmul",
                           pC[:, th * 512:th * 512 + 512], RT_(h),
                           qpre[:, th * 512:th * 512 + 512],
                           start=True, stop=True,
                           waits=[(ac, nqpre)] if th == 0 else (),
                           inc=(pe, 1))
            nqh = P.op("scalar", "activation", qh, pC[:, 0:TG], AF.Identity,
                       bias=0.0, scale=scalec[:], waits=[(pe, n_r)] + dg,
                       inc=(ac, 1))
            nsq = P.op("scalar", "activation", qpre, qh, AF.Square,
                       bias=0.0, scale=1.0, inc=(ac, 1))
            yield (nqpre, nqh, 0)
            nsm = 0
            for th in range(TH):
                nsm = P.op("tensor", "matmul",
                           pC[:, th * 512:th * 512 + 512], ONESB,
                           qpre[:, th * 512:th * 512 + 512],
                           start=True, stop=True,
                           waits=[(ac, nsq), (ve, ve_sgc[0])]
                           if th == 0 else (), inc=(pe, 1))
            sg_w = [(pe, nsm)]
            if boundary:
                sg_w += [(lxr[0], lxr[0].n)]
            nsgc = P.op("vector", "tensor_copy", sgrow, pC[0:1, 0:TG],
                        waits=sg_w + dg, inc=(ve, 1))
            ve_sgc[0] = nsgc
            P.op("vector", "tensor_scalar", sgrow.bitcast(I32),
                 sgrow.bitcast(I32), 1, None, ALU.arith_shift_right,
                 inc=(ve, 1))
            P.op("vector", "tensor_scalar", sgrow.bitcast(I32),
                 sgrow.bitcast(I32), 0x1FBD1DF5, None, ALU.add, inc=(ve, 1))
            P.op("vector", "tensor_scalar", sgrow, sgrow,
                 km2[0:1, KV + h:KV + h + 1], None, ALU.mult, inc=(ve, 1))
            sgb = attn[0:1, h * TG:(h + 1) * TG]
            nrow = P.op("vector", "tensor_copy", sgb, sgrow, inc=(ve, 1))
            nbc = 0
            for th in range(TH):
                nbc = P.op("tensor", "matmul",
                           pC[:, th * 512:th * 512 + 512], onesrb[:],
                           sgb[0:1, th * 512:th * 512 + 512],
                           start=True, stop=True,
                           waits=[(ve, nrow), (ve, nsgc)] if th == 0 else (),
                           inc=(pe, 1))
            sgbb = sgbb_bufs[h % 2]
            ev_w = [(pe, nbc), (sp_x2, spill_x2[0]), (sp_xq, spill_xq)]
            nsig = P.op("vector", "tensor_copy", sgbb, pC[:, 0:TG],
                        waits=ev_w, inc=(ve, 1))
            ve_sgc[0] = nsig
            yield (nqpre, nqh, nsig)

        for g in range(NG if run_C else 0):
            nxtb[0] = P.op("sync", "dma_start",
                           out=xTb.rearrange("p (c s) -> p c s", s=TG),
                           in_=xTbf_d[:, g * TG:(g + 1) * TG]
                           .rearrange("(c p) s -> p c s", p=128),
                           waits=[(pe, last_arena_pe), (pcxx[g], pcx_x[g])],
                           inc=(lxc, 16))
            wq_w0 = ([(pe, b_pe_done[NSG - 1])] if g == 0
                     else [(pe, last_arena_pe)])
            emit_wq_load(0, wq_w0)
            if H > 1:
                emit_wq_load(1, wq_w0)
            bx = ([(ve, n_km2)] if g == 0 else
                  [(ve, ve_y1[0]), (ve, ve_y1[1])])
            bchain = emit_q_chain(g, 0, q_extra=bx, boundary=(g > 0))
            next(bchain)
            next(bchain)
            nqpre, nqh, nsigdone = next(bchain)
            sgbb_cur = sgbb_bufs[0]
            if cfg.debug and g == 0:
                P.op("gpsimd", "dma_start", out=dbg["q"],
                     in_=arena[:, A_QH:A_QH + TG],
                     waits=[(ac, nqh)], inc=(sod, 16))
                P.op("gpsimd", "dma_start", out=dbg["sg"], in_=sgrow,
                     waits=[(ve, nsigdone)], inc=(sod, 16))
            for h in range(H):
                kv = h // G
                qh = arena[:, A_QH + (h % 2) * TG:A_QH + (h % 2 + 1) * TG]
                chain = None
                chain_res = None
                pt_sod = {0: 0, 1: 0, 2: 0}
                tree = TreeSum(P, ve, [
                    arena[:, A_TREE + k * TG:A_TREE + (k + 1) * TG]
                    for k in range(5)])
                rd = [None] * NS
                npv = [0] * NS
                treed = [0] * NS
                ptr = [None] * NS
                for j in range(NS):
                    slot2 = j % 2
                    slot3 = j % 3
                    psc = PSC2[slot2]
                    n_sc = 0
                    for th in range(TH):
                        wts = []
                        if th == 0:
                            wts = [(ac, exp_slot2[slot2])]
                            if j == 0:
                                wts += [(ve, nsigdone), (ve, ve_recip)]
                                if h == 0 and g > 0:
                                    wts += [(ve, ve_y1[0])]
                            if j == 1:
                                wts += [(ve, ve_recb)]
                                if h == 0 and g > 0:
                                    wts += [(ve, ve_y1[1])]
                        P.op("tensor", "matmul",
                             psc[:, th * 512:th * 512 + 512],
                             k_fm[:, kv * S + j * 128:kv * S + j * 128 + 128],
                             qh[:, th * 512:th * 512 + 512],
                             start=True, stop=False,
                             waits=wts, inc=None)
                        n_sc = P.op("tensor", "matmul",
                                    psc[:, th * 512:th * 512 + 512],
                                    ONESF,
                                    sgbb_cur[:, th * 512:th * 512 + 512],
                                    start=False, stop=True, inc=(pe, 1))
                    ptreg = arena[:, A_PT + slot3 * TG:A_PT + (slot3 + 1) * TG]
                    ptr[j] = ptreg
                    pg = ([(pe, npv[j - 3]), (ve, treed[j - 3])]
                          if j >= 3 else pt_guard0[slot3])
                    dg = [(sod, pt_sod[slot3])] if (cfg.debug and
                                                    pt_sod[slot3]) else []
                    mixed = (g * NMIX <= j < (g + 1) * NMIX)
                    if mixed:
                        pts = arena[:, A_PTS:A_PTS + TG]
                        nex = P.op("scalar", "activation", pts, psc[:, 0:TG],
                                   AF.Exp, bias=beta[:, g * NS + j:g * NS + j + 1],
                                   scale=1.0,
                                   waits=[(pe, n_sc), (ve, mix_nrd[0])],
                                   inc=(ac, 1))
                        off = (NMIX - 1 - (j - g * NMIX)) * 128
                        nrd = P.op("vector", "tensor_tensor", ptreg, pts,
                                   mbase[:, off:off + TG], ALU.mult,
                                   waits=[(ac, nex)] + pg + dg,
                                   inc=(ve, 1))
                        mix_nrd[0] = nrd
                        ready = (ve, nrd)
                    else:
                        nex = P.op("scalar", "activation", ptreg, psc[:, 0:TG],
                                   AF.Exp, bias=beta[:, g * NS + j:g * NS + j + 1],
                                   scale=1.0,
                                   waits=[(pe, n_sc)] + pg + dg, inc=(ac, 1))
                        ready = (ac, nex)
                    exp_slot2[slot2] = ac.n
                    rd[j] = ready
                    if cfg.debug and g == 0 and h == 0 and j < 4:
                        nd = P.op("gpsimd", "dma_start",
                                  out=dbg["pt"][:, j * TG:(j + 1) * TG],
                                  in_=ptreg, waits=[ready], inc=(sod, 16))
                        pt_sod[slot3] = nd
                    if j >= 1:
                        jp = j - 1
                        for th in range(TH):
                            wts = []
                            if th == 0:
                                wts = [rd[jp]]
                                if jp == 0:
                                    wts.append((ve, at_cnt))
                            npv[jp] = P.op(
                                "tensor", "matmul",
                                pV[:, th * 512:th * 512 + 512],
                                v_tm[:, jp * KVD + kv * 128:
                                     jp * KVD + kv * 128 + 128],
                                ptr[jp][:, th * 512:th * 512 + 512],
                                start=(jp == 0), stop=False,
                                waits=wts, inc=(pe, 1))
                    if j % 2 == 1:
                        tree.feed_pair(ptr[j - 1], rd[j - 1], ptreg, ready)
                        treed[j - 1] = treed[j] = ve.n
                        w1_step()
                    if h + 1 < H:
                        if j == J_PRE:
                            chain = emit_q_chain(g, h + 1)
                            chain_res = next(chain)
                        elif j == J_PRE + 1:
                            chain_res = next(chain)
                        elif j == J_PRE + 2:
                            chain_res = next(chain)
                    if j == 6 and h + 2 < H:
                        emit_wq_load(h + 2, [(pe, n_qmm_pc[h % 2])])
                jp = NS - 1
                for th in range(TH):
                    npv[jp] = P.op(
                        "tensor", "matmul", pV[:, th * 512:th * 512 + 512],
                        v_tm[:, jp * KVD + kv * 128:jp * KVD + kv * 128 + 128],
                        ptr[jp][:, th * 512:th * 512 + 512],
                        start=False, stop=True,
                        waits=[rd[jp]] if th == 0 else (), inc=(pe, 1))
                npv_last = npv[jp]
                for s3 in range(3):
                    jl = max(jj for jj in range(NS) if jj % 3 == s3)
                    pt_guard0[s3] = [(pe, npv[jl]), (ve, treed[jl])]
                root_ap, root_rdy = tree.root()
                if cfg.debug and g == 0 and h == 0:
                    P.op("gpsimd", "dma_start", out=dbg["sm"], in_=root_ap,
                         waits=[root_rdy], inc=(sod, 16))
                n_sums = 0
                for th in range(TH):
                    n_sums = P.op("tensor", "matmul",
                                  pA[:, th * 512:th * 512 + 512], ONESB,
                                  root_ap[:, th * 512:th * 512 + 512],
                                  start=True, stop=True,
                                  waits=[root_rdy, (ac, exp_slot2[0])]
                                  if th == 0 else (), inc=(pe, 1))
                if C >= 2 * TG:
                    sums_s = xres[:, 0:TG]
                    praw = xres[:, TG:2 * TG]
                    ve_recip = P.op("vector", "tensor_copy", sums_s,
                                    pA[:, 0:TG], waits=[(pe, n_sums)],
                                    inc=(ve, 1))
                    ve_recb = ve_recip
                    at_cnt = P.op("vector", "tensor_copy", praw, pV[:, 0:TG],
                                  waits=[(pe, npv_last)], inc=(ve, 1))
                    P.op("vector", "reciprocal", recipb[:], sums_s,
                         inc=(ve, 1))
                    at_fin = P.op("vector", "tensor_tensor",
                                  attn[:, h * TG:(h + 1) * TG], praw,
                                  recipb[:], ALU.mult, inc=(ve, 1))
                else:
                    ve_recip = P.op("vector", "reciprocal", recipb[:],
                                    pA[:, 0:TG], waits=[(pe, n_sums)],
                                    inc=(ve, 1))
                    ve_recb = ve_recip
                    at_cnt = P.op("vector", "tensor_tensor",
                                  attn[:, h * TG:(h + 1) * TG], pV[:, 0:TG],
                                  recipb[:], ALU.mult,
                                  waits=[(pe, npv_last), (ve, ve_recip)],
                                  inc=(ve, 1))
                    at_fin = at_cnt
                if chain_res is not None:
                    nqpre, nqh, nsigdone = chain_res
                    sgbb_cur = sgbb_bufs[(h + 1) % 2]
            last_arena_pe = pe.n
            at_cnt = at_fin
            if cfg.debug:
                P.op("gpsimd", "dma_start",
                     out=dbg["at"][:, dbg_at_off:dbg_at_off + H * TG],
                     in_=attn[:], waits=[(ve, at_cnt)], inc=(sod, 16))
                dbg_at_off += H * TG
            n_woload = P.op("sync", "dma_start",
                            out=arena[:, 0:NC * C]
                            .rearrange("p (c o) -> p c o", o=C),
                            in_=woTbf_d.rearrange("(c p) o -> p c o", p=128),
                            waits=[(pe, last_arena_pe), (ve, at_cnt),
                                   (ac, ac.n), (pcxw, pcx_wo[0])],
                            inc=(lwo, 16))
            PSY = [(pA, pB), (pC, pV)]
            ve_y1 = [at_cnt, at_cnt]
            last_nx2 = [0]
            for t in range(NTG):
                tt = g * NTG + t
                nx = P.op("sync", "dma_start", out=xres[:],
                          in_=xown_d[tt * 128:(tt + 1) * 128, :],
                          waits=[(ve, ve_y1[0]), (ve, ve_y1[1])] +
                          ([(ve, at_cnt)] if t == 0 else []),
                          inc=(lxr[0], 16))
                nxbo = P.op("vector", "tensor_tensor", xres[:], xres[:],
                            bob[:], ALU.add, waits=[(lxr[0], nx)],
                            inc=(ve, 1))
                pY0, pY1 = PSY[t % 2]
                n_wo = 0
                for o in range(C // 512):
                    pY = pY0 if o < 2 else pY1
                    po = pY[:, (o % 2) * 512:(o % 2) * 512 + 512]
                    for c2 in range(H):
                        wts = []
                        if c2 == 0:
                            wts = [(lwo, n_woload), (ve, ve_y1[t % 2])]
                        n_wo = P.op(
                            "tensor", "matmul", po,
                            attn[:, c2 * TG + t * 128:c2 * TG + t * 128 + 128],
                            arena[:, c2 * C + o * 512:c2 * C + o * 512 + 512],
                            start=(c2 == 0), stop=(c2 == H - 1),
                            waits=wts if c2 == 0 else (),
                            inc=(pe, 1) if c2 == H - 1 else None)
                nv = 0
                for half in range(max(1, C // 1024)):
                    pY = pY0 if half == 0 else pY1
                    nv = P.op("vector", "tensor_tensor",
                              y1[:, half * 1024:(half + 1) * 1024],
                              pY[:, 0:1024],
                              xres[:, half * 1024:half * 1024 + 1024],
                              ALU.add,
                              waits=[(pe, n_wo), (ve, nxbo),
                                     (ac, last_nx2[0])],
                              inc=(ve, 1))
                ve_y1[t % 2] = nv
                nsq2 = 0
                for hf in range(C // 1024):
                    nsq2 = P.op("scalar", "activation", xqt[:],
                                y1[:, hf * 1024:(hf + 1) * 1024], AF.Square,
                                bias=0.0, scale=1.0,
                                accum_out=stat[:, hf * 13:hf * 13 + 1],
                                waits=[(ve, nv), (sp_xq, spill_xq)],
                                inc=(ac, 1))
                if C > 1024:
                    P.op("vector", "tensor_tensor", stat[:, 0:1],
                         stat[:, 0:1], stat[:, 13:14], ALU.add,
                         waits=[(ac, nsq2)], inc=(ve, 1))
                P.op("vector", "reduce_sum", stat[:, 1:2], y1[:], axis=AX.X,
                     waits=[(ve, nv)], inc=(ve, 1))
                P.op("vector", "tensor_scalar", stat[:, 2:3], stat[:, 1:2],
                     1.0 / C, None, ALU.mult, inc=(ve, 1))
                P.op("vector", "tensor_tensor", stat[:, 3:4], stat[:, 2:3],
                     stat[:, 2:3], ALU.mult, inc=(ve, 1))
                P.op("vector", "tensor_scalar", stat[:, 4:5], stat[:, 0:1],
                     1.0 / C, None, ALU.mult, waits=[(ac, nsq2)],
                     inc=(ve, 1))
                P.op("vector", "tensor_tensor", stat[:, 5:6],
                     stat[:, 4:5], stat[:, 3:4], ALU.subtract, inc=(ve, 1))
                nvar = P.op("vector", "tensor_scalar", stat[:, 5:6],
                            stat[:, 5:6], 1e-5, None, ALU.add, inc=(ve, 1))
                nsd = P.op("scalar", "activation", stat[:, 6:7], stat[:, 5:6],
                           AF.Sqrt, bias=0.0, scale=1.0,
                           waits=[(ve, nvar)], inc=(ac, 1))
                nrstd = P.op("vector", "reciprocal", stat[:, 7:8],
                             stat[:, 6:7], waits=[(ac, nsd)], inc=(ve, 1))
                nnb = P.op("vector", "scalar_tensor_tensor", stat[:, 12:13],
                           stat[:, 2:3], -1.0, stat[:, 7:8], ALU.mult,
                           ALU.mult, inc=(ve, 1))
                for hf in range(C // 1024):
                    nx2 = P.op("scalar", "activation", x2t[:],
                               y1[:, hf * 1024:(hf + 1) * 1024], AF.Identity,
                               bias=stat[:, 12:13], scale=stat[:, 7:8],
                               waits=[(ve, nnb), (sp_x2, spill_x2[0])],
                               inc=(ac, 1))
                    last_nx2[0] = nx2
                    spill_x2[0] = P.op("gpsimd", "dma_start",
                                       out=x2_d[tt * 128:(tt + 1) * 128,
                                                hf * 1024:(hf + 1) * 1024],
                                       in_=x2t[:], waits=[(ac, nx2)],
                                       inc=(sp_x2, 16))
                P.op("vector", "reduce_max", stat[:, 8:9], y1[:], axis=AX.X,
                     waits=[(ve, nv)], inc=(ve, 1))
                P.op("vector", "reduce_max", stat[:, 3:4], y1[:], axis=AX.X,
                     negate=True, inc=(ve, 1))
                P.op("vector", "tensor_scalar", stat[:, 8:9], stat[:, 8:9],
                     stat[:, 7:8], None, ALU.mult, inc=(ve, 1))
                P.op("vector", "tensor_tensor", stat[:, 8:9], stat[:, 8:9],
                     stat[:, 12:13], ALU.add, inc=(ve, 1))
                P.op("vector", "tensor_scalar", stat[:, 3:4], stat[:, 3:4],
                     stat[:, 7:8], None, ALU.mult, inc=(ve, 1))
                P.op("vector", "tensor_tensor", stat[:, 3:4], stat[:, 3:4],
                     stat[:, 12:13], ALU.subtract, inc=(ve, 1))
                P.op("vector", "tensor_tensor", stat[:, 8:9], stat[:, 8:9],
                     stat[:, 3:4], ALU.max, inc=(ve, 1))
                P.op("vector", "tensor_scalar", stat[:, 9:10], stat[:, 8:9],
                     1.0 / 127.0, 1e-8, ALU.mult, ALU.max, inc=(ve, 1))
                P.op("vector", "tensor_copy", sx_all[:, tt:tt + 1],
                     stat[:, 9:10], inc=(ve, 1))
                nrq = P.op("vector", "reciprocal", stat[:, 10:11],
                           stat[:, 9:10], inc=(ve, 1))
                P.op("vector", "tensor_scalar", stat[:, 4:5], stat[:, 7:8],
                     stat[:, 10:11], None, ALU.mult, inc=(ve, 1))
                nrq = P.op("vector", "tensor_scalar", stat[:, 5:6],
                           stat[:, 12:13], stat[:, 10:11], None, ALU.mult,
                           inc=(ve, 1))
                nxq = 0
                for hf in range(C // 512):
                    na = P.op("scalar", "activation", i32q[:],
                              y1[:, hf * 512:(hf + 1) * 512], AF.Identity,
                              bias=stat[:, 5:6], scale=stat[:, 4:5],
                              waits=[(ve, nrq), (ve, i32g["ve"]),
                                     (sp_xq, spill_xq)], inc=(ac, 1))
                    nxq = P.op("vector", "tensor_copy",
                               xqt[:, (hf % 2) * 512:(hf % 2) * 512 + 512],
                               i32q[:], waits=[(ac, na)], inc=(ve, 1))
                    i32g["ve"] = nxq
                    if hf % 2 == 1:
                        spill_xq = P.op("gpsimd", "dma_start",
                                        out=xq_d[tt * 128:(tt + 1) * 128,
                                                 (hf - 1) * 512:(hf + 1) * 512],
                                        in_=xqt[:], waits=[(ve, nxq)],
                                        inc=(sp_xq, 16))
                last_nx2[0] = max(last_nx2[0], ac.n)
            last_arena_pe = pe.n
        while run_C and w1s["rb"] < NW1:
            w1_step()
        if cfg.debug:
            P.op("gpsimd", "dma_start", out=dbg["k"], in_=k_fm[:],
                 waits=[(ac, acB)], inc=(sod, 16))
            P.op("gpsimd", "dma_start", out=dbg["v"], in_=v_tm[:],
                 waits=[(ve, ve.n)], inc=(sod, 16))

        P.wait("gpsimd", sp_w1, sp_w1.n)
        P.wait("gpsimd", sp_x2, sp_x2.n)
        P.wait("gpsimd", sp_xq, sp_xq.n)
        esc.close()
        barrier()

        if not run_D1:
            NTx, NW2x, NOPx = 0, 0, 0
        else:
            NTx, NW2x, NOPx = NT, NW2, I // 512
        es1 = contextlib.ExitStack()
        xq_fm = sb(es1, "xq_fm", [128, NC * T], BF16)
        w1t = sb(es1, "w1t", [128, 4 * NC * 512], BF16)
        w2f = sb(es1, "w2f", [128, I], F32)
        w2qb = sb(es1, "w2qb", [128, I], BF16)
        w2ts = sb(es1, "w2ts", [128, C], BF16)
        w2ts2 = sb(es1, "w2ts2", [128, C], BF16)
        xqrow = sb(es1, "xqrow", [128, C], BF16)
        i32d = sb(es1, "i32d", [128, I // 2], I32)
        htile = sb(es1, "htile", [128, 512], F32)
        utile = sb(es1, "utile", [128, 512], BF16)
        pG = es1.enter_context(nc.psum_tensor("pG", [128, 1024], F32))
        pH = es1.enter_context(nc.psum_tensor("pH", [128, 1024], F32))
        pT = es1.enter_context(nc.psum_tensor("pT", [128, 1024], BF16))
        pX = es1.enter_context(nc.psum_tensor("pX", [128, 512], F32))

        xq_state = {"tt": 0, "evac": 0}
        xqfm_t = [0] * max(NT, 1)

        def xq_tile():
            tt = xq_state["tt"]
            if tt >= NTx:
                return
            xq_state["tt"] += 1
            nld = P.op("gpsimd", "dma_start", out=xqrow[:],
                       in_=xq_d[tt * 128:(tt + 1) * 128, :],
                       waits=[(ve, xq_state["evac"])] if tt else (),
                       inc=(lxq, 16))
            for cq in range(NC // 8):
                n_tr = 0
                for k in range(8):
                    c = cq * 8 + k
                    n_tr = P.op("tensor", "transpose",
                                pT[:, k * 128:(k + 1) * 128],
                                xqrow[:, c * 128:(c + 1) * 128], IDENT,
                                waits=[(lxq, nld), (ve, xq_state["evac"])]
                                if k == 0 else (), inc=(pe, 1))
                dst = xq_fm[:, cq * 8 * T + tt * 128:]
                dst = bass.AP(dst.tensor, dst.offset,
                              [[dst.ap[0][0], 128], [T, 8], [1, 128]])
                xq_state["evac"] = P.op("vector", "tensor_copy", dst,
                                        pT[:, 0:1024]
                                        .rearrange("p (k c) -> p k c", k=8),
                                        waits=[(pe, n_tr)], inc=(ve, 1))
            xqfm_t[tt] = xq_state["evac"]

        w2s = {"rb": 0, "st": 0, "sp": 0, "ve": 0, "ac": 0, "ld": 0,
               "nrc": 0, "i": 0, "tsp2": [0, 0]}

        def w2_step():
            rb = w2s["rb"]
            if rb >= NW2x:
                return
            st_ = w2s["st"]
            if st_ == 0:
                w2s["ld"] = P.op("scalar", "dma_start", out=w2f[:],
                                 in_=w2_d[rb * 128:(rb + 1) * 128, :],
                                 waits=[(ac, w2s["ac"]), (ve, w2s["ve"])]
                                 if rb else (), inc=(lw2, 16))
            elif st_ == 1:
                P.op("vector", "reduce_max", stat[:, 8:9], w2f[:], axis=AX.X,
                     apply_absolute_value=True,
                     waits=[(lw2, w2s["ld"]), (ac, w2s["ac"])], inc=(ve, 1))
                P.op("vector", "tensor_scalar", stat[:, 9:10], stat[:, 8:9],
                     1.0 / 127.0, 1e-8, ALU.mult, ALU.max, inc=(ve, 1))
                w2s["nrc"] = P.op("vector", "reciprocal", stat[:, 10:11],
                                  stat[:, 9:10], inc=(ve, 1))
            else:
                hf = st_ - 2
                na = P.op("scalar", "activation", i32d[:],
                          w2f[:, hf * (I // 2):(hf + 1) * (I // 2)],
                          AF.Identity, bias=0.0, scale=stat[:, 10:11],
                          waits=[(ve, w2s["nrc"]), (sp_w2, w2s["sp"])],
                          inc=(ac, 1))
                w2s["ac"] = na
                w2s["ve"] = P.op("vector", "tensor_scalar",
                                 w2qb[:, hf * (I // 2):(hf + 1) * (I // 2)],
                                 i32d[:], stat[:, 9:10], None, ALU.mult,
                                 waits=[(ac, na), (sp_w2, w2s["sp"])],
                                 inc=(ve, 1))
                w2s["sp"] = P.op(
                    "gpsimd", "dma_start",
                    out=w2q_d[rb * 128:(rb + 1) * 128,
                              hf * (I // 2):(hf + 1) * (I // 2)],
                    in_=w2qb[:, hf * (I // 2):(hf + 1) * (I // 2)],
                    waits=[(ve, w2s["ve"])], inc=(sp_w2, 16))
            w2s["st"] += 1
            if w2s["st"] == 4:
                w2s["st"] = 0
                w2s["rb"] += 1

        def w2_respill(k):
            while k > 0 and w2s["rb"] >= NW2x and w2s["i"] < NI:
                k -= 1
                i = w2s["i"]
                w2s["i"] += 1
                buf = w2ts2 if i % 2 else w2ts
                sem = lw2tb if i % 2 else lw2t
                ntr = P.op("sync", "dma_start",
                           out=buf[:],
                           in_=w2q_d[:, i * 128:(i + 1) * 128],
                           transpose=True,
                           waits=[(sp_w2, w2s["sp"]),
                                  (sp_w2u if i % 2 else sp_w2t,
                                   w2s["tsp2"][i % 2])],
                           inc=(sem, 16))
                for ob in range(NOB):
                    w2s["tsp2"][i % 2] = P.op(
                        "gpsimd", "dma_start",
                        out=w2qT_d[ob * I + i * 128:ob * I + (i + 1) * 128, :],
                        in_=buf[:, ob * 512:(ob + 1) * 512],
                        waits=[(sem, ntr)],
                        inc=(sp_w2u if i % 2 else sp_w2t, 16))

        NOP = I // 512
        w1rdy = {0: [], 1: []}

        def emit_w1t_load(op, wts=()):
            buf = op % 2
            base = buf * 2 * NC * 512
            first = True
            nn = 0
            for gu in range(2):
                rows = op * 512 + gu * I
                for c in range(NC):
                    sem = lw1t[buf] if c % 2 == 0 else lw1s[buf]
                    nn = P.op("sync", "dma_start",
                              out=w1t[:, base + gu * NC * 512 + c * 512:
                                      base + gu * NC * 512 + (c + 1) * 512],
                              in_=w1q_d[rows:rows + 512,
                                        c * 128:(c + 1) * 128],
                              transpose=True,
                              waits=wts if first else (),
                              inc=(sem, 16))
                    first = False
            w1rdy[buf] = [(lw1t[buf], lw1t[buf].n), (lw1s[buf], lw1s[buf].n)]

        w1pe = {"evac": 0}

        def w1pe_rowchunk(op, idx, wts_first=()):
            buf = op % 2
            base = buf * 2 * NC * 512
            gu, rc = divmod(idx, 4)
            rows = op * 512 + gu * I + rc * 128
            wl = list(wts_first) if wts_first else [(ve, w1pe["evac"])]
            nld = P.op("sync", "dma_start", out=xqrow[:],
                       in_=w1q_d[rows:rows + 128, :],
                       waits=wl, inc=(lwr, 16))
            for hc in range(NC // 8):
                n_tr = 0
                for k in range(8):
                    c = hc * 8 + k
                    n_tr = P.op("tensor", "transpose",
                                pT[:, k * 128:(k + 1) * 128],
                                xqrow[:, c * 128:(c + 1) * 128], IDENT,
                                waits=[(lwr, nld), (ve, w1pe["evac"])]
                                if k == 0 else (), inc=(pe, 1))
                dst = w1t[:, base + gu * NC * 512 + hc * 8 * 512 + rc * 128:]
                dst = bass.AP(dst.tensor, dst.offset,
                              [[dst.ap[0][0], 128], [512, 8], [1, 128]])
                w1pe["evac"] = P.op("vector", "tensor_copy", dst,
                                    pT[:, 0:1024]
                                    .rearrange("p (k c) -> p k c", k=8),
                                    waits=[(pe, n_tr)], inc=(ve, 1))
            if idx == 2 * 4 - 1:
                w1rdy[buf] = [(ve, w1pe["evac"])]

        if run_D1:
            emit_w1t_load(0)
            if NOPx > 1:
                emit_w1t_load(1)
            xq_tile()
            xq_tile()
        ac_sil = [0, 0]
        ve_hml = [0, 0]
        sph_n = 0
        for op in range(NOPx):
            buf = op % 2
            base = buf * 2 * NC * 512
            wrdy = list(w1rdy[buf])
            pe_mark = pe.n
            w1pe_idx = 0
            for t in range(NT):
                pGU = pG if t % 2 == 0 else pH
                n_g = 0
                for gu in range(2):
                    for c in range(NC):
                        wts2 = []
                        if gu == 0 and c == 0:
                            wts2 = wrdy + [
                                (ve, xqfm_t[t] if op == 0
                                 else xqfm_t[NT - 1]),
                                (ac, ac_sil[t % 2]), (ve, ve_hml[t % 2])]
                        n_g = P.op(
                            "tensor", "matmul",
                            pGU[:, gu * 512:gu * 512 + 512],
                            xq_fm[:, c * T + t * 128:c * T + t * 128 + 128],
                            w1t[:, base + gu * NC * 512 + c * 512:
                                base + gu * NC * 512 + (c + 1) * 512],
                            start=(c == 0), stop=(c == NC - 1),
                            waits=wts2 if (gu == 0 and c == 0) or c == 0 else (),
                            inc=(pe, 1) if c == NC - 1 else None)
                ac_sil[t % 2] = P.op("scalar", "activation", htile[:],
                                     pGU[:, 0:512], AF.Silu, bias=0.0,
                                     scale=sx_all[:, t:t + 1],
                                     waits=[(pe, n_g), (sp_h, sph_n)],
                                     inc=(ac, 1))
                nu = P.op("vector", "tensor_scalar", utile[:],
                          pGU[:, 512:1024], sx_all[:, t:t + 1], None,
                          ALU.mult, waits=[(pe, n_g)], inc=(ve, 1))
                ve_hml[t % 2] = P.op("vector", "tensor_tensor", htile[:],
                                     htile[:], utile[:], ALU.mult,
                                     waits=[(ac, ac_sil[t % 2]), (ve, nu)],
                                     inc=(ve, 1))
                sph_n = P.op("gpsimd", "dma_start",
                             out=h_d[t * 128:(t + 1) * 128,
                                     op * 512:(op + 1) * 512],
                             in_=htile[:], waits=[(ve, ve_hml[t % 2])],
                             inc=(sp_h, 16))
                w2_step()
                if op == 0:
                    xq_tile()
                elif op + 1 < NOPx and t % 2 == 0 and w1pe_idx < 8:
                    wf = ()
                    if w1pe_idx == 0:
                        wf = [(pe, pe_mark), (ve, xq_state["evac"])
                              ] if op == 1 else [(pe, pe_mark),
                                                 (ve, w1pe["evac"])]
                    w1pe_rowchunk(op + 1, w1pe_idx, wf)
                    w1pe_idx += 1
            while op + 1 < NOPx and 0 < w1pe_idx < 8:
                w1pe_rowchunk(op + 1, w1pe_idx)
                w1pe_idx += 1
            w2_respill(9)
        while run_D1 and xq_state["tt"] < NTx:
            xq_tile()
        while w2s["rb"] < NW2x:
            w2_step()
        while run_D1 and w2s["i"] < NI:
            w2_respill(4)

        P.wait("gpsimd", sp_h, sp_h.n)
        P.wait("gpsimd", sp_w2t, sp_w2t.n)
        P.wait("gpsimd", sp_w2u, sp_w2u.n)
        es1.close()
        barrier()

        es2 = contextlib.ExitStack()
        w2full = sb(es2, "w2full", [128, NI * 512], BF16)
        hq_fm = sb(es2, "hq_fm", [128, NI * 512], BF16)
        hrow = sb(es2, "hrow", [128, I], F32)
        hqrow = sb(es2, "hqrow", [128, I], BF16)
        i32h = sb(es2, "i32h", [128, I // 2], I32)
        y2g = sb(es2, "y2g", [128, 4 * C], F32)
        y2 = sb(es2, "y2", [128, C], F32)
        x2res = sb(es2, "x2res", [128, 2 * C], BF16)
        outt = sb(es2, "outt", [128, C], F32)
        sq2 = sb(es2, "sq2", [128, C], BF16)
        pD = es2.enter_context(nc.psum_tensor("pD", [128, 1024], F32))
        pE = es2.enter_context(nc.psum_tensor("pE", [128, 1024], F32))
        pT2 = es2.enter_context(nc.psum_tensor("pT2", [128, 1024], BF16))

        NT4 = (NT // 4) if run_D2 else 0
        CH = max(1, (NI + 3) // 4)
        n_out = [0, 0]
        last_nfin = 0
        pe_hq_read = 0
        ve_dn = [0, 0]
        ac_dn = [0, 0]
        hqs = {"tq": 0, "tt": 0, "tr": 0, "i32": 0, "hq": 0}
        ve_hq_all = [0] * max(NT, 1)
        nhq_all = [0] * max(NT, 1)

        def hq_quant():
            tt = hqs["tq"]
            if tt >= 4 * NT4:
                return
            hqs["tq"] += 1
            nlh = P.op("sync", "dma_start", out=hrow[:],
                       in_=h_d[tt * 128:(tt + 1) * 128, :],
                       waits=[(ac, hqs["i32"])], inc=(lh, 16))
            P.op("vector", "reduce_max", stat[:, 8:9], hrow[:], axis=AX.X,
                 apply_absolute_value=True, waits=[(lh, nlh)],
                 inc=(ve, 1))
            P.op("vector", "tensor_scalar", stat[:, 9:10], stat[:, 8:9],
                 1.0 / 127.0, 1e-8, ALU.mult, ALU.max, inc=(ve, 1))
            nrc = P.op("vector", "reciprocal", stat[:, 10:11],
                       stat[:, 9:10], inc=(ve, 1))
            nhq = 0
            for hf in range(2):
                na = P.op("scalar", "activation", i32h[:],
                          hrow[:, hf * (I // 2):(hf + 1) * (I // 2)],
                          AF.Identity, bias=0.0, scale=stat[:, 10:11],
                          waits=[(ve, nrc if hf == 0 else nhq)],
                          inc=(ac, 1))
                hqs["i32"] = na
                nhq = P.op("vector", "tensor_copy",
                           hqrow[:, hf * (I // 2):(hf + 1) * (I // 2)],
                           i32h[:], waits=[(ac, na),
                                           (pe, hqs["tr"])],
                           inc=(ve, 1))
            nhq = P.op("vector", "tensor_copy", s2g[:, tt % 4:tt % 4 + 1],
                       stat[:, 9:10],
                       waits=[(ac, ac_dn[0]), (ac, ac_dn[1])], inc=(ve, 1))
            nhq_all[tt] = nhq

        def hq_tr():
            tt = hqs["tt"]
            if tt >= hqs["tq"]:
                return
            hqs["tt"] += 1
            nhq = nhq_all[tt]
            for iq in range((NI + 7) // 8):
                k8 = min(8, NI - iq * 8)
                n_tr = 0
                for k in range(k8):
                    i = iq * 8 + k
                    n_tr = P.op("tensor", "transpose",
                                pT2[:, k * 128:(k + 1) * 128],
                                hqrow[:, i * 128:(i + 1) * 128], IDENT,
                                waits=[(ve, nhq), (ve, hqs["hq"]),
                                       (pe, pe_hq_read)]
                                if (k == 0 and iq == 0) else
                                ([(ve, nhq), (ve, hqs["hq"])]
                                 if k == 0 else ()), inc=(pe, 1))
                hqs["tr"] = n_tr
                dst = hq_fm[:, iq * 8 * 512 + (tt % 4) * 128:]
                dst = bass.AP(dst.tensor, dst.offset,
                              [[dst.ap[0][0], 128], [512, k8], [1, 128]])
                hqs["hq"] = P.op("vector", "tensor_copy", dst,
                                 pT2[:, 0:k8 * 128]
                                 .rearrange("p (k c) -> p k c", k=k8),
                                 waits=[(pe, n_tr)], inc=(ve, 1))
            ve_hq_all[tt] = hqs["hq"]

        def hq_tile():
            hq_quant()
            hq_tr()

        for tg in range(NT4):
            while hqs["tt"] < tg * 4 + 2:
                hq_tile()
            for ob in range(NOB):
                nw2c = [0] * 4
                for chk in range(4):
                    i0 = chk * CH
                    if i0 >= NI:
                        break
                    i1 = min(NI, i0 + CH)
                    nw2c[chk] = P.op(
                        "sync", "dma_start",
                        out=w2full[:, i0 * 512:i1 * 512]
                        .rearrange("p (i o) -> p i o", i=i1 - i0),
                        in_=w2qT_d[ob * I + i0 * 128:ob * I + i1 * 128, :]
                        .rearrange("(i p) o -> p i o", p=128),
                        waits=[(ac, ac_dn[0]), (ac, ac_dn[1])]
                        if chk == 0 else (),
                        inc=(lwf_[chk], 16))
                for t4 in range(4):
                    if ob == 0 and t4 < 2:
                        hq_tile()
                    if ob == NOB - 1 and t4 == 2:
                        hq_quant()
                    pDD = pD if t4 % 2 == 0 else pE
                    po = pDD[:, (ob % 2) * 512:(ob % 2) * 512 + 512]
                    n_d = 0
                    for i in range(NI):
                        wts3 = ()
                        if i == 0:
                            wts3 = [(lwf_[0], nw2c[0]),
                                    (ve, ve_hq_all[tg * 4 + t4]),
                                    (ac, ac_dn[t4 % 2])]
                        elif i % CH == 0:
                            wts3 = [(lwf_[i // CH], nw2c[i // CH])]
                        n_d = P.op(
                            "tensor", "matmul", po,
                            hq_fm[:, i * 512 + t4 * 128:i * 512 + t4 * 128 + 128],
                            w2full[:, i * 512:(i + 1) * 512],
                            start=(i == 0), stop=(i == NI - 1),
                            waits=wts3, inc=(pe, 1) if i == NI - 1 else None)
                    ac_dn[t4 % 2] = P.op(
                        "scalar", "activation",
                        y2g[:, t4 * C + ob * 512:t4 * C + (ob + 1) * 512],
                        po, AF.Identity, bias=0.0, scale=s2g[:, t4:t4 + 1],
                        waits=[(pe, n_d), (ve, ve_dn[t4 % 2])], inc=(ac, 1))
            pe_hq_read = pe.n
            if tg + 1 < NT4:
                hq_tr()
                hq_tile()
            for t4 in range(4):
                tt = tg * 4 + t4
                nx2l = P.op("sync", "dma_start",
                            out=x2res[:, (t4 % 2) * C:(t4 % 2 + 1) * C],
                            in_=x2_d[tt * 128:(tt + 1) * 128, :],
                            waits=[(ve, ve_dn[t4 % 2])],
                            inc=(lx2[t4 % 2], 16))
                nv2 = P.op("vector", "tensor_tensor", y2[:],
                           y2g[:, t4 * C:(t4 + 1) * C],
                           x2res[:, (t4 % 2) * C:(t4 % 2 + 1) * C], ALU.add,
                           waits=[(lx2[t4 % 2], nx2l), (ac, ac_dn[t4 % 2]),
                                  (so_[t4 % 2], n_out[t4 % 2]),
                                  (ac, last_nfin)], inc=(ve, 1))
                ve_dn[t4 % 2] = nv2
                nsq2 = P.op("scalar", "activation", sq2[:], y2[:], AF.Square,
                            bias=0.0, scale=1.0, accum_out=stat[:, 0:1],
                            waits=[(ve, nv2)], inc=(ac, 1))
                P.op("vector", "reduce_sum", stat[:, 1:2], y2[:], axis=AX.X,
                     waits=[(ve, nv2)], inc=(ve, 1))
                P.op("vector", "tensor_scalar", stat[:, 2:3], stat[:, 1:2],
                     1.0 / C, None, ALU.mult, inc=(ve, 1))
                P.op("vector", "tensor_tensor", stat[:, 3:4], stat[:, 2:3],
                     stat[:, 2:3], ALU.mult, inc=(ve, 1))
                P.op("vector", "tensor_scalar", stat[:, 4:5], stat[:, 0:1],
                     1.0 / C, None, ALU.mult, waits=[(ac, nsq2)], inc=(ve, 1))
                P.op("vector", "tensor_tensor", stat[:, 5:6],
                     stat[:, 4:5], stat[:, 3:4], ALU.subtract, inc=(ve, 1))
                nvar = P.op("vector", "tensor_scalar", stat[:, 5:6],
                            stat[:, 5:6], 1e-5, None, ALU.add, inc=(ve, 1))
                nsd = P.op("scalar", "activation", stat[:, 6:7], stat[:, 5:6],
                           AF.Sqrt, bias=0.0, scale=1.0, waits=[(ve, nvar)],
                           inc=(ac, 1))
                P.op("vector", "reciprocal", stat[:, 7:8], stat[:, 6:7],
                     waits=[(ac, nsd)], inc=(ve, 1))
                nnb = P.op("vector", "scalar_tensor_tensor", stat[:, 12:13],
                           stat[:, 2:3], -1.0, stat[:, 7:8], ALU.mult,
                           ALU.mult, inc=(ve, 1))
                nfin = P.op("scalar", "activation", outt[:], y2[:],
                            AF.Identity, bias=stat[:, 12:13],
                            scale=stat[:, 7:8],
                            waits=[(ve, nnb), (so_[0], n_out[0]),
                                   (so_[1], n_out[1])], inc=(ac, 1))
                last_nfin = nfin
                n_out[t4 % 2] = P.op("gpsimd", "dma_start",
                             out=out_d[tt * 128:(tt + 1) * 128, :],
                             in_=outt[:], waits=[(ac, nfin)],
                             inc=(so_[t4 % 2], 16))
        P.wait("gpsimd", so_[0], so_[0].n)
        P.wait("gpsimd", so_[1], so_[1].n)
        if cfg.debug:
            P.wait("gpsimd", sod, sod.n)
        es2.close()

        blk = es.enter_context(nc.Block())

        @blk.sync
        def _(e):
            P.replay("sync", e, nc, collect, fix)

        @blk.tensor
        def _(e):
            P.replay("tensor", e, nc, collect, fix)

        @blk.vector
        def _(e):
            P.replay("vector", e, nc, collect, fix)

        @blk.scalar
        def _(e):
            P.replay("scalar", e, nc, collect, fix)

        @blk.gpsimd
        def _(e):
            P.replay("gpsimd", e, nc, collect, fix)

    return nc


def build(cfg):
    m = {}
    _build(cfg, collect=m)
    return _build(cfg, fix=m)


def make_in_maps(cfg, x, wq, bq_, wk, bk_, wv, bv_, wo, bo_, w1, w2,
                 n_cores=8):
    consts = host_consts(cfg)
    wqT_b = np.ascontiguousarray(
        np.asarray(wq, np.float32).T.reshape(cfg.C, cfg.H, cfg.D)
        .transpose(1, 0, 2))
    shared = dict(
        wqT=wqT_b,
        wkT=np.ascontiguousarray(np.asarray(wk, np.float32).T),
        wvT=np.ascontiguousarray(np.asarray(wv, np.float32).T),
        woT=np.ascontiguousarray(np.asarray(wo, np.float32).T),
        w1=np.ascontiguousarray(np.asarray(w1, np.float32)),
        w2=np.ascontiguousarray(np.asarray(w2, np.float32)),
        bqT=np.ascontiguousarray(
            np.asarray(bq_, np.float32).reshape(cfg.H, cfg.D).T),
        bkT=np.ascontiguousarray(
            np.asarray(bk_, np.float32).reshape(cfg.KV, cfg.D).T),
        bv_r=np.ascontiguousarray(np.asarray(bv_, np.float32).reshape(1, -1)),
        bo_r=np.ascontiguousarray(np.asarray(bo_, np.float32).reshape(1, -1)),
        rts=consts["rts"], mbase=consts["mbase"], ident=consts["ident"],
        ones_col=consts["ones_col"], ones_row=consts["ones_row"],
        ones_rowb=consts["ones_rowb"], ones_f=consts["ones_f"],
        ones_b=consts["ones_b"], scale_col=consts["scale_col"])
    in_maps = []
    for c in range(n_cores):
        b, half = divmod(c, 2)
        r = half * cfg.T
        xrot = np.roll(x[b], -r, axis=0)
        in_maps.append(dict(shared,
                            xT=np.ascontiguousarray(xrot.T),
                            x_own=np.ascontiguousarray(x[b, r:r + cfg.T]),
                            beta=host_beta(cfg, half)))
    return in_maps


def kernel(L0_attn_x, L0_attn_wq, L0_attn_bq, L0_attn_wk, L0_attn_bk,
           L0_attn_wv, L0_attn_bv, L0_attn_wo, L0_attn_bo,
           L0_ffn_w_gate_up, L0_ffn_w_down):
    cfg = Cfg()
    x = np.ascontiguousarray(np.asarray(L0_attn_x, dtype=np.float32))
    in_maps = make_in_maps(cfg, x, L0_attn_wq, L0_attn_bq, L0_attn_wk,
                           L0_attn_bk, L0_attn_wv, L0_attn_bv, L0_attn_wo,
                           L0_attn_bo, L0_ffn_w_gate_up, L0_ffn_w_down)
    nc = build(cfg)
    res = run_bass_kernel_spmd(nc, in_maps, core_ids=list(range(8)))
    B = x.shape[0]
    outs = np.empty((B, cfg.S, cfg.C), dtype=np.float32)
    for c in range(8):
        b, half = divmod(c, 2)
        outs[b, half * cfg.T:(half + 1) * cfg.T] = res.results[c]["out"]
    return outs
